# version 08
# baseline (speedup 1.0000x reference)
import numpy as np
import ml_dtypes
import concourse.bass as bass
import concourse.mybir as mybir
from concourse.bass_utils import run_bass_kernel_spmd

F32 = mybir.dt.float32
BF16 = mybir.dt.bfloat16
AF = mybir.ActivationFunctionType
ALU = mybir.AluOpType

S = 4096
D = 1024
H = 8
HD = 128
FF = 4096
CT = 64
NCH = S // CT
NEG = -4294967295.0
EPS1 = 1e-5
EPS6 = 1e-6


class Sched:
    ENGINES = ("sync", "scalar", "tensor", "vector", "gpsimd")

    def __init__(self):
        self.ops = []
        self.fences = []

    def op(self, stream, fn, reads=(), writes=(), dma=False):
        assert stream in self.ENGINES
        self.ops.append(dict(stream=stream, fn=fn, reads=tuple(reads),
                             writes=tuple(writes), dma=dma,
                             need_inc=False, waits=None))

    def fence(self):
        self.fences.append(len(self.ops))

    def dma(self, stream, out, in_, wkey, rkeys=(), **kw):
        self.op(stream, lambda e: e.dma_start(out=out, in_=in_, **kw),
                reads=rkeys, writes=(wkey,), dma=True)

    DMA_RING = {"sync": 16, "gpsimd": 8, "scalar": 4}

    def analyze(self):
        fence_set = set(self.fences)
        writers = {}
        readers = {}
        last_by_chan = {}
        fence_deps = {}
        dma_count = {}
        deps = [set() for _ in self.ops]
        for i, o in enumerate(self.ops):
            if i in fence_set:
                fence_deps = dict(last_by_chan)
                writers = {}
                readers = {}
            for (st, dm_, slot), j in fence_deps.items():
                if st != o["stream"] or dm_:
                    deps[i].add(j)
            for k in o["reads"]:
                deps[i].update(writers.get(k, ()))
            for k in o["writes"]:
                deps[i].update(writers.get(k, ()))
                deps[i].update(readers.get(k, ()))
            for k in o["writes"]:
                writers.setdefault(k, []).append(i)
                readers[k] = []
            for k in o["reads"]:
                readers.setdefault(k, []).append(i)
            if o["dma"]:
                st = o["stream"]
                kcnt = dma_count.get(st, 0)
                dma_count[st] = kcnt + 1
                last_by_chan[(st, True, kcnt % self.DMA_RING[st])] = i
            else:
                last_by_chan[(o["stream"], False, 0)] = i
        for i, o in enumerate(self.ops):
            deps[i] = {j for j in deps[i]
                       if not (o["stream"] == "tensor"
                               and self.ops[j]["stream"] == "tensor"
                               and not self.ops[j]["dma"])}
        for i in range(len(self.ops)):
            for j in deps[i]:
                self.ops[j]["need_inc"] = True
        last_of = {}
        for i, o in enumerate(self.ops):
            if o["dma"]:
                o["need_inc"] = True
            last_of[(o["stream"], o["dma"])] = i
        for i in last_of.values():
            self.ops[i]["need_inc"] = True
        counts = {}
        pos = {}
        dma_idx = {}
        self.op_sem = {}
        for i, o in enumerate(self.ops):
            if o["dma"]:
                st = o["stream"]
                r = self.DMA_RING[st]
                k = dma_idx.get(st, 0)
                dma_idx[st] = k + 1
                sem = f"{st}_d{k % r}"
                prev = counts.get(sem, 0)
                o["throttle"] = (sem, prev) if prev > 0 else None
                counts[sem] = prev + 16
                pos[i] = (sem, prev + 16)
                self.op_sem[i] = sem
            else:
                o["throttle"] = None
                if o["need_inc"]:
                    sem = o["stream"] + "_c"
                    counts[sem] = counts.get(sem, 0) + 1
                    pos[i] = (sem, counts[sem])
                    self.op_sem[i] = sem
        known = {s: {} for s in self.ENGINES}
        nwaits = 0
        for i, o in enumerate(self.ops):
            need = {}
            for j in deps[i]:
                assert j in pos, f"dep {j} without inc"
                sem, c = pos[j]
                need[sem] = max(need.get(sem, 0), c)
            if o["throttle"]:
                sem, c = o["throttle"]
                need[sem] = max(need.get(sem, 0), c)
            w = []
            for sem, c in sorted(need.items()):
                if known[o["stream"]].get(sem, 0) < c:
                    w.append((sem, c))
                    known[o["stream"]][sem] = c
            o["waits"] = w
            nwaits += len(w)
        self.sem_totals = dict(counts)
        self.nwaits = nwaits

    def build(self, nc, block):
        sems = {}
        import contextlib
        self._es = contextlib.ExitStack()
        for name in sorted(set(self.sem_totals) | set(s for _, c in [(0, 0)] for s in [])):
            sems[name] = self._es.enter_context(nc.semaphore(name))
        ops = self.ops
        totals = self.sem_totals
        op_sem = self.op_sem

        def body_for(stream):
            def body(eng):
                for i, o in enumerate(ops):
                    if o["stream"] != stream:
                        continue
                    for sem, c in o["waits"]:
                        eng.wait_ge(sems[sem], c)
                    inst = o["fn"](eng)
                    if i in op_sem:
                        inst.then_inc(sems[op_sem[i]], 16 if o["dma"] else 1)
                if stream == "sync":
                    for sem, c in sorted(totals.items()):
                        eng.wait_ge(sems[sem], c)
            return body

        block.sync(body_for("sync"))
        block.vector(body_for("vector"))
        block.scalar(body_for("scalar"))
        block.tensor(body_for("tensor"))
        block.gpsimd(body_for("gpsimd"))


def make_factor_tiles(weights, transpose):
    FA = np.zeros((7, 32, 128, 128), np.float32)
    FB = np.zeros((5, 32, 128, 128), np.float32)
    for s in range(12):
        st = 2 ** s
        w = np.asarray(weights[s]).reshape(FF // (2 * st), st, 2, 2)
        f = np.arange(FF)
        b = f // (2 * st)
        j = f % st
        half = (f // st) % 2
        dcoef = np.where(half == 0, w[b, j, 0, 0], w[b, j, 1, 1])
        ocoef = np.where(half == 0, w[b, j, 0, 1], w[b, j, 1, 0])
        partner = f ^ st
        if transpose:
            ocoef = np.where(half == 0, w[b, j, 1, 0], w[b, j, 0, 1])
        if s < 7:
            c = f // 128
            p = f % 128
            pp = partner % 128
            FA[s, c, p, p] = dcoef
            FA[s, c, p, pp] = ocoef
        else:
            c = f % 32
            p = f // 32
            pp = partner // 32
            FB[s - 7, c, p, p] = dcoef
            FB[s - 7, c, p, pp] = ocoef
    FAT = FA.transpose(0, 1, 3, 2).copy()
    FBT = FB.transpose(0, 1, 3, 2).copy()
    return FA, FB, FAT, FBT


def make_host_aux(inputs):
    aux = {}
    rb = np.asarray(inputs["L0_attn_rel_bias"])
    idx = np.clip(np.arange(-4095, -4095 + 8192) + 2047, 0, 4094)
    aux["r_pad"] = rb[idx, :].T.copy().astype(np.float32)
    FA, FB, FAT, FBT = make_factor_tiles(np.asarray(inputs["L0_ffn_up_weights"]), False)
    aux["FAu"], aux["FBu"], aux["FATu"], aux["FBTu"] = FA, FB, FAT, FBT
    FA, FB, FAT, FBT = make_factor_tiles(np.asarray(inputs["L0_ffn_down_weights"]), True)
    aux["FAd"], aux["FBd"], aux["FATd"], aux["FBTd"] = FA, FB, FAT, FBT
    aux["ident"] = np.eye(128, dtype=np.float32)
    aux["antident"] = np.eye(128, dtype=np.float32)[::-1].copy()
    ii = np.arange(128)
    aux["m_tri_kq"] = (ii[:, None] <= ii[None, :]).astype(np.float32)
    C = 2 * CT
    st = np.arange(C)
    modd = 2 * np.arange(CT) + 1
    aux["m_l"] = (st[:, None] > st[None, :]).astype(np.float32)
    aux["m_lt"] = aux["m_l"].T.copy()
    aux["m_kbt"] = (modd[:, None] < st[None, :]).astype(np.float32)
    aux["m_aq"] = (st[:, None] <= modd[None, :]).astype(np.float32)
    aux["m_kq"] = (modd[:, None] <= modd[None, :]).astype(np.float32)
    return aux


class B:

    def __init__(self, upto=99, taps=()):
        self.upto = upto
        self.taps = set(taps)
        self.nc = bass.Bass("TRN2", target_bir_lowering=False)
        self.sc = Sched()
        self.dram = {}
        self.tap_handles = {}

    def din(self, name, shape, dtype=F32):
        h = self.nc.dram_tensor(name, list(shape), dtype, kind="ExternalInput")
        self.dram[name] = h
        return h

    def dscratch(self, name, shape, dtype=BF16):
        kind = "ExternalOutput" if name in self.taps else "Internal"
        h = self.nc.dram_tensor(name, list(shape), dtype, kind=kind)
        self.dram[name] = h
        return h

    def dout(self, name, shape, dtype=F32):
        h = self.nc.dram_tensor(name, list(shape), dtype, kind="ExternalOutput")
        self.dram[name] = h
        return h

    def tap(self, name, src_ap, shape, key):
        if name not in self.taps:
            return
        h = self.dout("tap_" + name, shape, F32)
        self.sc.dma("sync", h[tuple(slice(0, s) for s in shape)], src_ap,
                    "tap_" + name, rkeys=(key,))

    def declare(self):
        nc, d = self.nc, self.din
        d("x", (S, D))
        for n in ("L0_attn_Wq", "L0_attn_Wk", "L0_attn_Wv", "L0_attn_Wu"):
            d(n, (D, D))
        d("L0_ffn_W_expT", (FF, D))
        d("L0_ffn_b_exp", (FF,))
        d("L0_ffn_up_bias", (FF,))
        d("L0_ffn_W_con", (FF, D))
        d("L0_ffn_b_con", (D,))
        d("L0_ffn_down_bias", (D,))
        for n in ("L1_attn_Wq", "L1_attn_Wk", "L1_attn_Wv", "L1_attn_Wo"):
            d(n, (D, D))
        d("L1_attn_Wf1", (D, HD))
        d("L1_attn_Wf2", (HD, D))
        d("L1_attn_Wbeta", (D, H))
        d("L1_attn_Wog1", (D, HD))
        d("L1_attn_Wog2", (HD, D))
        d("L1_attn_norm_w", (D,))
        d("L1_ffn_norm_w", (D,))
        d("L1_ffn_w1T", (D, FF))
        d("L1_ffn_w3T", (D, FF))
        d("L1_ffn_w2T", (FF, D))
        d("L1_ffn_gate_wT", (D, H))
        d("L1_ffn_a1T", (D, H * 16))
        d("L1_ffn_a3T", (D, H * 16))
        d("L1_ffn_b1T", (H, 16, FF))
        d("L1_ffn_b3T", (H, 16, FF))
        d("L1_ffn_a2T", (H, FF, 16))
        d("L1_ffn_b2T", (H, 16, D))
        d("r_pad", (H, 8192))
        d("FATu", (7, 32, 128, 128)); d("FAu", (7, 32, 128, 128))
        d("FBTu", (5, 32, 128, 128)); d("FBu", (5, 32, 128, 128))
        d("FATd", (7, 32, 128, 128)); d("FAd", (7, 32, 128, 128))
        d("FBTd", (5, 32, 128, 128)); d("FBd", (5, 32, 128, 128))
        d("ident", (128, 128))
        d("antident", (128, 128))
        d("m_tri_kq", (128, 128))
        d("m_l", (128, 128)); d("m_lt", (128, 128))
        d("m_kbt", (64, 128)); d("m_aq", (128, 64)); d("m_kq", (64, 64))
        self.dout("out", (S, D))

        ds = self.dscratch
        for n in ("L0_attn_Wq", "L0_attn_Wk", "L0_attn_Wv", "L0_attn_Wu",
                  "L1_attn_Wq", "L1_attn_Wk", "L1_attn_Wv", "L1_attn_Wo"):
            ds("c_" + n, (D, D))
        ds("c_W_expT", (FF, D)); ds("c_W_con", (FF, D))
        ds("c_w1T", (D, FF)); ds("c_w3T", (D, FF)); ds("c_w2T", (FF, D))
        ds("c_gate_wT", (D, H))
        ds("c_a1T", (D, H * 16)); ds("c_a3T", (D, H * 16))
        ds("c_b1T", (H, 16, FF)); ds("c_b3T", (H, 16, FF))
        ds("c_a2T", (H, FF, 16)); ds("c_b2T", (H, 16, D))
        ds("c_wf1", (D, HD)); ds("c_wf2", (HD, D))
        ds("c_wbeta", (D, H)); ds("c_wog1", (D, HD)); ds("c_wog2", (HD, D))
        for n in ("FATu", "FAu", "FBTu", "FBu", "FATd", "FAd", "FBTd", "FBd"):
            ds("c_" + n, self.dram[n].shape)
        ds("w1pre", (FF, D))
        ds("xn0", (S, D)); ds("xn1", (S, D))
        ds("q0fm", (D, S)); ds("k0fm", (D, S))
        ds("u0s", (S, D)); ds("v0p", (S, D))
        ds("attn0", (S, D))
        ds("x1", (S, D), F32)
        ds("q1s", (S, D)); ds("kn", (S, D)); ds("knb", (S, D)); ds("v1s", (S, D))
        ds("lsp", (D, S), F32)
        ds("og1fm", (HD, S))
        ds("o1fm", (D, S))
        ds("yn2fm", (D, S))
        ds("l1a", (S, D))
        ds("trms", (S, D))
        ds("wrow", (H, S), F32)
        ds("base1", (FF, S)); ds("base3", (FF, S)); ds("mixedfm", (FF, S))
        ds("u2w", (H, 16, S))

        nb = self.nc
        self.RW1 = nb.alloc_sbuf_tensor("RW1", [128, 32768], BF16)
        self.RW2 = nb.alloc_sbuf_tensor("RW2", [128, 32768], BF16)
        self.RG = nb.alloc_sbuf_tensor("RG", [128, 16384], BF16)
        self.RF = nb.alloc_sbuf_tensor("RF", [128, 8192], F32)
        self.RS = nb.alloc_sbuf_tensor("RS", [128, 1024], F32)
        self.RM = nb.alloc_sbuf_tensor("RM", [128, 3072], BF16)
        self.RIDX = nb.alloc_sbuf_tensor("RIDX", [128, 16], mybir.dt.uint32)
        self.PS = [nb.alloc_psum_tensor(f"ps{i}", [128, 512], F32) for i in range(6)]
        self.PB = [nb.alloc_psum_tensor(f"pb{i}", [128, 1024], BF16) for i in range(2)]

    def p0_prep(self):
        sc, dm = self.sc, self.dram
        for n in ("L0_attn_Wq", "L0_attn_Wk", "L0_attn_Wv", "L0_attn_Wu",
                  "L1_attn_Wq", "L1_attn_Wk", "L1_attn_Wv", "L1_attn_Wo"):
            sc.dma("gpsimd", dm["c_" + n][:, :], dm[n][:, :], "c_" + n)
        sc.dma("gpsimd", dm["c_W_expT"][:, :], dm["L0_ffn_W_expT"][:, :], "c_W_expT")
        sc.dma("gpsimd", dm["c_W_con"][:, :], dm["L0_ffn_W_con"][:, :], "c_W_con")
        for a, b_ in (("c_w1T", "L1_ffn_w1T"), ("c_w3T", "L1_ffn_w3T"),
                      ("c_w2T", "L1_ffn_w2T"), ("c_gate_wT", "L1_ffn_gate_wT"),
                      ("c_a1T", "L1_ffn_a1T"), ("c_a3T", "L1_ffn_a3T"),
                      ("c_wf1", "L1_attn_Wf1"), ("c_wf2", "L1_attn_Wf2"),
                      ("c_wbeta", "L1_attn_Wbeta"), ("c_wog1", "L1_attn_Wog1"),
                      ("c_wog2", "L1_attn_Wog2")):
            sc.dma("gpsimd", dm[a][:, :], dm[b_][:, :], a)
        for a, b_ in (("c_b1T", "L1_ffn_b1T"), ("c_b3T", "L1_ffn_b3T"),
                      ("c_a2T", "L1_ffn_a2T"), ("c_b2T", "L1_ffn_b2T")):
            sc.dma("gpsimd", dm[a][:, :, :], dm[b_][:, :, :], a)
        for n in ("FATu", "FAu", "FBTu", "FBu", "FATd", "FAd", "FBTd", "FBd"):
            sc.dma("gpsimd", dm["c_" + n][:, :, :, :], dm[n][:, :, :, :], "c_" + n)
        RM = self.RM
        sc.dma("gpsimd", RM[:, 0:128], dm["ident"][:, :], "RM_id")
        sc.dma("gpsimd", RM[:, 128:256], dm["antident"][:, :], "RM_J")
        sc.dma("gpsimd", RM[:, 256:384], dm["m_tri_kq"][:, :], "RM_tri")
        sc.dma("gpsimd", RM[:, 384:512], dm["m_l"][:, :], "RM_ml")
        sc.dma("gpsimd", RM[:, 512:640], dm["m_lt"][:, :], "RM_mlt")
        sc.dma("gpsimd", RM[0:64, 640:768], dm["m_kbt"][:, :], "RM_mkbt")
        sc.dma("gpsimd", RM[:, 768:832], dm["m_aq"][:, :], "RM_maq")
        sc.dma("gpsimd", RM[0:64, 832:896], dm["m_kq"][:, :], "RM_mkq")
        self.sc.op("vector", lambda v: v.memset(RM[0:1, 896:1024], 1.0), writes=("RM_ones",))
        self.sc.op("vector", lambda v: v.memset(RM[0:1, 1024:1025], 1.0), writes=("RM_one1",))
        RS = self.RS
        self.sc.op("vector", lambda v: v.memset(RS[:, 0:1], EPS6), writes=("RS_eps6",))
        self.sc.op("vector", lambda v: v.memset(RS[:, 1:2], EPS1), writes=("RS_eps1",))
        sc.dma("sync", RS[:, 8:16], bass.AP(dm["L1_attn_norm_w"], 0, [[1, 128], [128, 8]]),
               "RS_nw1", allow_slow_non_contiguous=True)
        self.sc.op("vector", lambda v: v.memset(RS[:, 512:576], 0.0), writes=("RS_zero64",))
        self.sc.op("vector", lambda v: v.memset(self.RM[:, 1026:1027], 1.0), writes=("RM_onescol",))

    def p2_ln0(self):
        sc, dm = self.sc, self.dram
        RF, RS = self.RF, self.RS
        for it in range(S // 128):
            t0 = it * 128
            xk = f"ln0x{it & 1}"
            xt = RF[:, (it & 1) * 1024:(it & 1) * 1024 + 1024]
            sc.dma("sync", xt, dm["x"][t0:t0 + 128, :], xk)
            sq = RF[:, 2048 + (it & 1) * 1024: 3072 + (it & 1) * 1024]
            sqk = f"ln0sq{it & 1}"
            cols = RS[:, 16 + (it & 1) * 8: 24 + (it & 1) * 8]
            ck = f"ln0c{it & 1}"
            sc.op("vector", lambda v, xt=xt, c=cols: v.reduce_sum(c[:, 0:1], xt, axis=mybir.AxisListType.X),
                  reads=(xk,), writes=(ck,))
            sc.op("scalar", lambda s, xt=xt, sq=sq: s.activation(sq, xt, AF.Square),
                  reads=(xk,), writes=(sqk,))
            sc.op("vector", lambda v, sq=sq, c=cols: v.reduce_sum(c[:, 1:2], sq, axis=mybir.AxisListType.X),
                  reads=(sqk,), writes=(ck,))
            sc.op("vector", lambda v, c=cols: v.tensor_scalar_mul(c[:, 2:3], c[:, 0:1], 1.0 / D),
                  reads=(ck,), writes=(ck,))
            sc.op("vector", lambda v, c=cols: v.tensor_mul(c[:, 3:4], c[:, 2:3], c[:, 2:3]),
                  reads=(ck,), writes=(ck,))
            sc.op("vector", lambda v, c=cols: v.tensor_scalar(c[:, 4:5], c[:, 1:2], 1.0 / D, None, ALU.mult),
                  reads=(ck,), writes=(ck,))
            sc.op("vector", lambda v, c=cols: v.tensor_sub(c[:, 4:5], c[:, 4:5], c[:, 3:4]),
                  reads=(ck,), writes=(ck,))
            sc.op("scalar", lambda s, c=cols: s.activation(c[:, 5:6], c[:, 4:5], AF.Sqrt, bias=self.RS[:, 1:2]),
                  reads=(ck, "RS_eps1"), writes=(ck,))
            sc.op("vector", lambda v, c=cols: v.reciprocal(c[:, 6:7], c[:, 5:6]),
                  reads=(ck,), writes=(ck,))
            xn = self.RG[:, (it & 1) * 1024:(it & 1) * 1024 + 1024]
            xnk = f"ln0o{it & 1}"
            sc.op("vector", lambda v, xt=xt, xn=xn, c=cols: v.tensor_scalar(
                xn, xt, c[:, 2:3], c[:, 6:7], ALU.subtract, ALU.mult),
                reads=(xk, ck), writes=(xnk,))
            sc.dma("sync", dm["xn0"][t0:t0 + 128, :], xn, "xn0", rkeys=(xnk,))

    def load_w_lhsT(self, cname, region, off, douts, key):
        for ch in range(8):
            self.sc.dma("sync", region[:, off + ch * douts: off + (ch + 1) * douts],
                        self.dram[cname][ch * 128:(ch + 1) * 128, 0:douts],
                        key, rkeys=(cname,))

    def p3_l0proj(self):
        sc, dm = self.sc, self.dram
        RW1, RG, RF, RS, RM = self.RW1, self.RG, self.RF, self.RS, self.RM
        self.load_w_lhsT("c_L0_attn_Wq", RW1, 0, D, "wq0")
        self.load_w_lhsT("c_L0_attn_Wk", RW1, 8192, D, "wk0")
        self.load_w_lhsT("c_L0_attn_Wv", RW1, 16384, D, "wv0")
        self.load_w_lhsT("c_L0_attn_Wu", RW1, 24576, D, "wu0")
        PS = self.PS
        for tt in range(8):
            t0 = tt * 512
            pb = tt & 1
            xnf = RG[:, pb * 4096: pb * 4096 + 4096]
            xnfk = f"p3xn{pb}"
            for ch in range(8):
                sc.dma("sync", xnf[:, ch * 512:(ch + 1) * 512],
                       dm["xn0"][t0:t0 + 512, ch * 128:(ch + 1) * 128],
                       xnfk, rkeys=("xn0",), transpose=True)
            for wname, woff, dst in (("wq0", 0, "q0fm"), ("wk0", 8192, "k0fm")):
                for hh in range(8):
                    ps = PS[hh % 2]
                    pk = f"p3ps{hh % 2}"
                    for ch in range(8):
                        sc.op("tensor", lambda t, ps=ps, ch=ch, hh=hh, woff=woff, xnf=xnf:
                              t.matmul(ps[:, 0:512], self.RW1[:, woff + ch * D + hh * 128: woff + ch * D + hh * 128 + 128],
                                       xnf[:, ch * 512:(ch + 1) * 512],
                                       start=(ch == 0), stop=(ch == 7)),
                              reads=(wname, xnfk), writes=(pk,))
                    ev = RG[:, 8192 + (hh % 2) * 512: 8192 + (hh % 2) * 512 + 512]
                    evk = f"p3ev{hh % 2}"
                    sc.op("scalar", lambda s, ps=ps, ev=ev: s.activation(ev, ps[:, 0:512], AF.Silu),
                          reads=(pk,), writes=(evk,))
                    sc.dma("sync", dm[dst][hh * 128:(hh + 1) * 128, t0:t0 + 512], ev,
                           dst, rkeys=(evk,))
            for wname, woff, dst, dok in (("wu0", 24576, "u0s", 0), ("wv0", 16384, "v0p", 1)):
                for st4 in range(4):
                    ts0 = t0 + st4 * 128
                    for dh in range(2):
                        ps = PS[2 + dh]
                        pk = f"p3pt{dh}"
                        for ch in range(8):
                            sc.op("tensor", lambda t, ps=ps, ch=ch, dh=dh, woff=woff, xnf=xnf, st4=st4:
                                  t.matmul(ps[:, 0:512],
                                           xnf[:, ch * 512 + st4 * 128: ch * 512 + st4 * 128 + 128],
                                           self.RW1[:, woff + ch * D + dh * 512: woff + ch * D + dh * 512 + 512],
                                           start=(ch == 0), stop=(ch == 7)),
                                  reads=(wname, xnfk), writes=(pk,))
                    ev = RG[:, 9216 + (st4 & 1) * 1024: 9216 + (st4 & 1) * 1024 + 1024]
                    evk = f"p3evt{st4 & 1}"
                    for dh in range(2):
                        sc.op("scalar", lambda s, dh=dh, ev=ev: s.activation(
                            ev[:, dh * 512:(dh + 1) * 512], PS[2 + dh][:, 0:512], AF.Silu),
                            reads=(f"p3pt{dh}",), writes=(evk,))
                    sc.dma("sync", dm[dst][ts0:ts0 + 128, :], ev, dst, rkeys=(evk,))

    def p3b_keymask(self):
        sc, dm = self.sc, self.dram
        RG, RF = self.RG, self.RF
        for it in range(S // 128):
            t0 = it * 128
            pb = it & 1
            ktk = f"kmk{pb}"
            kt = RG[:, pb * 1024: pb * 1024 + 1024]
            for hh in range(8):
                sc.dma("sync", kt[:, hh * 128:(hh + 1) * 128],
                       dm["k0fm"][hh * 128:(hh + 1) * 128, t0:t0 + 128],
                       ktk, rkeys=("k0fm",), transpose=True)
            km = RF[:, 4096 + pb * 16: 4096 + pb * 16 + 16]
            kmk = f"kmc{pb}"
            for hh in range(8):
                sc.op("vector", lambda v, hh=hh, kt=kt, km=km: v.reduce_sum(
                    km[:, hh:hh + 1], kt[:, hh * 128:(hh + 1) * 128],
                    axis=mybir.AxisListType.X, apply_absolute_value=True),
                    reads=(ktk,), writes=(kmk,))
            sc.op("scalar", lambda s, km=km: s.activation(km[:, 8:16], km[:, 0:8], AF.Sign),
                  reads=(kmk,), writes=(kmk,))
            vt = RG[:, 2048 + pb * 1024: 2048 + pb * 1024 + 1024]
            vtk = f"kmv{pb}"
            sc.dma("sync", vt, dm["v0p"][t0:t0 + 128, :], vtk, rkeys=("v0p",))
            for hh in range(8):
                sc.op("vector", lambda v, hh=hh, vt=vt, km=km: v.tensor_scalar_mul(
                    vt[:, hh * 128:(hh + 1) * 128], vt[:, hh * 128:(hh + 1) * 128],
                    km[:, 8 + hh:9 + hh]),
                    reads=(kmk,), writes=(vtk,))
            sc.dma("sync", dm["v0p"][t0:t0 + 128, :], vt, "v0p", rkeys=(vtk,))

    def p4_l0attn(self):
        sc, dm = self.sc, self.dram
        RW1, RG, RM, PS = self.RW1, self.RG, self.RM, self.PS
        SK = 1.0 / np.sqrt(128.0)
        for h in range(H):
            skw = RW1[:, 0:8064]
            sc.dma("gpsimd", skw, bass.AP(dm["r_pad"], h * 8192, [[1, 128], [1, 8064]]),
                   "skw")
            kfm = RW1[:, 8192:12288]
            sc.dma("sync", kfm, dm["k0fm"][h * 128:(h + 1) * 128, :], "kfm")
            vst = RW1[:, 12288:16384]
            sc.dma("sync", vst, bass.AP(dm["v0p"], h * 128,
                                        [[1024, 128], [128 * 1024, 32], [1, 128]]),
                   "vst")
            for qg in range(8):
                q0 = qg * 512
                qb = RG[:, 0:512]
                sc.dma("sync", qb, dm["q0fm"][h * 128:(h + 1) * 128, q0:q0 + 512], "p4q")
                ub = RG[:, 512:1024]
                for sub in range(4):
                    sc.dma("sync", ub[:, sub * 128:(sub + 1) * 128],
                           dm["u0s"][q0 + sub * 128: q0 + (sub + 1) * 128,
                                     h * 128:(h + 1) * 128], "p4u")
                nkt = 4 * qg + 4

                def emit_scores(kt):
                    ps = PS[kt & 1]
                    pk = f"p4s{kt & 1}"
                    base = q0 - kt * 128 + 3968
                    sc.op("tensor", lambda t, ps=ps, kt=kt, qb=qb, kfm=kfm:
                          t.matmul(ps[:, 0:512], kfm[:, kt * 128:(kt + 1) * 128], qb,
                                   start=True, stop=False),
                          reads=("kfm", "p4q"), writes=(pk,))
                    sc.op("tensor", lambda t, ps=ps, base=base, skw=skw:
                          t.matmul(ps[:, 0:512], self.RM[:, 128:256],
                                   skw[:, base:base + 512], start=False, stop=True),
                          reads=("skw", "RM_J"), writes=(pk,))
                    wb = RG[:, 1024 + (kt & 1) * 512: 1024 + (kt & 1) * 512 + 512]
                    wk = f"p4w{kt & 1}"
                    sc.op("scalar", lambda s, ps=ps, wb=wb: s.activation(
                        wb, ps[:, 0:512], AF.Silu, scale=SK),
                        reads=(f"p4s{kt & 1}",), writes=(wk,))
                    j = kt - 4 * qg
                    if j >= 0:
                        if j > 0:
                            sc.op("vector", lambda v, wb=wb, j=j: v.memset(wb[:, 0:j * 128], 0.0),
                                  reads=(), writes=(wk,))
                        sc.op("vector", lambda v, wb=wb, j=j: v.tensor_mul(
                            wb[:, j * 128:(j + 1) * 128], wb[:, j * 128:(j + 1) * 128],
                            self.RM[:, 256:384]),
                            reads=("RM_tri",), writes=(wk,))

                def emit_attn(kt):
                    wb = RG[:, 1024 + (kt & 1) * 512: 1024 + (kt & 1) * 512 + 512]
                    wk = f"p4w{kt & 1}"
                    pa = PS[2]
                    for sub in range(4):
                        sc.op("tensor", lambda t, pa=pa, wb=wb, kt=kt, sub=sub, vst=vst, nkt=nkt:
                              t.matmul(pa[:, sub * 128:(sub + 1) * 128],
                                       wb[:, sub * 128:(sub + 1) * 128],
                                       vst[:, kt * 128:(kt + 1) * 128],
                                       start=(kt == 0 and sub == 0),
                                       stop=(kt == nkt - 1 and sub == 3)),
                              reads=(wk, "vst"), writes=("p4acc",))

                emit_scores(0)
                for kt in range(1, nkt):
                    emit_scores(kt)
                    emit_attn(kt - 1)
                emit_attn(nkt - 1)
                ob = RG[:, 3072:3584]
                sc.op("vector", lambda v, ob=ob, ub=ub: v.tensor_mul(
                    ob, PS[2][:, 0:512], ub),
                    reads=("p4acc", "p4u"), writes=("p4o",))
                sc.dma("sync",
                       bass.AP(dm["attn0"], q0 * D + h * 128,
                               [[D, 128], [128 * D, 4], [1, 128]]),
                       ob, "attn0", rkeys=("p4o",))
            self.sc.fence()

    def _compose(self, f_name, ft_name, stage_list, c, mslot, mtslot, keyb):
        sc, dm, RG = self.sc, self.dram, self.RG
        s0 = stage_list[0]
        m = RG[:, mslot:mslot + 128]
        mt = RG[:, mtslot:mtslot + 128]
        sc.dma("sync", m, dm[f_name][s0, c, :, :], keyb + "m")
        sc.dma("sync", mt, dm[ft_name][s0, c, :, :], keyb + "mt")
        ftile = RG[:, mslot + 512:mslot + 640]
        for i, s in enumerate(stage_list[1:]):
            sc.dma("sync", ftile, dm[ft_name][s, c, :, :], keyb + "f")
            pm, pmt = self.PS[4], self.PS[5]
            sc.op("tensor", lambda t, ftile=ftile, m=m, pm=pm: t.matmul(
                pm[:, 0:128], ftile, m, start=True, stop=True),
                reads=(keyb + "f", keyb + "m"), writes=(keyb + "pm",))
            sc.op("tensor", lambda t, ftile=ftile, mt=mt, pmt=pmt: t.matmul(
                pmt[:, 0:128], m, ftile, start=True, stop=True),
                reads=(keyb + "f", keyb + "m", keyb + "mt"), writes=(keyb + "pmt",))
            sc.op("vector", lambda v, m=m, pm=pm: v.tensor_copy(m, pm[:, 0:128]),
                  reads=(keyb + "pm",), writes=(keyb + "m",))
            sc.op("scalar", lambda s_, mt=mt, pmt=pmt: s_.copy(mt, pmt[:, 0:128]),
                  reads=(keyb + "pmt",), writes=(keyb + "mt",))
        return m, mt

    def p1_fold(self):
        sc, dm, RG, PS = self.sc, self.dram, self.RG, self.PS
        self.dscratch("w1a", (FF, D))
        self.dscratch("w2mid", (FF, D))
        self.dscratch("b1a", (FF,)); self.dscratch("b1b", (FF,))
        wct = RG[:, 1024:2048]
        bcol = RG[:, 2048:2049]
        for sub in range(4):
            for c in range(32):
                if sub == 0:
                    sc.dma("sync", wct, dm["c_W_expT"][c * 128:(c + 1) * 128, :], "p1w",
                           rkeys=("c_W_expT",))
                    sc.dma("gpsimd", bcol, bass.AP(dm["L0_ffn_b_exp"], c * 128, [[1, 128], [1, 1]]),
                           "p1b")
                    m, mt = self._compose("c_FAu", "c_FATu", [0, 1, 2, 3, 4, 5, 6], c, 4096, 4352, "p1")
                elif sub == 1:
                    sc.dma("sync", wct, bass.AP(dm["w1a"], c * 1024, [[32 * 1024, 128], [1, 1024]]),
                           "p1w", rkeys=("w1a",))
                    sc.dma("sync", bcol, bass.AP(dm["b1a"], c, [[32, 128], [1, 1]]),
                           "p1b", rkeys=("b1a",), allow_slow_non_contiguous=True)
                    m, mt = self._compose("c_FBu", "c_FBTu", [0, 1, 2, 3, 4], c, 4096, 4352, "p1")
                elif sub == 2:
                    sc.dma("sync", wct, bass.AP(dm["c_W_con"], c * 1024, [[32 * 1024, 128], [1, 1024]]),
                           "p1w", rkeys=("c_W_con",))
                    m, mt = self._compose("c_FBd", "c_FBTd", [4, 3, 2, 1, 0], c, 4096, 4352, "p1")
                else:
                    sc.dma("sync", wct, dm["w2mid"][c * 128:(c + 1) * 128, :], "p1w",
                           rkeys=("w2mid",))
                    m, mt = self._compose("c_FAd", "c_FATd", [6, 5, 4, 3, 2, 1, 0], c, 4096, 4352, "p1")
                for dh in range(2):
                    ps = PS[dh]
                    sc.op("tensor", lambda t, ps=ps, mt=mt, dh=dh, wct=wct: t.matmul(
                        ps[:, 0:512], mt, wct[:, dh * 512:(dh + 1) * 512],
                        start=True, stop=True),
                        reads=("p1m", "p1mt", "p1w"), writes=(f"p1p{dh}",))
                oc = RG[:, 2560:3584]
                for dh in range(2):
                    sc.op("vector", lambda v, oc=oc, dh=dh: v.tensor_copy(
                        oc[:, dh * 512:(dh + 1) * 512], PS[dh][:, 0:512]),
                        reads=(f"p1p{dh}",), writes=("p1o",))
                if sub in (0, 1):
                    ps = PS[2]
                    sc.op("tensor", lambda t, ps=ps, mt=mt, bcol=bcol: t.matmul(
                        ps[:, 0:1], mt, bcol, start=True, stop=True),
                        reads=("p1m", "p1mt", "p1b"), writes=("p1pb",))
                    ob = RG[:, 2049:2050]
                    sc.op("vector", lambda v, ob=ob, ps=ps: v.tensor_copy(ob, ps[:, 0:1]),
                          reads=("p1pb",), writes=("p1ob",))
                if sub == 0:
                    sc.dma("sync", dm["w1a"][c * 128:(c + 1) * 128, :], oc, "w1a", rkeys=("p1o",))
                    sc.dma("sync", bass.AP(dm["b1a"], c * 128, [[1, 128], [1, 1]]),
                           RG[:, 2049:2050], "b1a", rkeys=("p1ob",))
                elif sub == 1:
                    sc.dma("sync", bass.AP(dm["w1pre"], c * 1024, [[32 * 1024, 128], [1, 1024]]),
                           oc, "w1pre", rkeys=("p1o",))
                    sc.dma("sync", bass.AP(dm["b1b"], c, [[32, 128], [1, 1]]),
                           RG[:, 2049:2050], "b1b", rkeys=("p1ob",), allow_slow_non_contiguous=True)
                elif sub == 2:
                    sc.dma("sync", bass.AP(dm["w2mid"], c * 1024, [[32 * 1024, 128], [1, 1024]]),
                           oc, "w2mid", rkeys=("p1o",))
                else:
                    sc.op("vector", lambda v, c=c, oc=oc: v.tensor_copy(
                        self.RW2[:, c * 1024:(c + 1) * 1024], oc),
                        reads=("p1o",), writes=("W2res",))
                self.sc.fence()

    def _ln_tile(self, xt, xk, csl, ck, xn_out, xnk, eps_key="RS_eps1", par=0):
        sc, RS = self.sc, self.RS
        sq = self.RF[:, 6144 + par * 1024: 7168 + par * 1024]
        sqk = f"lnsq{par}"
        sc.op("vector", lambda v: v.reduce_sum(csl[:, 0:1], xt, axis=mybir.AxisListType.X),
              reads=(xk,), writes=(ck,))
        sc.op("scalar", lambda s: s.activation(sq, xt, AF.Square), reads=(xk,), writes=(sqk,))
        sc.op("vector", lambda v: v.reduce_sum(csl[:, 1:2], sq, axis=mybir.AxisListType.X),
              reads=(sqk,), writes=(ck,))
        sc.op("vector", lambda v: v.tensor_scalar_mul(csl[:, 2:3], csl[:, 0:1], 1.0 / D),
              reads=(ck,), writes=(ck,))
        sc.op("vector", lambda v: v.tensor_mul(csl[:, 3:4], csl[:, 2:3], csl[:, 2:3]),
              reads=(ck,), writes=(ck,))
        sc.op("vector", lambda v: v.tensor_scalar_mul(csl[:, 4:5], csl[:, 1:2], 1.0 / D),
              reads=(ck,), writes=(ck,))
        sc.op("vector", lambda v: v.tensor_sub(csl[:, 4:5], csl[:, 4:5], csl[:, 3:4]),
              reads=(ck,), writes=(ck,))
        ecol = self.RS[:, 1:2] if eps_key == "RS_eps1" else self.RS[:, 0:1]
        sc.op("scalar", lambda s: s.activation(csl[:, 5:6], csl[:, 4:5], AF.Sqrt, bias=ecol),
              reads=(ck, eps_key), writes=(ck,))
        sc.op("vector", lambda v: v.reciprocal(csl[:, 6:7], csl[:, 5:6]),
              reads=(ck,), writes=(ck,))
        sc.op("vector", lambda v: v.tensor_scalar(xn_out, xt, csl[:, 2:3], csl[:, 6:7],
                                                  ALU.subtract, ALU.mult),
              reads=(xk, ck), writes=(xnk,))

    def p5_l0ffn(self):
        sc, dm = self.sc, self.dram
        RW1, RW2, RG, RF, RS, RM, PS = self.RW1, self.RW2, self.RG, self.RF, self.RS, self.RM, self.PS
        for ch in range(8):
            sc.dma("sync", RW1[:, ch * 4096:(ch + 1) * 4096],
                   dm["w1pre"][:, ch * 128:(ch + 1) * 128], "W1res",
                   rkeys=("w1pre",), transpose=True)
        for ft in range(32):
            sc.dma("gpsimd", RF[:, 0:1], bass.AP(dm["b1b"], ft * 128, [[1, 128], [1, 1]]),
                   "p5b1", rkeys=("b1b",))
            sc.dma("sync", RF[:, 1:2], bass.AP(dm["L0_ffn_up_bias"], ft * 128, [[1, 128], [1, 1]]),
                   "p5ub")
            sc.op("vector", lambda v, ft=ft: v.tensor_add(RS[:, 32 + ft:33 + ft],
                                                          RF[:, 0:1], RF[:, 1:2]),
                  reads=("p5b1", "p5ub"), writes=("bgelu",))
            self.sc.fence()
        sc.dma("sync", RF[0:1, 0:1024], dm["L0_ffn_b_con"][:], "p5bc")
        sc.dma("sync", RF[0:1, 1024:2048], dm["L0_ffn_down_bias"][:], "p5db")
        sc.op("vector", lambda v: v.tensor_add(RF[0:1, 0:1024], RF[0:1, 0:1024],
                                               RF[0:1, 1024:2048]),
              reads=("p5bc", "p5db"), writes=("p5bc",))
        sc.op("vector", lambda v: v.tensor_copy(RM[0:1, 2048:3072], RF[0:1, 0:1024]),
              reads=("p5bc",), writes=("bsum",))
        self.sc.fence()
        for tt in range(16):
            t0 = tt * 256
            pb = tt & 1
            xb = RG[:, pb * 2048: pb * 2048 + 2048]
            xbk = f"p5x{pb}"
            for ch in range(8):
                sc.dma("sync", xb[:, ch * 256:(ch + 1) * 256],
                       dm["xn0"][t0:t0 + 256, ch * 128:(ch + 1) * 128],
                       xbk, rkeys=("xn0",), transpose=True)
            h2 = RG[:, 4096:12288]
            h2k = "p5h2"
            for ft in range(32):
                ps = PS[ft & 1]
                pk = f"p5p{ft & 1}"
                for ch in range(8):
                    sc.op("tensor", lambda t, ps=ps, ch=ch, ft=ft, xb=xb: t.matmul(
                        ps[:, 0:256], RW1[:, ch * 4096 + ft * 128: ch * 4096 + ft * 128 + 128],
                        xb[:, ch * 256:(ch + 1) * 256], start=(ch == 0), stop=(ch == 7)),
                        reads=("W1res", xbk), writes=(pk,))
                sc.op("scalar", lambda s, ps=ps, ft=ft, h2=h2: s.activation(
                    h2[:, ft * 256:(ft + 1) * 256], ps[:, 0:256], AF.Gelu,
                    bias=RS[:, 32 + ft:33 + ft]),
                    reads=(pk, "bgelu"), writes=(h2k,))
            for t128 in range(2):
                ts0 = t0 + t128 * 128
                x0t = RF[:, 0:1024]
                sc.dma("sync", x0t, dm["x"][ts0:ts0 + 128, :], "p5x0")
                att = RG[:, 12288:13312]
                sc.dma("sync", att, dm["attn0"][ts0:ts0 + 128, :], "p5at", rkeys=("attn0",))
                x1t = RF[:, 1024:2048]
                for dh in range(2):
                    ps = PS[2 + dh]
                    pk = f"p5y{dh}"
                    for ft in range(32):
                        sc.op("tensor", lambda t, ps=ps, ft=ft, dh=dh, h2=h2, t128=t128:
                              t.matmul(ps[:, 0:512],
                                       h2[:, ft * 256 + t128 * 128: ft * 256 + t128 * 128 + 128],
                                       RW2[:, ft * 1024 + dh * 512: ft * 1024 + dh * 512 + 512],
                                       start=(ft == 0), stop=False),
                              reads=(h2k, "W2res"), writes=(pk,))
                    sc.op("tensor", lambda t, ps=ps, dh=dh: t.matmul(
                        ps[:, 0:512], RM[0:1, 896:1024], RM[0:1, 2048 + dh * 512: 2048 + (dh + 1) * 512],
                        start=False, stop=True),
                        reads=("RM_ones", "bsum"), writes=(pk,))
                    sc.op("vector", lambda v, ps=ps, dh=dh, x0t=x0t, x1t=x1t: v.tensor_add(
                        x1t[:, dh * 512:(dh + 1) * 512], ps[:, 0:512], x0t[:, dh * 512:(dh + 1) * 512]),
                        reads=(pk, "p5x0"), writes=("p5x1",))
                    sc.op("vector", lambda v, dh=dh, x1t=x1t, att=att: v.tensor_add(
                        x1t[:, dh * 512:(dh + 1) * 512], x1t[:, dh * 512:(dh + 1) * 512],
                        att[:, dh * 512:(dh + 1) * 512]),
                        reads=("p5at",), writes=("p5x1",))
                sc.dma("sync", dm["x1"][ts0:ts0 + 128, :], x1t, "x1", rkeys=("p5x1",))
                xn1t = RG[:, 13312:14336]
                self._ln_tile(x1t, f"p5x1", RS[:, 16 + t128 * 8:24 + t128 * 8], f"p5lc{t128}",
                              xn1t, "p5xn1", par=t128)
                sc.dma("sync", dm["xn1"][ts0:ts0 + 128, :], xn1t, "xn1", rkeys=("p5xn1",))
            self.sc.fence()


    def p6_l1proj(self):
        sc, dm = self.sc, self.dram
        RW1, RG, RF, RS, PS = self.RW1, self.RG, self.RF, self.RS, self.PS
        self.load_w_lhsT("c_L1_attn_Wq", RW1, 0, D, "wq1")
        self.load_w_lhsT("c_L1_attn_Wk", RW1, 8192, D, "wk1")
        self.load_w_lhsT("c_L1_attn_Wv", RW1, 16384, D, "wv1")
        self.load_w_lhsT("c_wf1", RW1, 24576, HD, "wf1")
        self.load_w_lhsT("c_wog1", RW1, 25600, HD, "wog1")
        self.load_w_lhsT("c_wbeta", RW1, 26624, H, "wbeta")
        sc.dma("sync", RW1[:, 27648:28672], dm["c_wf2"][:, :], "wf2", rkeys=("c_wf2",))
        for tt in range(8):
            t0 = tt * 512
            pb = tt & 1
            xb = RG[:, pb * 4096: pb * 4096 + 4096]
            xbk = f"p6xn{pb}"
            for ch in range(8):
                sc.dma("sync", xb[:, ch * 512:(ch + 1) * 512],
                       dm["xn1"][t0:t0 + 512, ch * 128:(ch + 1) * 128],
                       xbk, rkeys=("xn1",), transpose=True)
            for wname, woff, dst in (("wq1", 0, "q1s"), ("wv1", 16384, "v1s")):
                for st4 in range(4):
                    ts0 = t0 + st4 * 128
                    for dh in range(2):
                        ps = PS[dh]
                        pk = f"p6pt{dh}"
                        for ch in range(8):
                            sc.op("tensor", lambda t, ps=ps, ch=ch, dh=dh, woff=woff, xb=xb, st4=st4:
                                  t.matmul(ps[:, 0:512],
                                           xb[:, ch * 512 + st4 * 128: ch * 512 + st4 * 128 + 128],
                                           RW1[:, woff + ch * D + dh * 512: woff + ch * D + dh * 512 + 512],
                                           start=(ch == 0), stop=(ch == 7)),
                                  reads=(wname, xbk), writes=(pk,))
                    ev = RG[:, 8192 + (st4 & 1) * 1024: 8192 + (st4 & 1) * 1024 + 1024]
                    evk = f"p6ev{st4 & 1}"
                    for dh in range(2):
                        sc.op("scalar", lambda s, dh=dh, ev=ev: s.activation(
                            ev[:, dh * 512:(dh + 1) * 512], PS[dh][:, 0:512], AF.Silu),
                            reads=(f"p6pt{dh}",), writes=(evk,))
                    sc.dma("sync", dm[dst][ts0:ts0 + 128, :], ev, dst, rkeys=(evk,))
            for st4 in range(4):
                ts0 = t0 + st4 * 128
                for dh in range(2):
                    ps = PS[dh]
                    pk = f"p6pt{dh}"
                    for ch in range(8):
                        sc.op("tensor", lambda t, ps=ps, ch=ch, dh=dh, xb=xb, st4=st4:
                              t.matmul(ps[:, 0:512],
                                       xb[:, ch * 512 + st4 * 128: ch * 512 + st4 * 128 + 128],
                                       RW1[:, 8192 + ch * D + dh * 512: 8192 + ch * D + dh * 512 + 512],
                                       start=(ch == 0), stop=(ch == 7)),
                              reads=("wk1", xbk), writes=(pk,))
                kt = RG[:, 8192 + (st4 & 1) * 1024: 8192 + (st4 & 1) * 1024 + 1024]
                ktk = f"p6ev{st4 & 1}"
                for dh in range(2):
                    sc.op("scalar", lambda s, dh=dh, kt=kt: s.activation(
                        kt[:, dh * 512:(dh + 1) * 512], PS[dh][:, 0:512], AF.Silu),
                        reads=(f"p6pt{dh}",), writes=(ktk,))
                psb = PS[2]
                for ch in range(8):
                    sc.op("tensor", lambda t, ch=ch, xb=xb, st4=st4, psb=psb:
                          t.matmul(psb[:, 0:8],
                                   xb[:, ch * 512 + st4 * 128: ch * 512 + st4 * 128 + 128],
                                   RW1[:, 26624 + ch * H: 26624 + (ch + 1) * H],
                                   start=(ch == 0), stop=(ch == 7)),
                          reads=("wbeta", xbk), writes=("p6bps",))
                cols = RF[:, 6144 + (st4 & 1) * 32: 6144 + (st4 & 1) * 32 + 32]
                ck = f"p6c{st4 & 1}"
                sc.op("scalar", lambda s, cols=cols, psb=psb: s.activation(
                    cols[:, 0:8], psb[:, 0:8], AF.Sigmoid),
                    reads=("p6bps",), writes=(ck,))
                sc.op("vector", lambda v, cols=cols: v.tensor_scalar_mul(
                    cols[:, 0:8], cols[:, 0:8], -2.0), reads=(ck,), writes=(ck,))
                sq = RF[:, 4096 + (st4 & 1) * 1024: 4096 + (st4 & 1) * 1024 + 1024]
                sqk = f"p6sq{st4 & 1}"
                sc.op("vector", lambda v, kt=kt, sq=sq: v.tensor_mul(sq, kt, kt),
                      reads=(ktk,), writes=(sqk,))
                for hh in range(8):
                    sc.op("vector", lambda v, hh=hh, sq=sq, cols=cols: v.reduce_sum(
                        cols[:, 8 + hh:9 + hh], sq[:, hh * 128:(hh + 1) * 128],
                        axis=mybir.AxisListType.X), reads=(sqk,), writes=(ck,))
                sc.op("scalar", lambda s, cols=cols: s.activation(
                    cols[:, 16:24], cols[:, 8:16], AF.Sqrt, bias=self.RS[:, 0:1]),
                    reads=(ck, "RS_eps6"), writes=(ck,))
                sc.op("vector", lambda v, cols=cols: v.reciprocal(cols[:, 24:32], cols[:, 16:24]),
                      reads=(ck,), writes=(ck,))
                knt = RG[:, 10240 + (st4 & 1) * 1024: 10240 + (st4 & 1) * 1024 + 1024]
                knbt = RG[:, 12288 + (st4 & 1) * 1024: 12288 + (st4 & 1) * 1024 + 1024]
                knk = f"p6kn{st4 & 1}"
                for hh in range(8):
                    sc.op("vector", lambda v, hh=hh, kt=kt, knt=knt, cols=cols: v.tensor_scalar_mul(
                        knt[:, hh * 128:(hh + 1) * 128], kt[:, hh * 128:(hh + 1) * 128],
                        cols[:, 24 + hh:25 + hh]), reads=(ktk, ck), writes=(knk,))
                    sc.op("vector", lambda v, hh=hh, knt=knt, knbt=knbt, cols=cols: v.tensor_scalar_mul(
                        knbt[:, hh * 128:(hh + 1) * 128], knt[:, hh * 128:(hh + 1) * 128],
                        cols[:, hh:hh + 1]), reads=(knk, ck), writes=(knk,))
                sc.dma("sync", dm["kn"][ts0:ts0 + 128, :], knt, "kn", rkeys=(knk,))
                sc.dma("sync", dm["knb"][ts0:ts0 + 128, :], knbt, "knb", rkeys=(knk,))
            ps = PS[3]
            for ch in range(8):
                sc.op("tensor", lambda t, ch=ch, xb=xb, ps=ps: t.matmul(
                    ps[:, 0:512], RW1[:, 24576 + ch * HD: 24576 + (ch + 1) * HD],
                    xb[:, ch * 512:(ch + 1) * 512], start=(ch == 0), stop=(ch == 7)),
                    reads=("wf1", xbk), writes=("p6f1",))
            t1 = RG[:, 14336:14848]
            sc.op("vector", lambda v, t1=t1, ps=ps: v.tensor_copy(t1, ps[:, 0:512]),
                  reads=("p6f1",), writes=("p6t1",))
            for hh in range(8):
                ps2 = PS[4]
                sc.op("tensor", lambda t, hh=hh, t1=t1, ps2=ps2: t.matmul(
                    ps2[:, 0:512], RW1[:, 27648 + hh * 128: 27648 + (hh + 1) * 128], t1,
                    start=True, stop=True),
                    reads=("wf2", "p6t1"), writes=("p6f2",))
                el = RF[:, 0 + (hh & 1) * 1024: 512 + (hh & 1) * 1024]
                ll = RF[:, 512 + (hh & 1) * 1024: 1024 + (hh & 1) * 1024]
                lk = f"p6l{hh & 1}"
                sc.op("scalar", lambda s, el=el, ps2=ps2: s.activation(el, ps2[:, 0:512], AF.Exp, scale=-1.0),
                      reads=("p6f2",), writes=(lk,))
                sc.op("scalar", lambda s, el=el, ll=ll: s.activation(ll, el, AF.Ln, bias=1.0),
                      reads=(lk,), writes=(lk,))
                sc.dma("sync", dm["lsp"][hh * 128:(hh + 1) * 128, t0:t0 + 512], ll,
                       "lsp", rkeys=(lk,))
            ps = PS[5]
            for ch in range(8):
                sc.op("tensor", lambda t, ch=ch, xb=xb, ps=ps: t.matmul(
                    ps[:, 0:512], RW1[:, 25600 + ch * HD: 25600 + (ch + 1) * HD],
                    xb[:, ch * 512:(ch + 1) * 512], start=(ch == 0), stop=(ch == 7)),
                    reads=("wog1", xbk), writes=("p6og",))
            og = RG[:, 14848:15360]
            sc.op("vector", lambda v, og=og, ps=ps: v.tensor_copy(og, ps[:, 0:512]),
                  reads=("p6og",), writes=("p6ogt",))
            sc.dma("sync", dm["og1fm"][:, t0:t0 + 512], og, "og1fm", rkeys=("p6ogt",))
            self.sc.fence()

    def p7_scan(self):
        sc, dm = self.sc, self.dram
        if "scan0" in self.taps:
            self.dout("tap_scan0", (128, 18 * 256), BF16)
            self.dout("tap_scan0f", (128, 4 * 80), F32)
        if "scan0x" in self.taps:
            self.dout("tap_scan0x", (128, 8 * 256), BF16)
        if "scan0p" in self.taps:
            self.dout("tap_scan0p", (128, 512), BF16)
        RW1, RW2, RF, RS, RM, PS, PB = self.RW1, self.RW2, self.RF, self.RS, self.RM, self.PS, self.PB
        for hh in range(H):
            sc.op("vector", lambda v, hh=hh: v.memset(RW2[:, hh * 128:(hh + 1) * 128], 0.0),
                  writes=(f"S0h{hh}",))
        self.sc.fence()
        ML = RM[:, 384:512]; MLT = RM[:, 512:640]
        MKBT = RM[0:64, 640:768]; MAQ = RM[:, 768:832]; MKQ = RM[0:64, 832:896]
        IDE = RM[:, 0:128]
        for c in range(NCH):
            t0 = c * CT
            for hh in range(H):
                par = hh & 3
                P = par * 2560
                W = lambda o, w: RW1[:, P + o: P + o + w]
                knf, knbf, qf = W(0, 64), W(64, 64), W(128, 64)
                A, Bf, Kf, Qt = W(192, 128), W(320, 128), W(448, 64), W(512, 64)
                Lsb, LTsb, KBTsb = W(576, 128), W(704, 128), W(832, 128)
                AQsb, KQsb = W(960, 64), W(1024, 64)
                X = W(1088, 256)
                Pp, PTp = W(1344, 128), W(1472, 128)
                At, Kt = W(1600, 128), W(1728, 128)
                QeffT = W(1856, 64)
                dphit = W(1920, 128)
                vtok = W(2048, 128)
                Fb = par * 640
                lt = RF[:, Fb: Fb + 64]
                cs = RF[:, Fb + 64: Fb + 129]
                gm = RF[:, Fb + 192: Fb + 257]
                igm = RF[:, Fb + 320: Fb + 385]
                tmpf = RF[:, Fb + 448: Fb + 576]
                k_ = lambda n: f"s{n}{par}"
                S0 = RW2[:, hh * 128:(hh + 1) * 128]
                S0k = f"S0h{hh}"
                sc.dma("sync", knf, dm["kn"][t0:t0 + CT, hh * 128:(hh + 1) * 128],
                       k_("kn"), rkeys=(), transpose=True)
                sc.dma("sync", knbf, dm["knb"][t0:t0 + CT, hh * 128:(hh + 1) * 128],
                       k_("knb"), rkeys=(), transpose=True)
                sc.dma("sync", qf, dm["q1s"][t0:t0 + CT, hh * 128:(hh + 1) * 128],
                       k_("qf"), rkeys=(), transpose=True)
                sc.dma("sync", vtok[0:64, :], dm["v1s"][t0:t0 + CT, hh * 128:(hh + 1) * 128],
                       k_("vt"))
                sc.dma("sync", lt, dm["lsp"][hh * 128:(hh + 1) * 128, t0:t0 + CT], k_("l"))
                sc.op("vector", lambda v, cs=cs: v.memset(cs[:, 0:1], 0.0),
                      writes=(k_("cs"),))
                sc.op("vector", lambda v, cs=cs, lt=lt: v.tensor_tensor_scan(
                    cs[:, 1:65], lt, RS[:, 512:576], 0.0, ALU.add, ALU.add),
                    reads=(k_("l"), "RS_zero64"), writes=(k_("cs"),))
                sc.op("scalar", lambda s, gm=gm, cs=cs: s.activation(gm, cs, AF.Exp, scale=-1.0),
                      reads=(k_("cs"),), writes=(k_("gm"),))
                sc.op("scalar", lambda s, igm=igm, cs=cs: s.activation(igm, cs, AF.Exp, scale=1.0),
                      reads=(k_("cs"),), writes=(k_("igm"),))
                sc.op("vector", lambda v, A=A, knbf=knbf, igm=igm: v.tensor_mul(
                    A[:, 0:128:2], knbf, igm[:, 0:64]),
                    reads=(k_("knb"), k_("igm")), writes=(k_("A"),))
                sc.op("vector", lambda v, A=A, knbf=knbf, igm=igm: v.tensor_mul(
                    A[:, 1:128:2], knbf, igm[:, 1:65]),
                    reads=(k_("knb"), k_("igm")), writes=(k_("A"),))
                sc.op("vector", lambda v, Bf=Bf, knf=knf, gm=gm: v.tensor_mul(
                    Bf[:, 0:128:2], knf, gm[:, 0:64]),
                    reads=(k_("kn"), k_("gm")), writes=(k_("B"),))
                sc.op("vector", lambda v, Bf=Bf, knf=knf, gm=gm: v.tensor_mul(
                    Bf[:, 1:128:2], knf, gm[:, 1:65]),
                    reads=(k_("kn"), k_("gm")), writes=(k_("B"),))
                sc.op("vector", lambda v, Kf=Kf, knf=knf, igm=igm: v.tensor_mul(
                    Kf, knf, igm[:, 1:65]),
                    reads=(k_("kn"), k_("igm")), writes=(k_("K"),))
                sc.op("vector", lambda v, Qt=Qt, qf=qf, gm=gm: v.tensor_mul(
                    Qt, qf, gm[:, 1:65]),
                    reads=(k_("qf"), k_("gm")), writes=(k_("Q"),))
                sc.op("tensor", lambda t, A=A: t.transpose(PB[par & 1][:, 0:128], A, IDE),
                      reads=(k_("A"), "RM_id"), writes=(f"pb{par & 1}",))
                sc.op("vector", lambda v, At=At: v.tensor_copy(At, PB[par & 1][:, 0:128]),
                      reads=(f"pb{par & 1}",), writes=(k_("At"),))
                sc.op("tensor", lambda t, Bf=Bf: t.transpose(PB[par & 1][:, 256:384], Bf, IDE),
                      reads=(k_("B"), "RM_id"), writes=(f"pb{par & 1}",))
                sc.op("vector", lambda v, X=X: v.tensor_copy(X[:, 0:128], PB[par & 1][:, 256:384]),
                      reads=(f"pb{par & 1}",), writes=(k_("X"),))
                sc.op("tensor", lambda t, Kf=Kf: t.transpose(PB[par & 1][0:64, 128:256], Kf, IDE),
                      reads=(k_("K"), "RM_id"), writes=(f"pb{par & 1}",))
                sc.op("vector", lambda v, Kt=Kt: v.tensor_copy(Kt[0:64, :], PB[par & 1][0:64, 128:256]),
                      reads=(f"pb{par & 1}",), writes=(k_("Kt"),))
                def gram(lhs, rhs, lk, rk, outsb, mask, ok, rows=128):
                    ps = PS[par]
                    sc.op("tensor", lambda t, ps=ps, lhs=lhs, rhs=rhs, rows=rows: t.matmul(
                        ps[0:rows, 0:128], lhs, rhs, start=True, stop=True),
                        reads=(lk, rk), writes=(f"ps{par}",))
                    sc.op("vector", lambda v, outsb=outsb, ps=ps, mask=mask, rows=rows:
                          v.tensor_mul(outsb[0:rows, 0:128] if rows != 128 else outsb,
                                       ps[0:rows, 0:128] if rows != 128 else ps[:, 0:128],
                                       mask),
                          reads=(f"ps{par}",), writes=(ok,))
                gram(Bf, A, k_("B"), k_("A"), Lsb, ML, k_("L"))
                gram(A, Bf, k_("A"), k_("B"), LTsb, MLT, k_("LT"))
                gram(Kf, Bf, k_("K"), k_("B"), KBTsb, MKBT, k_("KBT"), rows=64)
                ps = PS[par]
                sc.op("tensor", lambda t, ps=ps, A=A, Qt=Qt: t.matmul(
                    ps[:, 0:64], A, Qt, start=True, stop=True),
                    reads=(k_("A"), k_("Q")), writes=(f"ps{par}",))
                sc.op("vector", lambda v, ps=ps, AQsb=AQsb: v.tensor_mul(
                    AQsb, ps[:, 0:64], MAQ),
                    reads=(f"ps{par}",), writes=(k_("AQ"),))
                sc.op("tensor", lambda t, ps=ps, Kf=Kf, Qt=Qt: t.matmul(
                    ps[0:64, 64:128], Kf, Qt, start=True, stop=True),
                    reads=(k_("K"), k_("Q")), writes=(f"ps{par}",))
                sc.op("vector", lambda v, ps=ps, KQsb=KQsb: v.tensor_mul(
                    KQsb[0:64, :], ps[0:64, 64:128], MKQ),
                    reads=(f"ps{par}",), writes=(k_("KQ"),))
                ps = PS[par]
                sc.op("tensor", lambda t, ps=ps, KBTsb=KBTsb, vtok=vtok: t.matmul(
                    ps[:, 256:384], KBTsb[0:64, :], vtok[0:64, :], start=True, stop=True),
                    reads=(k_("KBT"), k_("vt")), writes=(f"ps{par}",))
                sc.op("vector", lambda v, X=X, ps=ps: v.tensor_copy(X[:, 128:256], ps[:, 256:384]),
                      reads=(f"ps{par}",), writes=(k_("X"),))
                dotap = (hh == 0 and c == 0 and ("scan0x" in self.taps or "scan0p" in self.taps))
                if dotap and "scan0x" in self.taps:
                    sc.dma("sync", self.dram["tap_scan0x"][:, 0:256], X, "tap_scan0x",
                           rkeys=(k_("X"),))
                pcur_k, ptcur_k = k_("L"), k_("LT")
                pcur, ptcur = Lsb, LTsb
                kpow = 1
                lev = 0
                while kpow <= 64:
                    ps = PS[par]
                    sc.op("tensor", lambda t, ps=ps, ptcur=ptcur, X=X: t.matmul(
                        ps[:, 0:256], ptcur, X, start=True, stop=True),
                        reads=(ptcur_k, k_("X")), writes=(f"ps{par}",))
                    sc.op("vector", lambda v, X=X, ps=ps: v.tensor_add(X, X, ps[:, 0:256]),
                          reads=(f"ps{par}",), writes=(k_("X"),))
                    lev += 1
                    if dotap and "scan0x" in self.taps:
                        sc.dma("sync", self.dram["tap_scan0x"][:, lev * 256:(lev + 1) * 256],
                               X, "tap_scan0x", rkeys=(k_("X"),))
                    if kpow < 64:
                        pm, pmt = PS[4 + (par & 1)], PS[4 + (par & 1)]
                        sc.op("tensor", lambda t, pm=pm, ptcur=ptcur, pcur=pcur: t.matmul(
                            pm[:, 0:128], ptcur, pcur, start=True, stop=True),
                            reads=(ptcur_k, pcur_k), writes=(f"ps{4 + (par & 1)}",))
                        sc.op("tensor", lambda t, pmt=pmt, ptcur=ptcur, pcur=pcur: t.matmul(
                            pmt[:, 128:256], pcur, ptcur, start=True, stop=True),
                            reads=(ptcur_k, pcur_k), writes=(f"ps{4 + (par & 1)}",))
                        sc.op("vector", lambda v, pm=pm, Pp=Pp: v.tensor_copy(Pp, pm[:, 0:128]),
                              reads=(f"ps{4 + (par & 1)}",), writes=(k_("Pp"),))
                        sc.op("vector", lambda v, pmt=pmt, PTp=PTp: v.tensor_copy(PTp, pmt[:, 128:256]),
                              reads=(f"ps{4 + (par & 1)}",), writes=(k_("PTp"),))
                        pcur, ptcur = Pp, PTp
                        pcur_k, ptcur_k = k_("Pp"), k_("PTp")
                        if dotap and kpow == 1:
                            sc.dma("sync", self.dram["tap_scan0p"][:, 0:128], Pp,
                                   "tap_scan0p", rkeys=(k_("Pp"),))
                            sc.dma("sync", self.dram["tap_scan0p"][:, 128:256], PTp,
                                   "tap_scan0p", rkeys=(k_("PTp"),))
                            sc.dma("sync", self.dram["tap_scan0p"][:, 256:384], Lsb,
                                   "tap_scan0p", rkeys=(k_("L"),))
                            sc.dma("sync", self.dram["tap_scan0p"][:, 384:512], LTsb,
                                   "tap_scan0p", rkeys=(k_("LT"),))
                    kpow *= 2
                ps = PS[par]
                sc.op("tensor", lambda t, ps=ps, X=X, AQsb=AQsb: t.matmul(
                    ps[:, 384:448], X[:, 0:128], AQsb, start=True, stop=True),
                    reads=(k_("X"), k_("AQ")), writes=(f"ps{par}",))
                sc.op("vector", lambda v, QeffT=QeffT, ps=ps, Qt=Qt: v.tensor_add(
                    QeffT, ps[:, 384:448], Qt),
                    reads=(f"ps{par}", k_("Q")), writes=(k_("QeffT"),))
                po = PS[par]
                sc.op("tensor", lambda t, po=po, S0=S0, QeffT=QeffT: t.matmul(
                    po[:, 448:512], S0, QeffT, start=True, stop=False),
                    reads=(S0k, k_("QeffT")), writes=(f"ps{par}",))
                sc.op("tensor", lambda t, po=po, X=X, AQsb=AQsb: t.matmul(
                    po[:, 448:512], X[:, 128:256], AQsb, start=False, stop=False),
                    reads=(k_("X"), k_("AQ")), writes=(f"ps{par}",))
                sc.op("tensor", lambda t, po=po, vtok=vtok, KQsb=KQsb: t.matmul(
                    po[:, 448:512], vtok[0:64, :], KQsb[0:64, :], start=False, stop=True),
                    reads=(k_("vt"), k_("KQ")), writes=(f"ps{par}",))
                ob = RW1[:, P + 2176: P + 2240]
                sc.op("vector", lambda v, ob=ob, po=po: v.tensor_copy(ob, po[:, 448:512]),
                      reads=(f"ps{par}",), writes=(k_("ob"),))
                sc.dma("sync", dm["o1fm"][hh * 128:(hh + 1) * 128, t0:t0 + CT], ob,
                       "o1fm", rkeys=(k_("ob"),))
                ps3 = PS[par]
                sc.op("tensor", lambda t, ps3=ps3, X=X, At=At: t.matmul(
                    ps3[:, 256:384], X[:, 0:128], At, start=True, stop=True),
                    reads=(k_("X"), k_("At")), writes=(f"ps{par}",))
                sc.op("vector", lambda v, dphit=dphit, ps3=ps3: v.tensor_copy(dphit, ps3[:, 256:384]),
                      reads=(f"ps{par}",), writes=(k_("dphit"),))
                sc.op("tensor", lambda t, ps3=ps3, dphit=dphit, S0=S0: t.matmul(
                    ps3[:, 256:384], dphit, S0, start=True, stop=False),
                    reads=(k_("dphit"), S0k), writes=(f"ps{par}",))
                sc.op("tensor", lambda t, ps3=ps3, At=At, X=X: t.matmul(
                    ps3[:, 256:384], At, X[:, 128:256], start=False, stop=False),
                    reads=(k_("At"), k_("X")), writes=(f"ps{par}",))
                sc.op("tensor", lambda t, ps3=ps3, Kt=Kt, vtok=vtok: t.matmul(
                    ps3[:, 256:384], Kt[0:64, :], vtok[0:64, :], start=False, stop=True),
                    reads=(k_("Kt"), k_("vt")), writes=(f"ps{par}",))
                sc.op("vector", lambda v, tmpf=tmpf, ps3=ps3, S0=S0: v.tensor_add(
                    tmpf, ps3[:, 256:384], S0),
                    reads=(f"ps{par}", S0k), writes=(k_("tf"),))
                sc.op("vector", lambda v, S0=S0, tmpf=tmpf, gm=gm: v.tensor_scalar_mul(
                    S0, tmpf, gm[:, 64:65]),
                    reads=(k_("tf"), k_("gm")), writes=(S0k,))
                if hh == 0 and c == 0 and "scan0" in self.taps:
                    tb = self.dram["tap_scan0"]
                    tf32 = self.dram["tap_scan0f"]
                    for off, (buf, bk, w_) in enumerate([
                            (knf, k_("kn"), 64), (knbf, k_("knb"), 64), (qf, k_("qf"), 64),
                            (A, k_("A"), 128), (Bf, k_("B"), 128), (Kf, k_("K"), 64),
                            (Qt, k_("Q"), 64), (Lsb, k_("L"), 128), (LTsb, k_("LT"), 128),
                            (KBTsb, k_("KBT"), 128), (AQsb, k_("AQ"), 64), (KQsb, k_("KQ"), 64),
                            (X, k_("X"), 256), (QeffT, k_("QeffT"), 64), (At, k_("At"), 128),
                            (Kt, k_("Kt"), 128), (vtok, k_("vt"), 128), (ob, k_("ob"), 64)]):
                        sc.dma("sync", tb[:, off * 256: off * 256 + w_], buf[:, 0:w_],
                               "tap_scan0", rkeys=(bk,))
                    for off, (buf, bk, w_) in enumerate([
                            (lt, k_("l"), 64), (cs, k_("cs"), 65), (gm, k_("gm"), 65),
                            (igm, k_("igm"), 65)]):
                        sc.dma("sync", tf32[:, off * 80: off * 80 + w_], buf[:, 0:w_],
                               "tap_scan0f", rkeys=(bk,))
                self.sc.fence()


    def p8_gate_gn(self):
        sc, dm = self.sc, self.dram
        RG, RW1, RS, RM, PS = self.RG, self.RW1, self.RS, self.RM, self.PS
        sc.dma("sync", RW1[:, 0:1024], dm["c_wog2"][:, :], "wog2", rkeys=("c_wog2",))
        for hh in range(H):
            for tt in range(8):
                t0 = tt * 512
                par = tt & 1
                Pb = par * 4096
                ot = RG[:, Pb: Pb + 512]
                og = RG[:, Pb + 512: Pb + 1024]
                g_ = RG[:, Pb + 1024: Pb + 1536]
                y_ = RG[:, Pb + 1536: Pb + 2048]
                yn = RG[:, Pb + 2048: Pb + 2560]
                y2 = RG[:, Pb + 2560: Pb + 3072]
                k_ = lambda n: f"g{n}{par}"
                sc.dma("sync", ot, dm["o1fm"][hh * 128:(hh + 1) * 128, t0:t0 + 512],
                       k_("o"), rkeys=())
                sc.dma("sync", og, dm["og1fm"][:, t0:t0 + 512], k_("og"), rkeys=())
                ps = PS[par]
                pk = f"ps{par}"
                sc.op("tensor", lambda t, ps=ps, og=og, hh=hh: t.matmul(
                    ps[:, 0:512], RW1[:, hh * 128:(hh + 1) * 128], og,
                    start=True, stop=True),
                    reads=("wog2", k_("og")), writes=(pk,))
                sc.op("scalar", lambda s, g_=g_, ps=ps: s.activation(g_, ps[:, 0:512], AF.Sigmoid),
                      reads=(pk,), writes=(k_("g"),))
                sc.op("vector", lambda v, y_=y_, ot=ot, g_=g_: v.tensor_mul(y_, ot, g_),
                      reads=(k_("o"), k_("g")), writes=(k_("y"),))
                sc.op("vector", lambda v, y2=y2, y_=y_: v.tensor_mul(y2, y_, y_),
                      reads=(k_("y"),), writes=(k_("y2"),))
                ps2 = PS[2 + par]
                pk2 = f"ps{2 + par}"
                sc.op("tensor", lambda t, ps2=ps2, y2=y2: t.matmul(
                    ps2[0:1, 0:512], RM[:, 1026:1027], y2, start=True, stop=True),
                    reads=("RM_onescol", k_("y2")), writes=(pk2,))
                rs_ = RS[:, 128 + par * 16: 128 + par * 16 + 8]
                sc.op("scalar", lambda s, rs_=rs_, ps2=ps2: s.activation(
                    rs_[0:1, 0:4].rearrange("a (b c) -> a (b c)") if False else rs_[0:1, 0:1],
                    ps2[0:1, 0:1], AF.Sqrt, bias=self.RS[0:1, 0:1], scale=1.0 / HD) if False else None,
                    reads=(), writes=()) if False else None
                rsr = RG[:, Pb + 3072: Pb + 3584]
                sc.op("scalar", lambda s, rsr=rsr, ps2=ps2: s.activation(
                    rsr[0:1, 0:512], ps2[0:1, 0:512], AF.Sqrt, bias=self.RS[0:1, 0:1],
                    scale=1.0 / HD),
                    reads=(pk2, "RS_eps6"), writes=(k_("rs"),))
                sc.op("vector", lambda v, rsr=rsr: v.reciprocal(rsr[0:1, 0:512], rsr[0:1, 0:512]),
                      reads=(k_("rs"),), writes=(k_("rs"),))
                sc.op("tensor", lambda t, ps2=ps2, rsr=rsr: t.matmul(
                    ps2[:, 0:512], RM[0:1, 896:1024], rsr[0:1, 0:512],
                    start=True, stop=True),
                    reads=("RM_ones", k_("rs")), writes=(pk2,))
                sc.op("vector", lambda v, yn=yn, y_=y_, ps2=ps2: v.tensor_mul(
                    yn, y_, ps2[:, 0:512]),
                    reads=(k_("y"), pk2), writes=(k_("yn"),))
                sc.op("vector", lambda v, yn=yn, hh=hh: v.tensor_scalar_mul(
                    yn, yn, RS[:, 8 + hh:9 + hh]),
                    reads=(k_("yn"), "RS_nw1"), writes=(k_("yn"),))
                sc.dma("sync", dm["yn2fm"][hh * 128:(hh + 1) * 128, t0:t0 + 512], yn,
                       "yn2fm", rkeys=(k_("yn"),))
            self.sc.fence()

    def p9_woproj(self):
        sc, dm = self.sc, self.dram
        RG, RW1, PS = self.RG, self.RW1, self.PS
        self.load_w_lhsT("c_L1_attn_Wo", RW1, 0, D, "wo1")
        for tt in range(8):
            t0 = tt * 512
            par = tt & 1
            Pb = par * 4096
            yt = RG[:, Pb: Pb + 4096]
            yk = f"p9y{par}"
            for hh in range(8):
                sc.dma("sync", yt[:, hh * 512:(hh + 1) * 512],
                       dm["yn2fm"][hh * 128:(hh + 1) * 128, t0:t0 + 512], yk, rkeys=())
            for st4 in range(4):
                ts0 = t0 + st4 * 128
                for dh in range(2):
                    ps = PS[dh]
                    pk = f"ps{dh}"
                    for hh in range(8):
                        sc.op("tensor", lambda t, ps=ps, hh=hh, dh=dh, st4=st4, yt=yt:
                              t.matmul(ps[:, 0:512],
                                       yt[:, hh * 512 + st4 * 128: hh * 512 + st4 * 128 + 128],
                                       RW1[:, hh * D + dh * 512: hh * D + dh * 512 + 512],
                                       start=(hh == 0), stop=(hh == 7)),
                              reads=("wo1", yk), writes=(pk,))
                ev = RG[:, 8192 + (st4 & 1) * 1024: 8192 + (st4 & 1) * 1024 + 1024]
                evk = f"p9e{st4 & 1}"
                for dh in range(2):
                    sc.op("vector", lambda v, dh=dh, ev=ev: v.tensor_copy(
                        ev[:, dh * 512:(dh + 1) * 512], PS[dh][:, 0:512]),
                        reads=(f"ps{dh}",), writes=(evk,))
                sc.dma("sync", dm["l1a"][ts0:ts0 + 128, :], ev, "l1a", rkeys=(evk,))
            self.sc.fence()

    def p10_l1ffn(self):
        sc, dm = self.sc, self.dram
        RG, RW1, RW2, RF, RS, RM, PS = self.RG, self.RW1, self.RW2, self.RF, self.RS, self.RM, self.PS
        sc.dma("sync", RS[:, 96:104], bass.AP(dm["L1_ffn_norm_w"], 0, [[1, 128], [128, 8]]),
               "RS_nwf", allow_slow_non_contiguous=True)
        sc.op("gpsimd", lambda g: g.iota(RS[:, 80:88], [[1, 8]], base=0,
                                         channel_multiplier=0,
                                         allow_small_or_imprecise_dtypes=True),
              writes=("RS_iota",))
        self.sc.fence()
        for it in range(S // 128):
            t0 = it * 128
            par = it & 1
            xt = RF[:, par * 1024: par * 1024 + 1024]
            xk = f"ax{par}"
            sq = RF[:, 2048 + par * 1024: 3072 + par * 1024]
            cols = RS[:, 128 + par * 8: 136 + par * 8]
            ck = f"ac{par}"
            sc.dma("gpsimd", xt, dm["xn1"][t0:t0 + 128, :], xk, rkeys=())
            sc.op("vector", lambda v, sq=sq, xt=xt: v.tensor_mul(sq, xt, xt),
                  reads=(xk,), writes=(f"asq{par}",))
            sc.op("vector", lambda v, cols=cols, sq=sq: v.reduce_sum(
                cols[:, 0:1], sq, axis=mybir.AxisListType.X),
                reads=(f"asq{par}",), writes=(ck,))
            sc.op("scalar", lambda s, cols=cols: s.activation(
                cols[:, 1:2], cols[:, 0:1], AF.Sqrt, bias=self.RS[:, 0:1], scale=1.0 / D),
                reads=(ck, "RS_eps6"), writes=(ck,))
            sc.op("vector", lambda v, cols=cols: v.reciprocal(cols[:, 2:3], cols[:, 1:2]),
                  reads=(ck,), writes=(ck,))
            ot = RG[:, par * 1024: par * 1024 + 1024]
            ok = f"ao{par}"
            sc.op("vector", lambda v, ot=ot, xt=xt, cols=cols: v.tensor_scalar_mul(
                ot, xt, cols[:, 2:3]),
                reads=(xk, ck), writes=(ok,))
            sc.dma("sync", dm["trms"][t0:t0 + 128, :], ot, "trms", rkeys=(ok,))
        self.sc.fence()
        gt = RW1[:, 0:64]
        sc.dma("sync", gt[:, 0:8], dm["c_gate_wT"][0:128, :], "gw", rkeys=("c_gate_wT",))
        for ch in range(1, 8):
            sc.dma("sync", gt[:, ch * 8:(ch + 1) * 8], dm["c_gate_wT"][ch * 128:(ch + 1) * 128, :],
                   "gw", rkeys=("c_gate_wT",))
        for ch in range(8):
            sc.op("vector", lambda v, ch=ch, gt=gt: v.tensor_scalar_mul(
                gt[:, ch * 8:(ch + 1) * 8], gt[:, ch * 8:(ch + 1) * 8], RS[:, 96 + ch:97 + ch]),
                reads=("gw", "RS_nwf"), writes=("gw",))
        self.sc.fence()
        for tt in range(8):
            t0 = tt * 512
            par = tt & 1
            tf = RG[:, par * 4096: par * 4096 + 4096]
            tfk = f"btf{par}"
            for ch in range(8):
                sc.dma("sync", tf[:, ch * 512:(ch + 1) * 512],
                       dm["trms"][t0:t0 + 512, ch * 128:(ch + 1) * 128],
                       tfk, rkeys=("trms",), transpose=True)
            for st4 in range(4):
                ts0 = t0 + st4 * 128
                ps = PS[par]
                pk = f"ps{par}"
                for ch in range(8):
                    sc.op("tensor", lambda t, ps=ps, ch=ch, st4=st4, tf=tf: t.matmul(
                        ps[:, 0:8], tf[:, ch * 512 + st4 * 128: ch * 512 + st4 * 128 + 128],
                        RW1[:, ch * 8:(ch + 1) * 8], start=(ch == 0), stop=(ch == 7)),
                        reads=("gw", tfk), writes=(pk,))
                w_ = RF[:, 4096 + par * 64: 4096 + par * 64 + 64]
                wk = f"bw{par}"
                sc.op("vector", lambda v, w_=w_, ps=ps: v.reduce_max(
                    w_[:, 8:9], ps[:, 0:8], axis=mybir.AxisListType.X, negate=True),
                    reads=(pk,), writes=(wk,))
                sc.op("scalar", lambda s, w_=w_, ps=ps: s.activation(
                    w_[:, 0:8], ps[:, 0:8], AF.Exp, bias=w_[:, 8:9]),
                    reads=(pk, wk), writes=(wk,))
                sc.op("vector", lambda v, w_=w_: v.reduce_sum(
                    w_[:, 9:10], w_[:, 0:8], axis=mybir.AxisListType.X),
                    reads=(wk,), writes=(wk,))
                sc.op("vector", lambda v, w_=w_: v.reciprocal(w_[:, 10:11], w_[:, 9:10]),
                      reads=(wk,), writes=(wk,))
                sc.op("vector", lambda v, w_=w_: v.tensor_scalar_mul(
                    w_[:, 0:8], w_[:, 0:8], w_[:, 10:11]),
                    reads=(wk,), writes=(wk,))
                sc.op("vector", lambda v, w_=w_: v.max(w_[:, 16:24], w_[:, 0:8]),
                      reads=(wk,), writes=(wk,))
                idx = self.RIDX[:, par * 8: par * 8 + 8]
                sc.op("vector", lambda v, w_=w_, idx=idx: v.max_index(
                    idx, w_[:, 16:24], w_[:, 0:8]),
                    reads=(wk,), writes=(f"bidx{par}",))
                sc.op("vector", lambda v, w_=w_: v.tensor_add(w_[:, 11:12], w_[:, 16:17], w_[:, 17:18]),
                      reads=(wk,), writes=(wk,))
                sc.op("vector", lambda v, w_=w_: v.reciprocal(w_[:, 12:13], w_[:, 11:12]),
                      reads=(wk,), writes=(wk,))
                sc.op("vector", lambda v, w_=w_: v.tensor_mul(w_[:, 13:14], w_[:, 16:17], w_[:, 12:13]),
                      reads=(wk,), writes=(wk,))
                sc.op("vector", lambda v, w_=w_: v.tensor_mul(w_[:, 14:15], w_[:, 17:18], w_[:, 12:13]),
                      reads=(wk,), writes=(wk,))
                sc.op("vector", lambda v, idx=idx, w_=w_: v.tensor_copy(w_[:, 24:26], idx[:, 0:2]),
                      reads=(f"bidx{par}",), writes=(wk,))
                oh = RF[:, 4352 + par * 32: 4352 + par * 32 + 16]
                sc.op("vector", lambda v, oh=oh, w_=w_: v.tensor_single_scalar(
                    oh[:, 0:8], RS[:, 80:88], w_[:, 24:25], ALU.is_equal),
                    reads=("RS_iota", wk), writes=(f"boh{par}",))
                sc.op("vector", lambda v, oh=oh, w_=w_: v.tensor_single_scalar(
                    oh[:, 8:16], RS[:, 80:88], w_[:, 25:26], ALU.is_equal),
                    reads=("RS_iota", wk), writes=(f"boh{par}",))
                sc.op("vector", lambda v, oh=oh, w_=w_: v.tensor_scalar_mul(
                    oh[:, 0:8], oh[:, 0:8], w_[:, 13:14]),
                    reads=(f"boh{par}", wk), writes=(f"boh{par}",))
                sc.op("vector", lambda v, oh=oh, w_=w_: v.scalar_tensor_tensor(
                    oh[:, 0:8], oh[:, 8:16], w_[:, 14:15], oh[:, 0:8], ALU.mult, ALU.add),
                    reads=(f"boh{par}", wk), writes=(f"boh{par}",))
                ohb = RG[:, 8192 + par * 128: 8192 + par * 128 + 8]
                sc.op("vector", lambda v, ohb=ohb, oh=oh: v.tensor_copy(ohb, oh[:, 0:8]),
                      reads=(f"boh{par}",), writes=(f"bohb{par}",))
                sc.op("tensor", lambda t, ohb=ohb: t.transpose(
                    self.PB[0][0:8, 0:128], ohb, RM[:, 0:128]),
                    reads=(f"bohb{par}", "RM_id"), writes=("pb0",))
                wr = RG[:, 8448 + par * 128: 8448 + par * 128 + 128]
                sc.op("vector", lambda v, wr=wr: v.tensor_copy(wr[0:8, :], self.PB[0][0:8, 0:128]),
                      reads=("pb0",), writes=(f"bwr{par}",))
                sc.dma("gpsimd", dm["wrow"][:, ts0:ts0 + 128], wr[0:8, :], "wrow",
                       rkeys=(f"bwr{par}",))
            self.sc.fence()
        for ipass, (wname, dst) in enumerate((("c_w1T", "base1"), ("c_w3T", "base3"))):
            for ch in range(8):
                sc.dma("sync", RW1[:, ch * 4096:(ch + 1) * 4096],
                       dm[wname][ch * 128:(ch + 1) * 128, :], "wbase", rkeys=(wname,))
            for ch in range(8):
                sc.op("vector", lambda v, ch=ch: v.tensor_scalar_mul(
                    RW1[:, ch * 4096:(ch + 1) * 4096], RW1[:, ch * 4096:(ch + 1) * 4096],
                    RS[:, 96 + ch:97 + ch]),
                    reads=("wbase", "RS_nwf"), writes=("wbase",))
            self.sc.fence()
            for tt in range(8):
                t0 = tt * 512
                par = tt & 1
                tf = RG[:, par * 4096: par * 4096 + 4096]
                tfk = f"ctf{par}"
                for ch in range(8):
                    sc.dma("sync", tf[:, ch * 512:(ch + 1) * 512],
                           dm["trms"][t0:t0 + 512, ch * 128:(ch + 1) * 128],
                           tfk, rkeys=("trms",), transpose=True)
                for ft in range(32):
                    ps = PS[ft & 1]
                    pk = f"ps{ft & 1}"
                    for ch in range(8):
                        sc.op("tensor", lambda t, ps=ps, ch=ch, ft=ft, tf=tf: t.matmul(
                            ps[:, 0:512],
                            RW1[:, ch * 4096 + ft * 128: ch * 4096 + ft * 128 + 128],
                            tf[:, ch * 512:(ch + 1) * 512], start=(ch == 0), stop=(ch == 7)),
                            reads=("wbase", tfk), writes=(pk,))
                    ev = RG[:, 8192 + (ft & 1) * 512: 8192 + (ft & 1) * 512 + 512]
                    evk = f"cev{ft & 1}"
                    sc.op("vector", lambda v, ev=ev, ps=ps: v.tensor_copy(ev, ps[:, 0:512]),
                          reads=(pk,), writes=(evk,))
                    sc.dma("sync", dm[dst][ft * 128:(ft + 1) * 128, t0:t0 + 512], ev,
                           dst, rkeys=(evk,))
                self.sc.fence()
        for tt in range(8):
            t0 = tt * 512
            tf = RW2[:, 16384:20480]
            for ch in range(8):
                sc.dma("sync", tf[:, ch * 512:(ch + 1) * 512],
                       dm["trms"][t0:t0 + 512, ch * 128:(ch + 1) * 128],
                       "dtf", rkeys=("trms",), transpose=True)
            sc.op("vector", lambda v: v.memset(RW2[:, 0:8192], 0.0),
                  writes=("dmix0", "dmix1", "dmix2", "dmix3"))
            sc.op("gpsimd", lambda g: g.memset(RW2[:, 8192:16384], 0.0),
                  writes=("dmix0", "dmix1", "dmix2", "dmix3"))
            for bname, off in (("base1", 0), ("base3", 16384)):
                for ft in range(32):
                    sc.dma("sync", RW1[:, off + ft * 512: off + (ft + 1) * 512],
                           dm[bname][ft * 128:(ft + 1) * 128, t0:t0 + 512],
                           "dbase", rkeys=(bname,))
            wr8 = self.RW2[:, 20480:24576]
            for e_ in range(H):
                sc.dma("gpsimd", wr8[0:1, e_ * 512:(e_ + 1) * 512],
                       dm["wrow"][e_, t0:t0 + 512], "dwr", rkeys=("wrow",))
            self.sc.fence()
            for e in range(H):
                for uu, aname in ((0, "c_a1T"), (1, "c_a3T")):
                    at = RG[:, uu * 128: uu * 128 + 128]
                    ak = f"da{uu}"
                    sc.dma("sync", at,
                           bass.AP(dm["c_" + aname[2:]] if False else dm[aname],
                                   e * 16, [[128, 128], [128 * 128, 8], [1, 16]]),
                           ak, rkeys=(aname,))
                    for ch in range(8):
                        sc.op("vector", lambda v, at=at, ch=ch: v.tensor_scalar_mul(
                            at[:, ch * 16:(ch + 1) * 16], at[:, ch * 16:(ch + 1) * 16],
                            RS[:, 96 + ch:97 + ch]),
                            reads=(ak, "RS_nwf"), writes=(ak,))
                    ps = PS[4 + uu]
                    pk = f"ps{4 + uu}"
                    for ch in range(8):
                        sc.op("tensor", lambda t, ps=ps, at=at, ch=ch, tf=tf: t.matmul(
                            ps[0:16, 0:512], at[:, ch * 16:(ch + 1) * 16],
                            tf[:, ch * 512:(ch + 1) * 512], start=(ch == 0), stop=(ch == 7)),
                            reads=(ak, "dtf"), writes=(pk,))
                    us = RG[:, 256 + uu * 512: 256 + uu * 512 + 512]
                    sc.op("scalar", lambda s, us=us, ps=ps: s.activation(
                        us[0:16, :], ps[0:16, 0:512], AF.Copy, scale=2.0),
                        reads=(pk,), writes=(f"dus{uu}",))
                for uu, bname in ((0, "c_b1T"), (1, "c_b3T")):
                    bt = RG[:, 1280 + uu * 4096: 1280 + uu * 4096 + 4096]
                    sc.dma("sync", bt[0:16, :], dm[bname][e, :, :], f"dbt{uu}", rkeys=(bname,))
                sc.op("tensor", lambda t, e=e: t.matmul(
                    PS[3][:, 0:512], RM[0:1, 896:1024],
                    self.RW2[0:1, 20480 + e * 512: 20480 + (e + 1) * 512],
                    start=True, stop=True),
                    reads=("RM_ones", "dwr"), writes=("ps3",))
                wbs = self.RW2[:, 24576:25088]
                sc.op("vector", lambda v, wbs=wbs: v.tensor_copy(wbs, PS[3][:, 0:512]),
                      reads=("ps3",), writes=("dwbs",))
                a2e = RG[:, 12544 - 512 + 512: 13056] if False else RG[:, 12544:13056]
                sc.dma("sync", a2e,
                       bass.AP(dm["c_a2T"], e * FF * 16, [[16, 128], [128 * 16, 32], [1, 16]]),
                       "da2", rkeys=("c_a2T",))
                for ft in range(32):
                    fp = ft & 1
                    psa = PS[fp]
                    pka = f"ps{fp}"
                    sil = RG[:, 9472 + fp * 512: 9472 + fp * 512 + 512]
                    act = RG[:, 10496 + fp * 512: 10496 + fp * 512 + 512]
                    tmp = RG[:, 11520 + fp * 512: 11520 + fp * 512 + 512]
                    sc.op("tensor", lambda t, psa=psa, ft=ft: t.matmul(
                        psa[:, 0:512], self.RG[0:16, 1280 + ft * 128: 1280 + ft * 128 + 128],
                        self.RG[0:16, 256:768], start=True, stop=True),
                        reads=("dbt0", "dus0"), writes=(pka,))
                    sc.op("vector", lambda v, psa=psa, ft=ft: v.tensor_add(
                        psa[:, 0:512], psa[:, 0:512], RW1[:, ft * 512:(ft + 1) * 512]),
                        reads=(pka, "dbase"), writes=(pka,))
                    sc.op("scalar", lambda s, psa=psa, sil=sil: s.activation(
                        sil, psa[:, 0:512], AF.Silu),
                        reads=(pka,), writes=(f"dsil{fp}",))
                    psb = PS[2] if fp == 0 else PS[3]
                    pkb = "ps2" if fp == 0 else "ps3"
                    sc.op("tensor", lambda t, psb=psb, ft=ft: t.matmul(
                        psb[:, 0:512], self.RG[0:16, 5376 + ft * 128: 5376 + ft * 128 + 128],
                        self.RG[0:16, 768:1280], start=True, stop=True),
                        reads=("dbt1", "dus1"), writes=(pkb,))
                    sc.op("vector", lambda v, psb=psb, ft=ft: v.tensor_add(
                        psb[:, 0:512], psb[:, 0:512], RW1[:, 16384 + ft * 512: 16384 + (ft + 1) * 512]),
                        reads=(pkb, "dbase"), writes=(pkb,))
                    sc.op("vector", lambda v, act=act, sil=sil, psb=psb: v.tensor_mul(
                        act, sil, psb[:, 0:512]),
                        reads=(f"dsil{fp}", pkb), writes=(f"dact{fp}",))
                    sc.op("tensor", lambda t, act=act, ft=ft: t.matmul(
                        PS[5][0:16, 0:512], self.RG[:, 12544 + ft * 16: 12544 + ft * 16 + 16],
                        act, start=(ft == 0), stop=(ft == 31)),
                        reads=("da2", f"dact{fp}"), writes=("ps5",))
                    sc.op("vector", lambda v, tmp=tmp, act=act: v.tensor_mul(
                        tmp, act, self.RW2[:, 24576:25088]),
                        reads=(f"dact{fp}", "dwbs"), writes=(f"dtmp{fp}",))
                    eng = "gpsimd" if fp else "vector"
                    sc.op(eng, lambda g, tmp=tmp, ft=ft: g.tensor_add(
                        RW2[:, ft * 512:(ft + 1) * 512], RW2[:, ft * 512:(ft + 1) * 512], tmp),
                        reads=(f"dtmp{fp}", f"dmix{ft & 3}"), writes=(f"dmix{ft & 3}",))
                u2s = RG[:, 12608:13120]
                u2c = RG[:, 13120:13632]
                sc.op("scalar", lambda s, u2s=u2s: s.activation(
                    u2s[0:16, :], PS[5][0:16, 0:512], AF.Copy, scale=2.0),
                    reads=("ps5",), writes=("du2s",))
                sc.op("vector", lambda v, u2c=u2c, u2s=u2s: v.tensor_mul(
                    u2c[0:16, :], u2s[0:16, :], self.RW2[0:16, 24576:25088]),
                    reads=("du2s", "dwbs"), writes=("du2c",))
                sc.dma("sync", dm["u2w"][e, :, t0:t0 + 512], u2c[0:16, :], "u2w",
                       rkeys=("du2c",))
            for ft in range(32):
                sc.dma("sync", dm["mixedfm"][ft * 128:(ft + 1) * 128, t0:t0 + 512],
                       RW2[:, ft * 512:(ft + 1) * 512], "mixedfm", rkeys=(f"dmix{ft & 3}",))
            self.sc.fence()
        for ch in range(32):
            sc.dma("sync", RW1[:, ch * 1024:(ch + 1) * 1024],
                   dm["c_w2T"][ch * 128:(ch + 1) * 128, :], "w2r", rkeys=("c_w2T",))
        for e in range(H):
            sc.dma("sync", RG[0:16, e * 512: e * 512 + 512],
                   dm["c_b2T"][e, :, 0:512], "b2ra", rkeys=("c_b2T",))
            sc.dma("sync", RG[0:16, 4096 + e * 512: 4096 + e * 512 + 512],
                   dm["c_b2T"][e, :, 512:1024], "b2rb", rkeys=("c_b2T",))
        self.sc.fence()
        for tt in range(8):
            t0 = tt * 512
            for ft in range(32):
                sc.dma("sync", RW2[:, ft * 512:(ft + 1) * 512],
                       dm["mixedfm"][ft * 128:(ft + 1) * 128, t0:t0 + 512],
                       "emx", rkeys=("mixedfm",))
            self.sc.fence()
            for st4 in range(4):
                ts0 = t0 + st4 * 128
                sp = st4 & 1
                x1t = RF[:, sp * 1024: sp * 1024 + 1024]
                sc.dma("sync", x1t, dm["x1"][ts0:ts0 + 128, :], f"ex1{sp}", rkeys=())
                l1t = RG[:, 8192 + sp * 1024: 9216 + sp * 1024]
                sc.dma("sync", l1t, dm["l1a"][ts0:ts0 + 128, :], f"el1{sp}", rkeys=())
                u2b = RG[:, 10240 + sp * 1024: 11264 + sp * 1024]
                for e in range(H):
                    sc.dma("gpsimd", u2b[0:16, e * 128:(e + 1) * 128],
                           dm["u2w"][e, :, ts0:ts0 + 128], f"eu2{sp}", rkeys=("u2w",))
                for dh in range(2):
                    ps = PS[2 * sp + dh]
                    pk = f"ps{2 * sp + dh}"
                    for ft in range(32):
                        sc.op("tensor", lambda t, ps=ps, ft=ft, dh=dh, st4=st4: t.matmul(
                            ps[:, 0:512],
                            RW2[:, ft * 512 + st4 * 128: ft * 512 + st4 * 128 + 128],
                            RW1[:, ft * 1024 + dh * 512: ft * 1024 + dh * 512 + 512],
                            start=(ft == 0), stop=False),
                            reads=("w2r", "emx"), writes=(pk,))
                    for e in range(H):
                        sc.op("tensor", lambda t, ps=ps, e=e, dh=dh, u2b=u2b: t.matmul(
                            ps[:, 0:512], u2b[0:16, e * 128:(e + 1) * 128],
                            self.RG[0:16, dh * 4096 + e * 512: dh * 4096 + e * 512 + 512],
                            start=False, stop=(e == H - 1)),
                            reads=(f"eu2{sp}", "b2ra", "b2rb"), writes=(pk,))
                    sc.op("vector", lambda v, ps=ps, dh=dh, x1t=x1t: v.tensor_add(
                        x1t[:, dh * 512:(dh + 1) * 512], ps[:, 0:512],
                        x1t[:, dh * 512:(dh + 1) * 512]),
                        reads=(pk, f"ex1{sp}"), writes=(f"ex1{sp}",))
                    sc.op("vector", lambda v, dh=dh, x1t=x1t, l1t=l1t: v.tensor_add(
                        x1t[:, dh * 512:(dh + 1) * 512], x1t[:, dh * 512:(dh + 1) * 512],
                        l1t[:, dh * 512:(dh + 1) * 512]),
                        reads=(f"el1{sp}",), writes=(f"ex1{sp}",))
                sc.dma("sync", dm["out"][ts0:ts0 + 128, :], x1t, "outw", rkeys=(f"ex1{sp}",))


def build_program(upto=99, taps=()):
    b = B(upto=upto, taps=taps)
    b.declare()
    phases = [
        (0, b.p0_prep),
        (1, b.p2_ln0),
        (2, b.p3_l0proj),
        (3, b.p3b_keymask),
        (4, b.p4_l0attn),
        (5, b.p1_fold),
        (6, b.p5_l0ffn),
    ]
    more = [
        (7, getattr(b, "p6_l1proj", None)),
        (8, getattr(b, "p7_scan", None)),
        (9, getattr(b, "p8_gate_gn", None)),
        (10, getattr(b, "p9_woproj", None)),
        (11, getattr(b, "p10_l1ffn", None)),
    ]
    for i, fn in phases + more:
        if fn is None or i > upto:
            continue
        fn()
        b.sc.fence()
    with b.nc.allow_low_precision(
            "bf16 operands feeding bf16 matmuls; the only bf16 accumulation is "
            "the top-2 expert mixture where 6 of 8 addends are exactly zero"):
        with b.nc.Block() as block:
            b.sc.analyze()
            b.sc.build(b.nc, block)
    return b


def make_in_map(inputs, aux, core):
    T = np.ascontiguousarray
    im = {
        "x": np.ascontiguousarray(np.asarray(inputs["L0_attn_x"])[core]).astype(np.float32),
        "L0_ffn_W_expT": T(np.asarray(inputs["L0_ffn_W_exp"]).T),
        "L1_ffn_w1T": T(np.asarray(inputs["L1_ffn_w1"]).T),
        "L1_ffn_w3T": T(np.asarray(inputs["L1_ffn_w3"]).T),
        "L1_ffn_w2T": T(np.asarray(inputs["L1_ffn_w2"]).T),
        "L1_ffn_gate_wT": T(np.asarray(inputs["L1_ffn_gate_w"]).T),
        "L1_ffn_a1T": T(np.asarray(inputs["L1_ffn_a1"]).transpose(2, 0, 1).reshape(D, -1)),
        "L1_ffn_a3T": T(np.asarray(inputs["L1_ffn_a3"]).transpose(2, 0, 1).reshape(D, -1)),
        "L1_ffn_b1T": T(np.asarray(inputs["L1_ffn_b1"]).transpose(0, 2, 1)),
        "L1_ffn_b3T": T(np.asarray(inputs["L1_ffn_b3"]).transpose(0, 2, 1)),
        "L1_ffn_a2T": T(np.asarray(inputs["L1_ffn_a2"]).transpose(0, 2, 1)),
        "L1_ffn_b2T": T(np.asarray(inputs["L1_ffn_b2"]).transpose(0, 2, 1)),
    }
    for n in ("L0_attn_Wq", "L0_attn_Wk", "L0_attn_Wv", "L0_attn_Wu",
              "L0_ffn_b_exp", "L0_ffn_up_bias", "L0_ffn_W_con", "L0_ffn_b_con",
              "L0_ffn_down_bias", "L1_attn_Wq", "L1_attn_Wk", "L1_attn_Wv",
              "L1_attn_Wo", "L1_attn_Wf1", "L1_attn_Wf2", "L1_attn_Wbeta",
              "L1_attn_Wog1", "L1_attn_Wog2", "L1_attn_norm_w", "L1_ffn_norm_w"):
        im[n] = np.ascontiguousarray(np.asarray(inputs[n]), dtype=np.float32)
    for n in ("r_pad", "FATu", "FAu", "FBTu", "FBu", "FATd", "FAd", "FBTd", "FBd",
              "ident", "antident", "m_tri_kq", "m_l", "m_lt", "m_kbt", "m_aq", "m_kq"):
        im[n] = aux[n]
    return im


def kernel(**inputs):
    aux = make_host_aux(inputs)
    b = build_program()
    in_maps = [make_in_map(inputs, aux, c) for c in range(8)]
    res = run_bass_kernel_spmd(b.nc, in_maps, core_ids=list(range(8)))
    return np.stack([res.results[c]["out"] for c in range(8)], axis=0)

# version 09
# speedup vs baseline: 1.2832x; 1.2832x over previous
import numpy as np
import ml_dtypes
import concourse.bass as bass
import concourse.mybir as mybir
from concourse.bass_utils import run_bass_kernel_spmd

F32 = mybir.dt.float32
BF16 = mybir.dt.bfloat16
AF = mybir.ActivationFunctionType
ALU = mybir.AluOpType

S = 4096
D = 1024
H = 8
HD = 128
FF = 4096
CT = 64
NCH = S // CT
NEG = -4294967295.0
EPS1 = 1e-5
EPS6 = 1e-6


class Sched:
    ENGINES = ("sync", "scalar", "tensor", "vector", "gpsimd")

    def __init__(self):
        self.ops = []
        self.fences = []

    def op(self, stream, fn, reads=(), writes=(), dma=False):
        assert stream in self.ENGINES
        self.ops.append(dict(stream=stream, fn=fn, reads=tuple(reads),
                             writes=tuple(writes), dma=dma,
                             need_inc=False, waits=None))

    def fence(self):
        self.fences.append(len(self.ops))

    def dma(self, stream, out, in_, wkey, rkeys=(), **kw):
        self.op(stream, lambda e: e.dma_start(out=out, in_=in_, **kw),
                reads=rkeys, writes=(wkey,), dma=True)

    DMA_RING = {"sync": 16, "gpsimd": 8, "scalar": 4}

    def analyze(self):
        fence_set = set(self.fences)
        writers = {}
        readers = {}
        last_by_chan = {}
        fence_deps = {}
        dma_count = {}
        deps = [set() for _ in self.ops]
        for i, o in enumerate(self.ops):
            if i in fence_set:
                fence_deps = dict(last_by_chan)
                writers = {}
                readers = {}
            for (st, dm_, slot), j in fence_deps.items():
                if st != o["stream"] or dm_:
                    deps[i].add(j)
            for k in o["reads"]:
                deps[i].update(writers.get(k, ()))
            for k in o["writes"]:
                deps[i].update(writers.get(k, ()))
                deps[i].update(readers.get(k, ()))
            for k in o["writes"]:
                writers.setdefault(k, []).append(i)
                readers[k] = []
            for k in o["reads"]:
                readers.setdefault(k, []).append(i)
            if o["dma"]:
                st = o["stream"]
                kcnt = dma_count.get(st, 0)
                dma_count[st] = kcnt + 1
                last_by_chan[(st, True, kcnt % self.DMA_RING[st])] = i
            else:
                last_by_chan[(o["stream"], False, 0)] = i
        for i, o in enumerate(self.ops):
            deps[i] = {j for j in deps[i]
                       if not (o["stream"] == "tensor"
                               and self.ops[j]["stream"] == "tensor"
                               and not self.ops[j]["dma"])}
        for i in range(len(self.ops)):
            for j in deps[i]:
                self.ops[j]["need_inc"] = True
        last_of = {}
        for i, o in enumerate(self.ops):
            if o["dma"]:
                o["need_inc"] = True
            last_of[(o["stream"], o["dma"])] = i
        for i in last_of.values():
            self.ops[i]["need_inc"] = True
        counts = {}
        pos = {}
        dma_idx = {}
        self.op_sem = {}
        for i, o in enumerate(self.ops):
            if o["dma"]:
                st = o["stream"]
                r = self.DMA_RING[st]
                k = dma_idx.get(st, 0)
                dma_idx[st] = k + 1
                sem = f"{st}_d{k % r}"
                prev = counts.get(sem, 0)
                o["throttle"] = (sem, prev) if prev > 0 else None
                counts[sem] = prev + 16
                pos[i] = (sem, prev + 16)
                self.op_sem[i] = sem
            else:
                o["throttle"] = None
                if o["need_inc"]:
                    sem = o["stream"] + "_c"
                    counts[sem] = counts.get(sem, 0) + 1
                    pos[i] = (sem, counts[sem])
                    self.op_sem[i] = sem
        known = {s: {} for s in self.ENGINES}
        nwaits = 0
        for i, o in enumerate(self.ops):
            need = {}
            for j in deps[i]:
                assert j in pos, f"dep {j} without inc"
                sem, c = pos[j]
                need[sem] = max(need.get(sem, 0), c)
            if o["throttle"]:
                sem, c = o["throttle"]
                need[sem] = max(need.get(sem, 0), c)
            w = []
            for sem, c in sorted(need.items()):
                if known[o["stream"]].get(sem, 0) < c:
                    w.append((sem, c))
                    known[o["stream"]][sem] = c
            o["waits"] = w
            nwaits += len(w)
        self.sem_totals = dict(counts)
        self.nwaits = nwaits

    def build(self, nc, block):
        sems = {}
        import contextlib
        self._es = contextlib.ExitStack()
        for name in sorted(set(self.sem_totals) | set(s for _, c in [(0, 0)] for s in [])):
            sems[name] = self._es.enter_context(nc.semaphore(name))
        ops = self.ops
        totals = self.sem_totals
        op_sem = self.op_sem

        def body_for(stream):
            def body(eng):
                for i, o in enumerate(ops):
                    if o["stream"] != stream:
                        continue
                    for sem, c in o["waits"]:
                        eng.wait_ge(sems[sem], c)
                    inst = o["fn"](eng)
                    if i in op_sem:
                        inst.then_inc(sems[op_sem[i]], 16 if o["dma"] else 1)
                if stream == "sync":
                    for sem, c in sorted(totals.items()):
                        eng.wait_ge(sems[sem], c)
            return body

        block.sync(body_for("sync"))
        block.vector(body_for("vector"))
        block.scalar(body_for("scalar"))
        block.tensor(body_for("tensor"))
        block.gpsimd(body_for("gpsimd"))


def make_factor_tiles(weights, transpose):
    FA = np.zeros((7, 32, 128, 128), np.float32)
    FB = np.zeros((5, 32, 128, 128), np.float32)
    for s in range(12):
        st = 2 ** s
        w = np.asarray(weights[s]).reshape(FF // (2 * st), st, 2, 2)
        f = np.arange(FF)
        b = f // (2 * st)
        j = f % st
        half = (f // st) % 2
        dcoef = np.where(half == 0, w[b, j, 0, 0], w[b, j, 1, 1])
        ocoef = np.where(half == 0, w[b, j, 0, 1], w[b, j, 1, 0])
        partner = f ^ st
        if transpose:
            ocoef = np.where(half == 0, w[b, j, 1, 0], w[b, j, 0, 1])
        if s < 7:
            c = f // 128
            p = f % 128
            pp = partner % 128
            FA[s, c, p, p] = dcoef
            FA[s, c, p, pp] = ocoef
        else:
            c = f % 32
            p = f // 32
            pp = partner // 32
            FB[s - 7, c, p, p] = dcoef
            FB[s - 7, c, p, pp] = ocoef
    FAT = FA.transpose(0, 1, 3, 2).copy()
    FBT = FB.transpose(0, 1, 3, 2).copy()
    return FA, FB, FAT, FBT


def make_host_aux(inputs):
    aux = {}
    rb = np.asarray(inputs["L0_attn_rel_bias"])
    idx = np.clip(np.arange(-4095, -4095 + 8192) + 2047, 0, 4094)
    aux["r_pad"] = rb[idx, :].T.copy().astype(np.float32)
    FA, FB, FAT, FBT = make_factor_tiles(np.asarray(inputs["L0_ffn_up_weights"]), False)
    aux["FAu"], aux["FBu"], aux["FATu"], aux["FBTu"] = FA, FB, FAT, FBT
    FA, FB, FAT, FBT = make_factor_tiles(np.asarray(inputs["L0_ffn_down_weights"]), True)
    aux["FAd"], aux["FBd"], aux["FATd"], aux["FBTd"] = FA, FB, FAT, FBT
    aux["ident"] = np.eye(128, dtype=np.float32)
    aux["antident"] = np.eye(128, dtype=np.float32)[::-1].copy()
    ii = np.arange(128)
    aux["m_tri_kq"] = (ii[:, None] <= ii[None, :]).astype(np.float32)
    C = 2 * CT
    st = np.arange(C)
    modd = 2 * np.arange(CT) + 1
    aux["m_l"] = (st[:, None] > st[None, :]).astype(np.float32)
    aux["m_lt"] = aux["m_l"].T.copy()
    aux["m_kbt"] = (modd[:, None] < st[None, :]).astype(np.float32)
    aux["m_aq"] = (st[:, None] <= modd[None, :]).astype(np.float32)
    aux["m_kq"] = (modd[:, None] <= modd[None, :]).astype(np.float32)
    return aux


class B:

    def __init__(self, upto=99, taps=()):
        self.upto = upto
        self.taps = set(taps)
        self.nc = bass.Bass("TRN2", target_bir_lowering=False)
        self.sc = Sched()
        self.dram = {}
        self.tap_handles = {}

    def din(self, name, shape, dtype=F32):
        h = self.nc.dram_tensor(name, list(shape), dtype, kind="ExternalInput")
        self.dram[name] = h
        return h

    def dscratch(self, name, shape, dtype=BF16):
        kind = "ExternalOutput" if name in self.taps else "Internal"
        h = self.nc.dram_tensor(name, list(shape), dtype, kind=kind)
        self.dram[name] = h
        return h

    def dout(self, name, shape, dtype=F32):
        h = self.nc.dram_tensor(name, list(shape), dtype, kind="ExternalOutput")
        self.dram[name] = h
        return h

    def tap(self, name, src_ap, shape, key):
        if name not in self.taps:
            return
        h = self.dout("tap_" + name, shape, F32)
        self.sc.dma("sync", h[tuple(slice(0, s) for s in shape)], src_ap,
                    "tap_" + name, rkeys=(key,))

    def declare(self):
        nc, d = self.nc, self.din
        d("x", (S, D))
        for n in ("L0_attn_Wq", "L0_attn_Wk", "L0_attn_Wv", "L0_attn_Wu"):
            d(n, (D, D))
        d("L0_ffn_W_expT", (FF, D))
        d("L0_ffn_b_exp", (FF,))
        d("L0_ffn_up_bias", (FF,))
        d("L0_ffn_W_con", (FF, D))
        d("L0_ffn_b_con", (D,))
        d("L0_ffn_down_bias", (D,))
        for n in ("L1_attn_Wq", "L1_attn_Wk", "L1_attn_Wv", "L1_attn_Wo"):
            d(n, (D, D))
        d("L1_attn_Wf1", (D, HD))
        d("L1_attn_Wf2", (HD, D))
        d("L1_attn_Wbeta", (D, H))
        d("L1_attn_Wog1", (D, HD))
        d("L1_attn_Wog2", (HD, D))
        d("L1_attn_norm_w", (D,))
        d("L1_ffn_norm_w", (D,))
        d("L1_ffn_w1T", (D, FF))
        d("L1_ffn_w3T", (D, FF))
        d("L1_ffn_w2T", (FF, D))
        d("L1_ffn_gate_wT", (D, H))
        d("L1_ffn_a1T", (D, H * 16))
        d("L1_ffn_a3T", (D, H * 16))
        d("L1_ffn_b1T", (H, 16, FF))
        d("L1_ffn_b3T", (H, 16, FF))
        d("L1_ffn_a2T", (H, FF, 16))
        d("L1_ffn_b2T", (H, 16, D))
        d("r_pad", (H, 8192))
        d("FATu", (7, 32, 128, 128)); d("FAu", (7, 32, 128, 128))
        d("FBTu", (5, 32, 128, 128)); d("FBu", (5, 32, 128, 128))
        d("FATd", (7, 32, 128, 128)); d("FAd", (7, 32, 128, 128))
        d("FBTd", (5, 32, 128, 128)); d("FBd", (5, 32, 128, 128))
        d("ident", (128, 128))
        d("antident", (128, 128))
        d("m_tri_kq", (128, 128))
        d("m_l", (128, 128)); d("m_lt", (128, 128))
        d("m_kbt", (64, 128)); d("m_aq", (128, 64)); d("m_kq", (64, 64))
        self.dout("out", (S, D))

        ds = self.dscratch
        for n in ("L0_attn_Wq", "L0_attn_Wk", "L0_attn_Wv", "L0_attn_Wu",
                  "L1_attn_Wq", "L1_attn_Wk", "L1_attn_Wv", "L1_attn_Wo"):
            ds("c_" + n, (D, D))
        ds("c_W_expT", (FF, D)); ds("c_W_con", (FF, D))
        ds("c_w1T", (D, FF)); ds("c_w3T", (D, FF)); ds("c_w2T", (FF, D))
        ds("c_gate_wT", (D, H))
        ds("c_a1T", (D, H * 16)); ds("c_a3T", (D, H * 16))
        ds("c_b1T", (H, 16, FF)); ds("c_b3T", (H, 16, FF))
        ds("c_a2T", (H, FF, 16)); ds("c_b2T", (H, 16, D))
        ds("c_wf1", (D, HD)); ds("c_wf2", (HD, D))
        ds("c_wbeta", (D, H)); ds("c_wog1", (D, HD)); ds("c_wog2", (HD, D))
        for n in ("FATu", "FAu", "FBTu", "FBu", "FATd", "FAd", "FBTd", "FBd"):
            ds("c_" + n, self.dram[n].shape)
        ds("w1pre", (FF, D))
        ds("xn0", (S, D)); ds("xn1", (S, D))
        ds("q0fm", (D, S)); ds("k0fm", (D, S))
        ds("u0s", (S, D)); ds("v0p", (S, D))
        ds("attn0", (S, D))
        ds("x1", (S, D), F32)
        ds("q1s", (S, D)); ds("kn", (S, D)); ds("knb", (S, D)); ds("v1s", (S, D))
        ds("lsp", (D, S), F32)
        ds("og1fm", (HD, S))
        ds("o1fm", (D, S))
        ds("yn2fm", (D, S))
        ds("l1a", (S, D))
        ds("trms", (S, D))
        ds("wrow", (H, S), F32)
        ds("base1", (FF, S)); ds("base3", (FF, S)); ds("mixedfm", (FF, S))
        ds("u2w", (H, 16, S))

        nb = self.nc
        self.RW1 = nb.alloc_sbuf_tensor("RW1", [128, 32768], BF16)
        self.RW2 = nb.alloc_sbuf_tensor("RW2", [128, 32768], BF16)
        self.RG = nb.alloc_sbuf_tensor("RG", [128, 16384], BF16)
        self.RF = nb.alloc_sbuf_tensor("RF", [128, 8192], F32)
        self.RS = nb.alloc_sbuf_tensor("RS", [128, 1024], F32)
        self.RM = nb.alloc_sbuf_tensor("RM", [128, 3072], BF16)
        self.RIDX = nb.alloc_sbuf_tensor("RIDX", [128, 16], mybir.dt.uint32)
        self.PS = [nb.alloc_psum_tensor(f"ps{i}", [128, 512], F32) for i in range(6)]
        self.PB = [nb.alloc_psum_tensor(f"pb{i}", [128, 1024], BF16) for i in range(2)]

    def p0_prep(self):
        sc, dm = self.sc, self.dram
        for n in ("L0_attn_Wq", "L0_attn_Wk", "L0_attn_Wv", "L0_attn_Wu",
                  "L1_attn_Wq", "L1_attn_Wk", "L1_attn_Wv", "L1_attn_Wo"):
            sc.dma("gpsimd", dm["c_" + n][:, :], dm[n][:, :], "c_" + n)
        sc.dma("gpsimd", dm["c_W_expT"][:, :], dm["L0_ffn_W_expT"][:, :], "c_W_expT")
        sc.dma("gpsimd", dm["c_W_con"][:, :], dm["L0_ffn_W_con"][:, :], "c_W_con")
        for a, b_ in (("c_w1T", "L1_ffn_w1T"), ("c_w3T", "L1_ffn_w3T"),
                      ("c_w2T", "L1_ffn_w2T"), ("c_gate_wT", "L1_ffn_gate_wT"),
                      ("c_a1T", "L1_ffn_a1T"), ("c_a3T", "L1_ffn_a3T"),
                      ("c_wf1", "L1_attn_Wf1"), ("c_wf2", "L1_attn_Wf2"),
                      ("c_wbeta", "L1_attn_Wbeta"), ("c_wog1", "L1_attn_Wog1"),
                      ("c_wog2", "L1_attn_Wog2")):
            sc.dma("gpsimd", dm[a][:, :], dm[b_][:, :], a)
        for a, b_ in (("c_b1T", "L1_ffn_b1T"), ("c_b3T", "L1_ffn_b3T"),
                      ("c_a2T", "L1_ffn_a2T"), ("c_b2T", "L1_ffn_b2T")):
            sc.dma("gpsimd", dm[a][:, :, :], dm[b_][:, :, :], a)
        for n in ("FATu", "FAu", "FBTu", "FBu", "FATd", "FAd", "FBTd", "FBd"):
            sc.dma("gpsimd", dm["c_" + n][:, :, :, :], dm[n][:, :, :, :], "c_" + n)
        RM = self.RM
        sc.dma("gpsimd", RM[:, 0:128], dm["ident"][:, :], "RM_id")
        sc.dma("gpsimd", RM[:, 128:256], dm["antident"][:, :], "RM_J")
        sc.dma("gpsimd", RM[:, 256:384], dm["m_tri_kq"][:, :], "RM_tri")
        sc.dma("gpsimd", RM[:, 384:512], dm["m_l"][:, :], "RM_ml")
        sc.dma("gpsimd", RM[:, 512:640], dm["m_lt"][:, :], "RM_mlt")
        sc.dma("gpsimd", RM[0:64, 640:768], dm["m_kbt"][:, :], "RM_mkbt")
        sc.dma("gpsimd", RM[:, 768:832], dm["m_aq"][:, :], "RM_maq")
        sc.dma("gpsimd", RM[0:64, 832:896], dm["m_kq"][:, :], "RM_mkq")
        self.sc.op("vector", lambda v: v.memset(RM[0:1, 896:1024], 1.0), writes=("RM_ones",))
        self.sc.op("vector", lambda v: v.memset(RM[0:1, 1024:1025], 1.0), writes=("RM_one1",))
        RS = self.RS
        self.sc.op("vector", lambda v: v.memset(RS[:, 0:1], EPS6), writes=("RS_eps6",))
        self.sc.op("vector", lambda v: v.memset(RS[:, 1:2], EPS1), writes=("RS_eps1",))
        sc.dma("sync", RS[:, 8:16], bass.AP(dm["L1_attn_norm_w"], 0, [[1, 128], [128, 8]]),
               "RS_nw1", allow_slow_non_contiguous=True)
        self.sc.op("vector", lambda v: v.memset(RS[:, 512:576], 0.0), writes=("RS_zero64",))
        self.sc.op("vector", lambda v: v.memset(self.RM[:, 1026:1027], 1.0), writes=("RM_onescol",))

    def p2_ln0(self):
        sc, dm = self.sc, self.dram
        RF, RS = self.RF, self.RS
        for it in range(S // 128):
            t0 = it * 128
            xk = f"ln0x{it & 1}"
            xt = RF[:, (it & 1) * 1024:(it & 1) * 1024 + 1024]
            sc.dma("sync", xt, dm["x"][t0:t0 + 128, :], xk)
            sq = RF[:, 2048 + (it & 1) * 1024: 3072 + (it & 1) * 1024]
            sqk = f"ln0sq{it & 1}"
            cols = RS[:, 16 + (it & 1) * 8: 24 + (it & 1) * 8]
            ck = f"ln0c{it & 1}"
            sc.op("vector", lambda v, xt=xt, c=cols: v.reduce_sum(c[:, 0:1], xt, axis=mybir.AxisListType.X),
                  reads=(xk,), writes=(ck,))
            sc.op("scalar", lambda s, xt=xt, sq=sq: s.activation(sq, xt, AF.Square),
                  reads=(xk,), writes=(sqk,))
            sc.op("vector", lambda v, sq=sq, c=cols: v.reduce_sum(c[:, 1:2], sq, axis=mybir.AxisListType.X),
                  reads=(sqk,), writes=(ck,))
            sc.op("vector", lambda v, c=cols: v.tensor_scalar_mul(c[:, 2:3], c[:, 0:1], 1.0 / D),
                  reads=(ck,), writes=(ck,))
            sc.op("vector", lambda v, c=cols: v.tensor_mul(c[:, 3:4], c[:, 2:3], c[:, 2:3]),
                  reads=(ck,), writes=(ck,))
            sc.op("vector", lambda v, c=cols: v.tensor_scalar(c[:, 4:5], c[:, 1:2], 1.0 / D, None, ALU.mult),
                  reads=(ck,), writes=(ck,))
            sc.op("vector", lambda v, c=cols: v.tensor_sub(c[:, 4:5], c[:, 4:5], c[:, 3:4]),
                  reads=(ck,), writes=(ck,))
            sc.op("scalar", lambda s, c=cols: s.activation(c[:, 5:6], c[:, 4:5], AF.Sqrt, bias=self.RS[:, 1:2]),
                  reads=(ck, "RS_eps1"), writes=(ck,))
            sc.op("vector", lambda v, c=cols: v.reciprocal(c[:, 6:7], c[:, 5:6]),
                  reads=(ck,), writes=(ck,))
            xn = self.RG[:, (it & 1) * 1024:(it & 1) * 1024 + 1024]
            xnk = f"ln0o{it & 1}"
            sc.op("vector", lambda v, xt=xt, xn=xn, c=cols: v.tensor_scalar(
                xn, xt, c[:, 2:3], c[:, 6:7], ALU.subtract, ALU.mult),
                reads=(xk, ck), writes=(xnk,))
            sc.dma("sync", dm["xn0"][t0:t0 + 128, :], xn, "xn0", rkeys=(xnk,))

    def load_w_lhsT(self, cname, region, off, douts, key):
        for ch in range(8):
            self.sc.dma("sync", region[:, off + ch * douts: off + (ch + 1) * douts],
                        self.dram[cname][ch * 128:(ch + 1) * 128, 0:douts],
                        key, rkeys=(cname,))

    def p3_l0proj(self):
        sc, dm = self.sc, self.dram
        RW1, RG, RF, RS, RM = self.RW1, self.RG, self.RF, self.RS, self.RM
        self.load_w_lhsT("c_L0_attn_Wq", RW1, 0, D, "wq0")
        self.load_w_lhsT("c_L0_attn_Wk", RW1, 8192, D, "wk0")
        self.load_w_lhsT("c_L0_attn_Wv", RW1, 16384, D, "wv0")
        self.load_w_lhsT("c_L0_attn_Wu", RW1, 24576, D, "wu0")
        PS = self.PS
        for tt in range(8):
            t0 = tt * 512
            pb = tt & 1
            xnf = RG[:, pb * 4096: pb * 4096 + 4096]
            xnfk = f"p3xn{pb}"
            for ch in range(8):
                sc.dma("sync", xnf[:, ch * 512:(ch + 1) * 512],
                       dm["xn0"][t0:t0 + 512, ch * 128:(ch + 1) * 128],
                       xnfk, rkeys=("xn0",), transpose=True)
            for wname, woff, dst in (("wq0", 0, "q0fm"), ("wk0", 8192, "k0fm")):
                for hh in range(8):
                    ps = PS[hh % 2]
                    pk = f"p3ps{hh % 2}"
                    for ch in range(8):
                        sc.op("tensor", lambda t, ps=ps, ch=ch, hh=hh, woff=woff, xnf=xnf:
                              t.matmul(ps[:, 0:512], self.RW1[:, woff + ch * D + hh * 128: woff + ch * D + hh * 128 + 128],
                                       xnf[:, ch * 512:(ch + 1) * 512],
                                       start=(ch == 0), stop=(ch == 7)),
                              reads=(wname, xnfk), writes=(pk,))
                    ev = RG[:, 8192 + (hh % 2) * 512: 8192 + (hh % 2) * 512 + 512]
                    evk = f"p3ev{hh % 2}"
                    sc.op("scalar", lambda s, ps=ps, ev=ev: s.activation(ev, ps[:, 0:512], AF.Silu),
                          reads=(pk,), writes=(evk,))
                    sc.dma("sync", dm[dst][hh * 128:(hh + 1) * 128, t0:t0 + 512], ev,
                           dst, rkeys=(evk,))
            for wname, woff, dst, dok in (("wu0", 24576, "u0s", 0), ("wv0", 16384, "v0p", 1)):
                for st4 in range(4):
                    ts0 = t0 + st4 * 128
                    for dh in range(2):
                        ps = PS[2 + dh]
                        pk = f"p3pt{dh}"
                        for ch in range(8):
                            sc.op("tensor", lambda t, ps=ps, ch=ch, dh=dh, woff=woff, xnf=xnf, st4=st4:
                                  t.matmul(ps[:, 0:512],
                                           xnf[:, ch * 512 + st4 * 128: ch * 512 + st4 * 128 + 128],
                                           self.RW1[:, woff + ch * D + dh * 512: woff + ch * D + dh * 512 + 512],
                                           start=(ch == 0), stop=(ch == 7)),
                                  reads=(wname, xnfk), writes=(pk,))
                    ev = RG[:, 9216 + (st4 & 1) * 1024: 9216 + (st4 & 1) * 1024 + 1024]
                    evk = f"p3evt{st4 & 1}"
                    for dh in range(2):
                        sc.op("scalar", lambda s, dh=dh, ev=ev: s.activation(
                            ev[:, dh * 512:(dh + 1) * 512], PS[2 + dh][:, 0:512], AF.Silu),
                            reads=(f"p3pt{dh}",), writes=(evk,))
                    sc.dma("sync", dm[dst][ts0:ts0 + 128, :], ev, dst, rkeys=(evk,))

    def p3b_keymask(self):
        sc, dm = self.sc, self.dram
        RG, RF = self.RG, self.RF
        for it in range(S // 128):
            t0 = it * 128
            pb = it & 1
            ktk = f"kmk{pb}"
            kt = RG[:, pb * 1024: pb * 1024 + 1024]
            for hh in range(8):
                sc.dma("sync", kt[:, hh * 128:(hh + 1) * 128],
                       dm["k0fm"][hh * 128:(hh + 1) * 128, t0:t0 + 128],
                       ktk, rkeys=("k0fm",), transpose=True)
            km = RF[:, 4096 + pb * 16: 4096 + pb * 16 + 16]
            kmk = f"kmc{pb}"
            for hh in range(8):
                sc.op("vector", lambda v, hh=hh, kt=kt, km=km: v.reduce_sum(
                    km[:, hh:hh + 1], kt[:, hh * 128:(hh + 1) * 128],
                    axis=mybir.AxisListType.X, apply_absolute_value=True),
                    reads=(ktk,), writes=(kmk,))
            sc.op("scalar", lambda s, km=km: s.activation(km[:, 8:16], km[:, 0:8], AF.Sign),
                  reads=(kmk,), writes=(kmk,))
            vt = RG[:, 2048 + pb * 1024: 2048 + pb * 1024 + 1024]
            vtk = f"kmv{pb}"
            sc.dma("sync", vt, dm["v0p"][t0:t0 + 128, :], vtk, rkeys=("v0p",))
            for hh in range(8):
                sc.op("vector", lambda v, hh=hh, vt=vt, km=km: v.tensor_scalar_mul(
                    vt[:, hh * 128:(hh + 1) * 128], vt[:, hh * 128:(hh + 1) * 128],
                    km[:, 8 + hh:9 + hh]),
                    reads=(kmk,), writes=(vtk,))
            sc.dma("sync", dm["v0p"][t0:t0 + 128, :], vt, "v0p", rkeys=(vtk,))

    def p4_l0attn(self):
        sc, dm = self.sc, self.dram
        RW1, RG, RM, PS = self.RW1, self.RG, self.RM, self.PS
        SK = 1.0 / np.sqrt(128.0)
        for h in range(H):
            skw = RW1[:, 0:8064]
            sc.dma("gpsimd", skw, bass.AP(dm["r_pad"], h * 8192, [[1, 128], [1, 8064]]),
                   "skw")
            kfm = RW1[:, 8192:12288]
            sc.dma("sync", kfm, dm["k0fm"][h * 128:(h + 1) * 128, :], "kfm")
            vst = RW1[:, 12288:16384]
            sc.dma("sync", vst, bass.AP(dm["v0p"], h * 128,
                                        [[1024, 128], [128 * 1024, 32], [1, 128]]),
                   "vst")
            for qg in range(8):
                q0 = qg * 512
                qb = RG[:, 0:512]
                sc.dma("sync", qb, dm["q0fm"][h * 128:(h + 1) * 128, q0:q0 + 512], "p4q")
                ub = RG[:, 512:1024]
                for sub in range(4):
                    sc.dma("sync", ub[:, sub * 128:(sub + 1) * 128],
                           dm["u0s"][q0 + sub * 128: q0 + (sub + 1) * 128,
                                     h * 128:(h + 1) * 128], "p4u")
                nkt = 4 * qg + 4

                def emit_scores(kt):
                    ps = PS[kt & 1]
                    pk = f"p4s{kt & 1}"
                    base = q0 - kt * 128 + 3968
                    sc.op("tensor", lambda t, ps=ps, kt=kt, qb=qb, kfm=kfm:
                          t.matmul(ps[:, 0:512], kfm[:, kt * 128:(kt + 1) * 128], qb,
                                   start=True, stop=False),
                          reads=("kfm", "p4q"), writes=(pk,))
                    sc.op("tensor", lambda t, ps=ps, base=base, skw=skw:
                          t.matmul(ps[:, 0:512], self.RM[:, 128:256],
                                   skw[:, base:base + 512], start=False, stop=True),
                          reads=("skw", "RM_J"), writes=(pk,))
                    wb = RG[:, 1024 + (kt & 1) * 512: 1024 + (kt & 1) * 512 + 512]
                    wk = f"p4w{kt & 1}"
                    sc.op("scalar", lambda s, ps=ps, wb=wb: s.activation(
                        wb, ps[:, 0:512], AF.Silu, scale=SK),
                        reads=(f"p4s{kt & 1}",), writes=(wk,))
                    j = kt - 4 * qg
                    if j >= 0:
                        if j > 0:
                            sc.op("vector", lambda v, wb=wb, j=j: v.memset(wb[:, 0:j * 128], 0.0),
                                  reads=(), writes=(wk,))
                        sc.op("vector", lambda v, wb=wb, j=j: v.tensor_mul(
                            wb[:, j * 128:(j + 1) * 128], wb[:, j * 128:(j + 1) * 128],
                            self.RM[:, 256:384]),
                            reads=("RM_tri",), writes=(wk,))

                def emit_attn(kt):
                    wb = RG[:, 1024 + (kt & 1) * 512: 1024 + (kt & 1) * 512 + 512]
                    wk = f"p4w{kt & 1}"
                    pa = PS[2]
                    for sub in range(4):
                        sc.op("tensor", lambda t, pa=pa, wb=wb, kt=kt, sub=sub, vst=vst, nkt=nkt:
                              t.matmul(pa[:, sub * 128:(sub + 1) * 128],
                                       wb[:, sub * 128:(sub + 1) * 128],
                                       vst[:, kt * 128:(kt + 1) * 128],
                                       start=(kt == 0 and sub == 0),
                                       stop=(kt == nkt - 1 and sub == 3)),
                              reads=(wk, "vst"), writes=("p4acc",))

                emit_scores(0)
                for kt in range(1, nkt):
                    emit_scores(kt)
                    emit_attn(kt - 1)
                emit_attn(nkt - 1)
                ob = RG[:, 3072:3584]
                sc.op("vector", lambda v, ob=ob, ub=ub: v.tensor_mul(
                    ob, PS[2][:, 0:512], ub),
                    reads=("p4acc", "p4u"), writes=("p4o",))
                sc.dma("sync",
                       bass.AP(dm["attn0"], q0 * D + h * 128,
                               [[D, 128], [128 * D, 4], [1, 128]]),
                       ob, "attn0", rkeys=("p4o",))
            self.sc.fence()

    def _compose(self, f_name, ft_name, stage_list, c, mslot, mtslot, keyb):
        sc, dm, RG = self.sc, self.dram, self.RG
        s0 = stage_list[0]
        m = RG[:, mslot:mslot + 128]
        mt = RG[:, mtslot:mtslot + 128]
        sc.dma("sync", m, dm[f_name][s0, c, :, :], keyb + "m")
        sc.dma("sync", mt, dm[ft_name][s0, c, :, :], keyb + "mt")
        ftile = RG[:, mslot + 512:mslot + 640]
        for i, s in enumerate(stage_list[1:]):
            sc.dma("sync", ftile, dm[ft_name][s, c, :, :], keyb + "f")
            pm, pmt = self.PS[4], self.PS[5]
            sc.op("tensor", lambda t, ftile=ftile, m=m, pm=pm: t.matmul(
                pm[:, 0:128], ftile, m, start=True, stop=True),
                reads=(keyb + "f", keyb + "m"), writes=(keyb + "pm",))
            sc.op("tensor", lambda t, ftile=ftile, mt=mt, pmt=pmt: t.matmul(
                pmt[:, 0:128], m, ftile, start=True, stop=True),
                reads=(keyb + "f", keyb + "m", keyb + "mt"), writes=(keyb + "pmt",))
            sc.op("vector", lambda v, m=m, pm=pm: v.tensor_copy(m, pm[:, 0:128]),
                  reads=(keyb + "pm",), writes=(keyb + "m",))
            sc.op("scalar", lambda s_, mt=mt, pmt=pmt: s_.copy(mt, pmt[:, 0:128]),
                  reads=(keyb + "pmt",), writes=(keyb + "mt",))
        return m, mt

    def p1_fold(self):
        sc, dm, RG, PS = self.sc, self.dram, self.RG, self.PS
        self.dscratch("w1a", (FF, D))
        self.dscratch("w2mid", (FF, D))
        self.dscratch("b1a", (FF,)); self.dscratch("b1b", (FF,))
        wct = RG[:, 1024:2048]
        bcol = RG[:, 2048:2049]
        for sub in range(4):
            for c in range(32):
                if sub == 0:
                    sc.dma("sync", wct, dm["c_W_expT"][c * 128:(c + 1) * 128, :], "p1w",
                           rkeys=("c_W_expT",))
                    sc.dma("gpsimd", bcol, bass.AP(dm["L0_ffn_b_exp"], c * 128, [[1, 128], [1, 1]]),
                           "p1b")
                    m, mt = self._compose("c_FAu", "c_FATu", [0, 1, 2, 3, 4, 5, 6], c, 4096, 4352, "p1")
                elif sub == 1:
                    sc.dma("sync", wct, bass.AP(dm["w1a"], c * 1024, [[32 * 1024, 128], [1, 1024]]),
                           "p1w", rkeys=("w1a",))
                    sc.dma("sync", bcol, bass.AP(dm["b1a"], c, [[32, 128], [1, 1]]),
                           "p1b", rkeys=("b1a",), allow_slow_non_contiguous=True)
                    m, mt = self._compose("c_FBu", "c_FBTu", [0, 1, 2, 3, 4], c, 4096, 4352, "p1")
                elif sub == 2:
                    sc.dma("sync", wct, bass.AP(dm["c_W_con"], c * 1024, [[32 * 1024, 128], [1, 1024]]),
                           "p1w", rkeys=("c_W_con",))
                    m, mt = self._compose("c_FBd", "c_FBTd", [4, 3, 2, 1, 0], c, 4096, 4352, "p1")
                else:
                    sc.dma("sync", wct, dm["w2mid"][c * 128:(c + 1) * 128, :], "p1w",
                           rkeys=("w2mid",))
                    m, mt = self._compose("c_FAd", "c_FATd", [6, 5, 4, 3, 2, 1, 0], c, 4096, 4352, "p1")
                for dh in range(2):
                    ps = PS[dh]
                    sc.op("tensor", lambda t, ps=ps, mt=mt, dh=dh, wct=wct: t.matmul(
                        ps[:, 0:512], mt, wct[:, dh * 512:(dh + 1) * 512],
                        start=True, stop=True),
                        reads=("p1m", "p1mt", "p1w"), writes=(f"p1p{dh}",))
                oc = RG[:, 2560:3584]
                for dh in range(2):
                    sc.op("vector", lambda v, oc=oc, dh=dh: v.tensor_copy(
                        oc[:, dh * 512:(dh + 1) * 512], PS[dh][:, 0:512]),
                        reads=(f"p1p{dh}",), writes=("p1o",))
                if sub in (0, 1):
                    ps = PS[2]
                    sc.op("tensor", lambda t, ps=ps, mt=mt, bcol=bcol: t.matmul(
                        ps[:, 0:1], mt, bcol, start=True, stop=True),
                        reads=("p1m", "p1mt", "p1b"), writes=("p1pb",))
                    ob = RG[:, 2049:2050]
                    sc.op("vector", lambda v, ob=ob, ps=ps: v.tensor_copy(ob, ps[:, 0:1]),
                          reads=("p1pb",), writes=("p1ob",))
                if sub == 0:
                    sc.dma("sync", dm["w1a"][c * 128:(c + 1) * 128, :], oc, "w1a", rkeys=("p1o",))
                    sc.dma("sync", bass.AP(dm["b1a"], c * 128, [[1, 128], [1, 1]]),
                           RG[:, 2049:2050], "b1a", rkeys=("p1ob",))
                elif sub == 1:
                    sc.dma("sync", bass.AP(dm["w1pre"], c * 1024, [[32 * 1024, 128], [1, 1024]]),
                           oc, "w1pre", rkeys=("p1o",))
                    sc.dma("sync", bass.AP(dm["b1b"], c, [[32, 128], [1, 1]]),
                           RG[:, 2049:2050], "b1b", rkeys=("p1ob",), allow_slow_non_contiguous=True)
                elif sub == 2:
                    sc.dma("sync", bass.AP(dm["w2mid"], c * 1024, [[32 * 1024, 128], [1, 1024]]),
                           oc, "w2mid", rkeys=("p1o",))
                else:
                    sc.op("vector", lambda v, c=c, oc=oc: v.tensor_copy(
                        self.RW2[:, c * 1024:(c + 1) * 1024], oc),
                        reads=("p1o",), writes=("W2res",))
                self.sc.fence()

    def _ln_tile(self, xt, xk, csl, ck, xn_out, xnk, eps_key="RS_eps1", par=0):
        sc, RS = self.sc, self.RS
        sq = self.RF[:, 6144 + par * 1024: 7168 + par * 1024]
        sqk = f"lnsq{par}"
        sc.op("vector", lambda v: v.reduce_sum(csl[:, 0:1], xt, axis=mybir.AxisListType.X),
              reads=(xk,), writes=(ck,))
        sc.op("scalar", lambda s: s.activation(sq, xt, AF.Square), reads=(xk,), writes=(sqk,))
        sc.op("vector", lambda v: v.reduce_sum(csl[:, 1:2], sq, axis=mybir.AxisListType.X),
              reads=(sqk,), writes=(ck,))
        sc.op("vector", lambda v: v.tensor_scalar_mul(csl[:, 2:3], csl[:, 0:1], 1.0 / D),
              reads=(ck,), writes=(ck,))
        sc.op("vector", lambda v: v.tensor_mul(csl[:, 3:4], csl[:, 2:3], csl[:, 2:3]),
              reads=(ck,), writes=(ck,))
        sc.op("vector", lambda v: v.tensor_scalar_mul(csl[:, 4:5], csl[:, 1:2], 1.0 / D),
              reads=(ck,), writes=(ck,))
        sc.op("vector", lambda v: v.tensor_sub(csl[:, 4:5], csl[:, 4:5], csl[:, 3:4]),
              reads=(ck,), writes=(ck,))
        ecol = self.RS[:, 1:2] if eps_key == "RS_eps1" else self.RS[:, 0:1]
        sc.op("scalar", lambda s: s.activation(csl[:, 5:6], csl[:, 4:5], AF.Sqrt, bias=ecol),
              reads=(ck, eps_key), writes=(ck,))
        sc.op("vector", lambda v: v.reciprocal(csl[:, 6:7], csl[:, 5:6]),
              reads=(ck,), writes=(ck,))
        sc.op("vector", lambda v: v.tensor_scalar(xn_out, xt, csl[:, 2:3], csl[:, 6:7],
                                                  ALU.subtract, ALU.mult),
              reads=(xk, ck), writes=(xnk,))

    def p5_l0ffn(self):
        sc, dm = self.sc, self.dram
        RW1, RW2, RG, RF, RS, RM, PS = self.RW1, self.RW2, self.RG, self.RF, self.RS, self.RM, self.PS
        for ch in range(8):
            sc.dma("sync", RW1[:, ch * 4096:(ch + 1) * 4096],
                   dm["w1pre"][:, ch * 128:(ch + 1) * 128], "W1res",
                   rkeys=("w1pre",), transpose=True)
        for ft in range(32):
            sc.dma("gpsimd", RF[:, 0:1], bass.AP(dm["b1b"], ft * 128, [[1, 128], [1, 1]]),
                   "p5b1", rkeys=("b1b",))
            sc.dma("sync", RF[:, 1:2], bass.AP(dm["L0_ffn_up_bias"], ft * 128, [[1, 128], [1, 1]]),
                   "p5ub")
            sc.op("vector", lambda v, ft=ft: v.tensor_add(RS[:, 32 + ft:33 + ft],
                                                          RF[:, 0:1], RF[:, 1:2]),
                  reads=("p5b1", "p5ub"), writes=("bgelu",))
            self.sc.fence()
        sc.dma("sync", RF[0:1, 0:1024], dm["L0_ffn_b_con"][:], "p5bc")
        sc.dma("sync", RF[0:1, 1024:2048], dm["L0_ffn_down_bias"][:], "p5db")
        sc.op("vector", lambda v: v.tensor_add(RF[0:1, 0:1024], RF[0:1, 0:1024],
                                               RF[0:1, 1024:2048]),
              reads=("p5bc", "p5db"), writes=("p5bc",))
        sc.op("vector", lambda v: v.tensor_copy(RM[0:1, 2048:3072], RF[0:1, 0:1024]),
              reads=("p5bc",), writes=("bsum",))
        self.sc.fence()
        for tt in range(16):
            t0 = tt * 256
            pb = tt & 1
            xb = RG[:, pb * 2048: pb * 2048 + 2048]
            xbk = f"p5x{pb}"
            for ch in range(8):
                sc.dma("sync", xb[:, ch * 256:(ch + 1) * 256],
                       dm["xn0"][t0:t0 + 256, ch * 128:(ch + 1) * 128],
                       xbk, rkeys=("xn0",), transpose=True)
            h2 = RG[:, 4096:12288]
            h2k = "p5h2"
            for ft in range(32):
                ps = PS[ft & 1]
                pk = f"p5p{ft & 1}"
                for ch in range(8):
                    sc.op("tensor", lambda t, ps=ps, ch=ch, ft=ft, xb=xb: t.matmul(
                        ps[:, 0:256], RW1[:, ch * 4096 + ft * 128: ch * 4096 + ft * 128 + 128],
                        xb[:, ch * 256:(ch + 1) * 256], start=(ch == 0), stop=(ch == 7)),
                        reads=("W1res", xbk), writes=(pk,))
                sc.op("scalar", lambda s, ps=ps, ft=ft, h2=h2: s.activation(
                    h2[:, ft * 256:(ft + 1) * 256], ps[:, 0:256], AF.Gelu,
                    bias=RS[:, 32 + ft:33 + ft]),
                    reads=(pk, "bgelu"), writes=(h2k,))
            for t128 in range(2):
                ts0 = t0 + t128 * 128
                x0t = RF[:, 0:1024]
                sc.dma("sync", x0t, dm["x"][ts0:ts0 + 128, :], "p5x0")
                att = RG[:, 12288:13312]
                sc.dma("sync", att, dm["attn0"][ts0:ts0 + 128, :], "p5at", rkeys=("attn0",))
                x1t = RF[:, 1024:2048]
                for dh in range(2):
                    ps = PS[2 + dh]
                    pk = f"p5y{dh}"
                    for ft in range(32):
                        sc.op("tensor", lambda t, ps=ps, ft=ft, dh=dh, h2=h2, t128=t128:
                              t.matmul(ps[:, 0:512],
                                       h2[:, ft * 256 + t128 * 128: ft * 256 + t128 * 128 + 128],
                                       RW2[:, ft * 1024 + dh * 512: ft * 1024 + dh * 512 + 512],
                                       start=(ft == 0), stop=False),
                              reads=(h2k, "W2res"), writes=(pk,))
                    sc.op("tensor", lambda t, ps=ps, dh=dh: t.matmul(
                        ps[:, 0:512], RM[0:1, 896:1024], RM[0:1, 2048 + dh * 512: 2048 + (dh + 1) * 512],
                        start=False, stop=True),
                        reads=("RM_ones", "bsum"), writes=(pk,))
                    sc.op("vector", lambda v, ps=ps, dh=dh, x0t=x0t, x1t=x1t: v.tensor_add(
                        x1t[:, dh * 512:(dh + 1) * 512], ps[:, 0:512], x0t[:, dh * 512:(dh + 1) * 512]),
                        reads=(pk, "p5x0"), writes=("p5x1",))
                    sc.op("vector", lambda v, dh=dh, x1t=x1t, att=att: v.tensor_add(
                        x1t[:, dh * 512:(dh + 1) * 512], x1t[:, dh * 512:(dh + 1) * 512],
                        att[:, dh * 512:(dh + 1) * 512]),
                        reads=("p5at",), writes=("p5x1",))
                sc.dma("sync", dm["x1"][ts0:ts0 + 128, :], x1t, "x1", rkeys=("p5x1",))
                xn1t = RG[:, 13312:14336]
                self._ln_tile(x1t, f"p5x1", RS[:, 16 + t128 * 8:24 + t128 * 8], f"p5lc{t128}",
                              xn1t, "p5xn1", par=t128)
                sc.dma("sync", dm["xn1"][ts0:ts0 + 128, :], xn1t, "xn1", rkeys=("p5xn1",))
            self.sc.fence()


    def p6_l1proj(self):
        sc, dm = self.sc, self.dram
        RW1, RG, RF, RS, PS = self.RW1, self.RG, self.RF, self.RS, self.PS
        self.load_w_lhsT("c_L1_attn_Wq", RW1, 0, D, "wq1")
        self.load_w_lhsT("c_L1_attn_Wk", RW1, 8192, D, "wk1")
        self.load_w_lhsT("c_L1_attn_Wv", RW1, 16384, D, "wv1")
        self.load_w_lhsT("c_wf1", RW1, 24576, HD, "wf1")
        self.load_w_lhsT("c_wog1", RW1, 25600, HD, "wog1")
        self.load_w_lhsT("c_wbeta", RW1, 26624, H, "wbeta")
        sc.dma("sync", RW1[:, 27648:28672], dm["c_wf2"][:, :], "wf2", rkeys=("c_wf2",))
        for tt in range(8):
            t0 = tt * 512
            pb = tt & 1
            xb = RG[:, pb * 4096: pb * 4096 + 4096]
            xbk = f"p6xn{pb}"
            for ch in range(8):
                sc.dma("sync", xb[:, ch * 512:(ch + 1) * 512],
                       dm["xn1"][t0:t0 + 512, ch * 128:(ch + 1) * 128],
                       xbk, rkeys=("xn1",), transpose=True)
            for wname, woff, dst in (("wq1", 0, "q1s"), ("wv1", 16384, "v1s")):
                for st4 in range(4):
                    ts0 = t0 + st4 * 128
                    for dh in range(2):
                        ps = PS[dh]
                        pk = f"p6pt{dh}"
                        for ch in range(8):
                            sc.op("tensor", lambda t, ps=ps, ch=ch, dh=dh, woff=woff, xb=xb, st4=st4:
                                  t.matmul(ps[:, 0:512],
                                           xb[:, ch * 512 + st4 * 128: ch * 512 + st4 * 128 + 128],
                                           RW1[:, woff + ch * D + dh * 512: woff + ch * D + dh * 512 + 512],
                                           start=(ch == 0), stop=(ch == 7)),
                                  reads=(wname, xbk), writes=(pk,))
                    ev = RG[:, 8192 + (st4 & 1) * 1024: 8192 + (st4 & 1) * 1024 + 1024]
                    evk = f"p6ev{st4 & 1}"
                    for dh in range(2):
                        sc.op("scalar", lambda s, dh=dh, ev=ev: s.activation(
                            ev[:, dh * 512:(dh + 1) * 512], PS[dh][:, 0:512], AF.Silu),
                            reads=(f"p6pt{dh}",), writes=(evk,))
                    sc.dma("sync", dm[dst][ts0:ts0 + 128, :], ev, dst, rkeys=(evk,))
            for st4 in range(4):
                ts0 = t0 + st4 * 128
                for dh in range(2):
                    ps = PS[dh]
                    pk = f"p6pt{dh}"
                    for ch in range(8):
                        sc.op("tensor", lambda t, ps=ps, ch=ch, dh=dh, xb=xb, st4=st4:
                              t.matmul(ps[:, 0:512],
                                       xb[:, ch * 512 + st4 * 128: ch * 512 + st4 * 128 + 128],
                                       RW1[:, 8192 + ch * D + dh * 512: 8192 + ch * D + dh * 512 + 512],
                                       start=(ch == 0), stop=(ch == 7)),
                              reads=("wk1", xbk), writes=(pk,))
                kt = RG[:, 8192 + (st4 & 1) * 1024: 8192 + (st4 & 1) * 1024 + 1024]
                ktk = f"p6ev{st4 & 1}"
                for dh in range(2):
                    sc.op("scalar", lambda s, dh=dh, kt=kt: s.activation(
                        kt[:, dh * 512:(dh + 1) * 512], PS[dh][:, 0:512], AF.Silu),
                        reads=(f"p6pt{dh}",), writes=(ktk,))
                psb = PS[2]
                for ch in range(8):
                    sc.op("tensor", lambda t, ch=ch, xb=xb, st4=st4, psb=psb:
                          t.matmul(psb[:, 0:8],
                                   xb[:, ch * 512 + st4 * 128: ch * 512 + st4 * 128 + 128],
                                   RW1[:, 26624 + ch * H: 26624 + (ch + 1) * H],
                                   start=(ch == 0), stop=(ch == 7)),
                          reads=("wbeta", xbk), writes=("p6bps",))
                cols = RF[:, 6144 + (st4 & 1) * 32: 6144 + (st4 & 1) * 32 + 32]
                ck = f"p6c{st4 & 1}"
                sc.op("scalar", lambda s, cols=cols, psb=psb: s.activation(
                    cols[:, 0:8], psb[:, 0:8], AF.Sigmoid),
                    reads=("p6bps",), writes=(ck,))
                sc.op("vector", lambda v, cols=cols: v.tensor_scalar_mul(
                    cols[:, 0:8], cols[:, 0:8], -2.0), reads=(ck,), writes=(ck,))
                sq = RF[:, 4096 + (st4 & 1) * 1024: 4096 + (st4 & 1) * 1024 + 1024]
                sqk = f"p6sq{st4 & 1}"
                sc.op("vector", lambda v, kt=kt, sq=sq: v.tensor_mul(sq, kt, kt),
                      reads=(ktk,), writes=(sqk,))
                for hh in range(8):
                    sc.op("vector", lambda v, hh=hh, sq=sq, cols=cols: v.reduce_sum(
                        cols[:, 8 + hh:9 + hh], sq[:, hh * 128:(hh + 1) * 128],
                        axis=mybir.AxisListType.X), reads=(sqk,), writes=(ck,))
                sc.op("scalar", lambda s, cols=cols: s.activation(
                    cols[:, 16:24], cols[:, 8:16], AF.Sqrt, bias=self.RS[:, 0:1]),
                    reads=(ck, "RS_eps6"), writes=(ck,))
                sc.op("vector", lambda v, cols=cols: v.reciprocal(cols[:, 24:32], cols[:, 16:24]),
                      reads=(ck,), writes=(ck,))
                knt = RG[:, 10240 + (st4 & 1) * 1024: 10240 + (st4 & 1) * 1024 + 1024]
                knbt = RG[:, 12288 + (st4 & 1) * 1024: 12288 + (st4 & 1) * 1024 + 1024]
                knk = f"p6kn{st4 & 1}"
                for hh in range(8):
                    sc.op("vector", lambda v, hh=hh, kt=kt, knt=knt, cols=cols: v.tensor_scalar_mul(
                        knt[:, hh * 128:(hh + 1) * 128], kt[:, hh * 128:(hh + 1) * 128],
                        cols[:, 24 + hh:25 + hh]), reads=(ktk, ck), writes=(knk,))
                    sc.op("vector", lambda v, hh=hh, knt=knt, knbt=knbt, cols=cols: v.tensor_scalar_mul(
                        knbt[:, hh * 128:(hh + 1) * 128], knt[:, hh * 128:(hh + 1) * 128],
                        cols[:, hh:hh + 1]), reads=(knk, ck), writes=(knk,))
                sc.dma("sync", dm["kn"][ts0:ts0 + 128, :], knt, "kn", rkeys=(knk,))
                sc.dma("sync", dm["knb"][ts0:ts0 + 128, :], knbt, "knb", rkeys=(knk,))
            ps = PS[3]
            for ch in range(8):
                sc.op("tensor", lambda t, ch=ch, xb=xb, ps=ps: t.matmul(
                    ps[:, 0:512], RW1[:, 24576 + ch * HD: 24576 + (ch + 1) * HD],
                    xb[:, ch * 512:(ch + 1) * 512], start=(ch == 0), stop=(ch == 7)),
                    reads=("wf1", xbk), writes=("p6f1",))
            t1 = RG[:, 14336:14848]
            sc.op("vector", lambda v, t1=t1, ps=ps: v.tensor_copy(t1, ps[:, 0:512]),
                  reads=("p6f1",), writes=("p6t1",))
            for hh in range(8):
                ps2 = PS[4]
                sc.op("tensor", lambda t, hh=hh, t1=t1, ps2=ps2: t.matmul(
                    ps2[:, 0:512], RW1[:, 27648 + hh * 128: 27648 + (hh + 1) * 128], t1,
                    start=True, stop=True),
                    reads=("wf2", "p6t1"), writes=("p6f2",))
                el = RF[:, 0 + (hh & 1) * 1024: 512 + (hh & 1) * 1024]
                ll = RF[:, 512 + (hh & 1) * 1024: 1024 + (hh & 1) * 1024]
                lk = f"p6l{hh & 1}"
                sc.op("scalar", lambda s, el=el, ps2=ps2: s.activation(el, ps2[:, 0:512], AF.Exp, scale=-1.0),
                      reads=("p6f2",), writes=(lk,))
                sc.op("scalar", lambda s, el=el, ll=ll: s.activation(ll, el, AF.Ln, bias=1.0),
                      reads=(lk,), writes=(lk,))
                sc.dma("sync", dm["lsp"][hh * 128:(hh + 1) * 128, t0:t0 + 512], ll,
                       "lsp", rkeys=(lk,))
            ps = PS[5]
            for ch in range(8):
                sc.op("tensor", lambda t, ch=ch, xb=xb, ps=ps: t.matmul(
                    ps[:, 0:512], RW1[:, 25600 + ch * HD: 25600 + (ch + 1) * HD],
                    xb[:, ch * 512:(ch + 1) * 512], start=(ch == 0), stop=(ch == 7)),
                    reads=("wog1", xbk), writes=("p6og",))
            og = RG[:, 14848:15360]
            sc.op("vector", lambda v, og=og, ps=ps: v.tensor_copy(og, ps[:, 0:512]),
                  reads=("p6og",), writes=("p6ogt",))
            sc.dma("sync", dm["og1fm"][:, t0:t0 + 512], og, "og1fm", rkeys=("p6ogt",))
            self.sc.fence()

    def p7_scan(self):
        sc, dm = self.sc, self.dram
        RW1, RW2, RF, RS, RM, PS, PB = self.RW1, self.RW2, self.RF, self.RS, self.RM, self.PS, self.PB
        for hh in range(H):
            sc.op("vector", lambda v, hh=hh: v.memset(RW2[:, hh * 128:(hh + 1) * 128], 0.0),
                  writes=(f"S0q{hh // 4}",))
        for r in range(4):
            sc.dma("gpsimd", RW2[:, 25088 + r * 128: 25216 + r * 128], dm["m_l"][:, :], "WM")
            sc.dma("gpsimd", RW2[:, 25600 + r * 128: 25728 + r * 128], dm["m_lt"][:, :], "WM")
            sc.dma("gpsimd", RW2[0:64, 26112 + r * 128: 26240 + r * 128], dm["m_kbt"][:, :], "WM")
            sc.dma("gpsimd", RW2[:, 26624 + r * 64: 26688 + r * 64], dm["m_aq"][:, :], "WM")
            sc.dma("gpsimd", RW2[0:64, 26880 + r * 64: 26944 + r * 64], dm["m_kq"][:, :], "WM")
        ML4 = RW2[:, 25088:25600]; MLT4 = RW2[:, 25600:26112]
        MKBT4 = RW2[0:64, 26112:26624]; MAQ4 = RW2[:, 26624:26880]; MKQ4 = RW2[0:64, 26880:27136]
        IDE = RM[:, 0:128]
        self.sc.fence()
        for c in range(NCH):
            t0 = c * CT
            for q in range(2):
                P = q * 9216
                W = lambda o, w: RW1[:, P + o: P + o + w]
                kn4, knb4, q4 = W(0, 256), W(256, 256), W(512, 256)
                A4, B4 = W(768, 512), W(1280, 512)
                K4, Q4 = W(1792, 256), W(2048, 256)
                L4, LT4 = W(2304, 512), W(2816, 512)
                KBT4 = W(3328, 512)
                AQ4, KQ4 = W(3840, 256), W(4096, 256)
                X4 = W(4352, 1024)
                PPT4a, PPT4b = W(5376, 512), W(5888, 512)
                At4, Kt4 = W(6400, 512), W(6912, 512)
                Qe4 = W(7424, 256)
                dph4 = W(7680, 512)
                vt4 = W(8192, 512)
                ob4 = W(8704 - 0, 0) if False else None
                Fq = q * 1600
                lt4 = RF[:, Fq: Fq + 256]
                cs4 = RF[:, Fq + 256: Fq + 516]
                gm4 = RF[:, Fq + 516: Fq + 776]
                igm4 = RF[:, Fq + 776: Fq + 1036]
                tmpf4 = RF[:, Fq + 1036: Fq + 1548]
                k_ = lambda n: f"s{n}{q}"
                S0q = RW2[:, q * 512:(q + 1) * 512]
                S0k = f"S0q{q}"
                h0 = q * 4
                for j in range(4):
                    hh = h0 + j
                    sc.dma("sync", kn4[:, j * 64:(j + 1) * 64],
                           dm["kn"][t0:t0 + CT, hh * 128:(hh + 1) * 128],
                           k_("kn"), rkeys=(), transpose=True)
                    sc.dma("sync", knb4[:, j * 64:(j + 1) * 64],
                           dm["knb"][t0:t0 + CT, hh * 128:(hh + 1) * 128],
                           k_("knb"), rkeys=(), transpose=True)
                    sc.dma("sync", q4[:, j * 64:(j + 1) * 64],
                           dm["q1s"][t0:t0 + CT, hh * 128:(hh + 1) * 128],
                           k_("qf"), rkeys=(), transpose=True)
                sc.dma("sync", vt4[0:64, :], dm["v1s"][t0:t0 + CT, h0 * 128:(h0 + 4) * 128],
                       k_("vt"))
                sc.dma("sync", lt4, bass.AP(dm["lsp"], (h0 * 128) * S + t0,
                                            [[S, 128], [128 * S, 4], [1, 64]]),
                       k_("l"))
                sc.op("vector", lambda v, cs4=cs4: v.memset(
                    bass.AP(cs4.tensor, cs4.offset, [[8192, 128], [65, 4], [1, 1]]), 0.0),
                    writes=(k_("cs"),))
                for j in range(4):
                    sc.op("vector", lambda v, cs4=cs4, lt4=lt4, j=j: v.tensor_tensor_scan(
                        cs4[:, j * 65 + 1: j * 65 + 65], lt4[:, j * 64:(j + 1) * 64],
                        RS[:, 512:576], 0.0, ALU.add, ALU.add),
                        reads=(k_("l"), "RS_zero64"), writes=(k_("cs"),))
                sc.op("scalar", lambda s, gm4=gm4, cs4=cs4: s.activation(gm4, cs4, AF.Exp, scale=-1.0),
                      reads=(k_("cs"),), writes=(k_("gm"),))
                sc.op("scalar", lambda s, igm4=igm4, cs4=cs4: s.activation(igm4, cs4, AF.Exp, scale=1.0),
                      reads=(k_("cs"),), writes=(k_("igm"),))
                gprev = lambda g4: bass.AP(g4.tensor, g4.offset, [[8192, 128], [65, 4], [1, 64]])
                gcur = lambda g4: bass.AP(g4.tensor, g4.offset + 1, [[8192, 128], [65, 4], [1, 64]])
                evenv = lambda t4: bass.AP(t4.tensor, t4.offset, [[32768, 128], [128, 4], [2, 64]])
                oddv = lambda t4: bass.AP(t4.tensor, t4.offset + 1, [[32768, 128], [128, 4], [2, 64]])
                v3 = lambda t4, w: bass.AP(t4.tensor, t4.offset, [[32768, 128], [w, 4], [1, w]])
                for dst, s1, s2, ka, kb, ko in (
                        (evenv(A4), v3(knb4, 64), gprev(igm4), k_("knb"), k_("igm"), k_("A")),
                        (oddv(A4), v3(knb4, 64), gcur(igm4), k_("knb"), k_("igm"), k_("A")),
                        (evenv(B4), v3(kn4, 64), gprev(gm4), k_("kn"), k_("gm"), k_("B")),
                        (oddv(B4), v3(kn4, 64), gcur(gm4), k_("kn"), k_("gm"), k_("B")),
                        (K4, kn4, gcur(igm4), k_("kn"), k_("igm"), k_("K")),
                        (Q4, q4, gcur(gm4), k_("qf"), k_("gm"), k_("Q"))):
                    sc.op("vector", lambda v, dst=dst, s1=s1, s2=s2: v.tensor_mul(dst, s1, s2),
                          reads=(ka, kb), writes=(ko,))
                for j in range(4):
                    sc.op("tensor", lambda t, j=j, A4=A4: t.transpose(
                        PB[q][:, (j & 1) * 256:(j & 1) * 256 + 128],
                        A4[:, j * 128:(j + 1) * 128], IDE),
                        reads=(k_("A"), "RM_id"), writes=(f"pb{q}",))
                    sc.op("vector", lambda v, j=j, At4=At4: v.tensor_copy(
                        At4[:, j * 128:(j + 1) * 128],
                        PB[q][:, (j & 1) * 256:(j & 1) * 256 + 128]),
                        reads=(f"pb{q}",), writes=(k_("At"),))
                    sc.op("tensor", lambda t, j=j, B4=B4: t.transpose(
                        PB[q][:, (j & 1) * 256 + 128:(j & 1) * 256 + 256],
                        B4[:, j * 128:(j + 1) * 128], IDE),
                        reads=(k_("B"), "RM_id"), writes=(f"pb{q}",))
                    sc.op("vector", lambda v, j=j, X4=X4: v.tensor_copy(
                        X4[:, j * 256: j * 256 + 128],
                        PB[q][:, (j & 1) * 256 + 128:(j & 1) * 256 + 256]),
                        reads=(f"pb{q}",), writes=(k_("X"),))
                    sc.op("tensor", lambda t, j=j, K4=K4: t.transpose(
                        PB[q][0:64, (j & 1) * 256:(j & 1) * 256 + 128],
                        K4[:, j * 64:(j + 1) * 64], IDE),
                        reads=(k_("K"), "RM_id"), writes=(f"pb{q}",))
                    sc.op("vector", lambda v, j=j, Kt4=Kt4: v.tensor_copy(
                        Kt4[0:64, j * 128:(j + 1) * 128],
                        PB[q][0:64, (j & 1) * 256:(j & 1) * 256 + 128]),
                        reads=(f"pb{q}",), writes=(k_("Kt"),))
                G = PS[q]
                gk = f"psg{q}"
                def gram4(lhs4, rhs4, lw, rw, lk, rk, out4, mask, ok, rows=128, G=G, gk=gk):
                    for j in range(4):
                        po_ = G[0:rows, j * 128: j * 128 + rw] if rw == 128 else G[0:rows, j * 64:(j + 1) * 64]
                        sc.op("tensor", lambda t, po_=po_, la=lhs4[:, j * lw:(j + 1) * lw],
                              ra=rhs4[:, j * rw:(j + 1) * rw]: t.matmul(
                            po_, la, ra, start=True, stop=True),
                            reads=(lk, rk), writes=(gk,))
                    gsl = G[0:rows, 0:512] if rw == 128 else G[0:rows, 0:256]
                    sc.op("vector", lambda v, o_=out4[0:rows, :], gsl=gsl, mask=mask: v.tensor_mul(
                        o_, gsl, mask),
                        reads=(gk,), writes=(ok,))
                gram4(B4, A4, 128, 128, k_("B"), k_("A"), L4, ML4, k_("L"))
                gram4(A4, B4, 128, 128, k_("A"), k_("B"), LT4, MLT4, k_("LT"))
                gram4(K4, B4, 64, 128, k_("K"), k_("B"), KBT4, MKBT4, k_("KBT"), rows=64)
                gram4(A4, Q4, 128, 64, k_("A"), k_("Q"), AQ4, MAQ4, k_("AQ"))
                gram4(K4, Q4, 64, 64, k_("K"), k_("Q"), KQ4, MKQ4, k_("KQ"), rows=64)
                for j in range(4):
                    sc.op("tensor", lambda t, G=G, j=j, KBT4=KBT4, vt4=vt4: t.matmul(
                        G[:, j * 128:(j + 1) * 128], KBT4[0:64, j * 128:(j + 1) * 128],
                        vt4[0:64, j * 128:(j + 1) * 128], start=True, stop=True),
                        reads=(k_("KBT"), k_("vt")), writes=(gk,))
                x4r2 = bass.AP(X4.tensor, X4.offset + 128, [[32768, 128], [256, 4], [1, 128]])
                sc.op("vector", lambda v, x4r2=x4r2, G=G: v.tensor_copy(x4r2, G[:, 0:512]),
                      reads=(gk,), writes=(k_("X"),))
                BA, BB = PS[4 - 2 * q], PS[5 - 2 * q]
                bak, bbk = f"psa{q}", f"psb{q}"
                psl = lambda j: L4[:, j * 128:(j + 1) * 128]
                ptl = lambda j: LT4[:, j * 128:(j + 1) * 128]
                psp = lambda j: RW1[:, P + 5376 + j * 256: P + 5376 + j * 256 + 128]
                ptp = lambda j: RW1[:, P + 5376 + j * 256 + 128: P + 5376 + j * 256 + 256]
                pget, ptget, pck, ptk = psl, ptl, k_("L"), k_("LT")
                kpow = 1
                while kpow <= 64:
                    for j in range(4):
                        bank, bkk = (BA, bak) if j < 2 else (BB, bbk)
                        sc.op("tensor", lambda t, j=j, bank=bank, pt=ptget(j), X4=X4: t.matmul(
                            bank[:, (j & 1) * 256:(j & 1) * 256 + 256],
                            pt, X4[:, j * 256:(j + 1) * 256],
                            start=True, stop=True),
                            reads=(ptk, k_("X")), writes=(bkk,))
                    sc.op("vector", lambda v, X4=X4, BA=BA: v.tensor_add(
                        X4[:, 0:512], X4[:, 0:512], BA[:, 0:512]),
                        reads=(bak,), writes=(k_("X"),))
                    sc.op("vector", lambda v, X4=X4, BB=BB: v.tensor_add(
                        X4[:, 512:1024], X4[:, 512:1024], BB[:, 0:512]),
                        reads=(bbk,), writes=(k_("X"),))
                    if kpow < 64:
                        for j in range(4):
                            bank, bkk = (BA, bak) if j < 2 else (BB, bbk)
                            r0 = (j & 1) * 256
                            sc.op("tensor", lambda t, bank=bank, r0=r0, pt=ptget(j), pc=pget(j): t.matmul(
                                bank[:, r0:r0 + 128], pt, pc, start=True, stop=True),
                                reads=(ptk, pck), writes=(bkk,))
                            sc.op("tensor", lambda t, bank=bank, r0=r0, pt=ptget(j), pc=pget(j): t.matmul(
                                bank[:, r0 + 128:r0 + 256], pc, pt, start=True, stop=True),
                                reads=(ptk, pck), writes=(bkk,))
                        sc.op("vector", lambda v, PPT4a=PPT4a, BA=BA: v.tensor_copy(PPT4a, BA[:, 0:512]),
                              reads=(bak,), writes=(k_("PP"),))
                        sc.op("vector", lambda v, PPT4b=PPT4b, BB=BB: v.tensor_copy(PPT4b, BB[:, 0:512]),
                              reads=(bbk,), writes=(k_("PP"),))
                        pget, ptget, pck, ptk = psp, ptp, k_("PP"), k_("PP")
                    kpow *= 2
                for j in range(4):
                    sc.op("tensor", lambda t, G=G, j=j, X4=X4, AQ4=AQ4: t.matmul(
                        G[:, j * 64:(j + 1) * 64], X4[:, j * 256: j * 256 + 128],
                        AQ4[:, j * 64:(j + 1) * 64], start=True, stop=True),
                        reads=(k_("X"), k_("AQ")), writes=(gk,))
                sc.op("vector", lambda v, Qe4=Qe4, G=G, Q4=Q4: v.tensor_add(Qe4, G[:, 0:256], Q4),
                      reads=(gk, k_("Q")), writes=(k_("Qe"),))
                for j in range(4):
                    sc.op("tensor", lambda t, G=G, j=j, S0q=S0q, Qe4=Qe4: t.matmul(
                        G[:, 256 + j * 64: 256 + (j + 1) * 64], S0q[:, j * 128:(j + 1) * 128],
                        Qe4[:, j * 64:(j + 1) * 64], start=True, stop=(False)),
                        reads=(S0k, k_("Qe")), writes=(gk,))
                    sc.op("tensor", lambda t, G=G, j=j, X4=X4, AQ4=AQ4: t.matmul(
                        G[:, 256 + j * 64: 256 + (j + 1) * 64], X4[:, j * 256 + 128: j * 256 + 256],
                        AQ4[:, j * 64:(j + 1) * 64], start=False, stop=False),
                        reads=(k_("X"), k_("AQ")), writes=(gk,))
                    sc.op("tensor", lambda t, G=G, j=j, vt4=vt4, KQ4=KQ4: t.matmul(
                        G[:, 256 + j * 64: 256 + (j + 1) * 64], vt4[0:64, j * 128:(j + 1) * 128],
                        KQ4[0:64, j * 64:(j + 1) * 64], start=False, stop=True),
                        reads=(k_("vt"), k_("KQ")), writes=(gk,))
                ob4 = RW1[:, P + 8704: P + 8960]
                sc.op("vector", lambda v, ob4=ob4, G=G: v.tensor_copy(ob4, G[:, 256:512]),
                      reads=(gk,), writes=(k_("ob"),))
                sc.dma("sync", bass.AP(dm["o1fm"], (h0 * 128) * S + t0,
                                       [[S, 128], [128 * S, 4], [1, 64]]),
                       ob4, "o1fm", rkeys=(k_("ob"),))
                for j in range(4):
                    sc.op("tensor", lambda t, G=G, j=j, X4=X4, At4=At4: t.matmul(
                        G[:, j * 128:(j + 1) * 128], X4[:, j * 256: j * 256 + 128],
                        At4[:, j * 128:(j + 1) * 128], start=True, stop=True),
                        reads=(k_("X"), k_("At")), writes=(gk,))
                sc.op("vector", lambda v, dph4=dph4, G=G: v.tensor_copy(dph4, G[:, 0:512]),
                      reads=(gk,), writes=(k_("dph"),))
                for j in range(4):
                    sc.op("tensor", lambda t, G=G, j=j, dph4=dph4, S0q=S0q: t.matmul(
                        G[:, j * 128:(j + 1) * 128], dph4[:, j * 128:(j + 1) * 128],
                        S0q[:, j * 128:(j + 1) * 128], start=True, stop=False),
                        reads=(k_("dph"), S0k), writes=(gk,))
                    sc.op("tensor", lambda t, G=G, j=j, At4=At4, X4=X4: t.matmul(
                        G[:, j * 128:(j + 1) * 128], At4[:, j * 128:(j + 1) * 128],
                        X4[:, j * 256 + 128: j * 256 + 256], start=False, stop=False),
                        reads=(k_("At"), k_("X")), writes=(gk,))
                    sc.op("tensor", lambda t, G=G, j=j, Kt4=Kt4, vt4=vt4: t.matmul(
                        G[:, j * 128:(j + 1) * 128], Kt4[0:64, j * 128:(j + 1) * 128],
                        vt4[0:64, j * 128:(j + 1) * 128], start=False, stop=True),
                        reads=(k_("Kt"), k_("vt")), writes=(gk,))
                sc.op("vector", lambda v, tmpf4=tmpf4, S0q=S0q, G=G: v.tensor_add(
                    tmpf4, G[:, 0:512], S0q),
                    reads=(gk, S0k), writes=(k_("tf"),))
                for j in range(4):
                    sc.op("vector", lambda v, j=j, S0q=S0q, tmpf4=tmpf4, gm4=gm4: v.tensor_scalar_mul(
                        S0q[:, j * 128:(j + 1) * 128], tmpf4[:, j * 128:(j + 1) * 128],
                        gm4[:, j * 65 + 64: j * 65 + 65]),
                        reads=(k_("tf"), k_("gm")), writes=(S0k,))
            if (c & 3) == 3:
                self.sc.fence()

    def p8_gate_gn(self):
        sc, dm = self.sc, self.dram
        RG, RW1, RS, RM, PS = self.RG, self.RW1, self.RS, self.RM, self.PS
        sc.dma("sync", RW1[:, 0:1024], dm["c_wog2"][:, :], "wog2", rkeys=("c_wog2",))
        for hh in range(H):
            for tt in range(8):
                t0 = tt * 512
                par = tt & 1
                Pb = par * 4096
                ot = RG[:, Pb: Pb + 512]
                og = RG[:, Pb + 512: Pb + 1024]
                g_ = RG[:, Pb + 1024: Pb + 1536]
                y_ = RG[:, Pb + 1536: Pb + 2048]
                yn = RG[:, Pb + 2048: Pb + 2560]
                y2 = RG[:, Pb + 2560: Pb + 3072]
                k_ = lambda n: f"g{n}{par}"
                sc.dma("sync", ot, dm["o1fm"][hh * 128:(hh + 1) * 128, t0:t0 + 512],
                       k_("o"), rkeys=())
                sc.dma("sync", og, dm["og1fm"][:, t0:t0 + 512], k_("og"), rkeys=())
                ps = PS[par]
                pk = f"ps{par}"
                sc.op("tensor", lambda t, ps=ps, og=og, hh=hh: t.matmul(
                    ps[:, 0:512], RW1[:, hh * 128:(hh + 1) * 128], og,
                    start=True, stop=True),
                    reads=("wog2", k_("og")), writes=(pk,))
                sc.op("scalar", lambda s, g_=g_, ps=ps: s.activation(g_, ps[:, 0:512], AF.Sigmoid),
                      reads=(pk,), writes=(k_("g"),))
                sc.op("vector", lambda v, y_=y_, ot=ot, g_=g_: v.tensor_mul(y_, ot, g_),
                      reads=(k_("o"), k_("g")), writes=(k_("y"),))
                sc.op("vector", lambda v, y2=y2, y_=y_: v.tensor_mul(y2, y_, y_),
                      reads=(k_("y"),), writes=(k_("y2"),))
                ps2 = PS[2 + par]
                pk2 = f"ps{2 + par}"
                sc.op("tensor", lambda t, ps2=ps2, y2=y2: t.matmul(
                    ps2[0:1, 0:512], RM[:, 1026:1027], y2, start=True, stop=True),
                    reads=("RM_onescol", k_("y2")), writes=(pk2,))
                rs_ = RS[:, 128 + par * 16: 128 + par * 16 + 8]
                sc.op("scalar", lambda s, rs_=rs_, ps2=ps2: s.activation(
                    rs_[0:1, 0:4].rearrange("a (b c) -> a (b c)") if False else rs_[0:1, 0:1],
                    ps2[0:1, 0:1], AF.Sqrt, bias=self.RS[0:1, 0:1], scale=1.0 / HD) if False else None,
                    reads=(), writes=()) if False else None
                rsr = RG[:, Pb + 3072: Pb + 3584]
                sc.op("scalar", lambda s, rsr=rsr, ps2=ps2: s.activation(
                    rsr[0:1, 0:512], ps2[0:1, 0:512], AF.Sqrt, bias=self.RS[0:1, 0:1],
                    scale=1.0 / HD),
                    reads=(pk2, "RS_eps6"), writes=(k_("rs"),))
                sc.op("vector", lambda v, rsr=rsr: v.reciprocal(rsr[0:1, 0:512], rsr[0:1, 0:512]),
                      reads=(k_("rs"),), writes=(k_("rs"),))
                sc.op("tensor", lambda t, ps2=ps2, rsr=rsr: t.matmul(
                    ps2[:, 0:512], RM[0:1, 896:1024], rsr[0:1, 0:512],
                    start=True, stop=True),
                    reads=("RM_ones", k_("rs")), writes=(pk2,))
                sc.op("vector", lambda v, yn=yn, y_=y_, ps2=ps2: v.tensor_mul(
                    yn, y_, ps2[:, 0:512]),
                    reads=(k_("y"), pk2), writes=(k_("yn"),))
                sc.op("vector", lambda v, yn=yn, hh=hh: v.tensor_scalar_mul(
                    yn, yn, RS[:, 8 + hh:9 + hh]),
                    reads=(k_("yn"), "RS_nw1"), writes=(k_("yn"),))
                sc.dma("sync", dm["yn2fm"][hh * 128:(hh + 1) * 128, t0:t0 + 512], yn,
                       "yn2fm", rkeys=(k_("yn"),))
            self.sc.fence()

    def p9_woproj(self):
        sc, dm = self.sc, self.dram
        RG, RW1, PS = self.RG, self.RW1, self.PS
        self.load_w_lhsT("c_L1_attn_Wo", RW1, 0, D, "wo1")
        for tt in range(8):
            t0 = tt * 512
            par = tt & 1
            Pb = par * 4096
            yt = RG[:, Pb: Pb + 4096]
            yk = f"p9y{par}"
            for hh in range(8):
                sc.dma("sync", yt[:, hh * 512:(hh + 1) * 512],
                       dm["yn2fm"][hh * 128:(hh + 1) * 128, t0:t0 + 512], yk, rkeys=())
            for st4 in range(4):
                ts0 = t0 + st4 * 128
                for dh in range(2):
                    ps = PS[dh]
                    pk = f"ps{dh}"
                    for hh in range(8):
                        sc.op("tensor", lambda t, ps=ps, hh=hh, dh=dh, st4=st4, yt=yt:
                              t.matmul(ps[:, 0:512],
                                       yt[:, hh * 512 + st4 * 128: hh * 512 + st4 * 128 + 128],
                                       RW1[:, hh * D + dh * 512: hh * D + dh * 512 + 512],
                                       start=(hh == 0), stop=(hh == 7)),
                              reads=("wo1", yk), writes=(pk,))
                ev = RG[:, 8192 + (st4 & 1) * 1024: 8192 + (st4 & 1) * 1024 + 1024]
                evk = f"p9e{st4 & 1}"
                for dh in range(2):
                    sc.op("vector", lambda v, dh=dh, ev=ev: v.tensor_copy(
                        ev[:, dh * 512:(dh + 1) * 512], PS[dh][:, 0:512]),
                        reads=(f"ps{dh}",), writes=(evk,))
                sc.dma("sync", dm["l1a"][ts0:ts0 + 128, :], ev, "l1a", rkeys=(evk,))
            self.sc.fence()

    def p10_l1ffn(self):
        sc, dm = self.sc, self.dram
        RG, RW1, RW2, RF, RS, RM, PS = self.RG, self.RW1, self.RW2, self.RF, self.RS, self.RM, self.PS
        sc.dma("sync", RS[:, 96:104], bass.AP(dm["L1_ffn_norm_w"], 0, [[1, 128], [128, 8]]),
               "RS_nwf", allow_slow_non_contiguous=True)
        sc.op("gpsimd", lambda g: g.iota(RS[:, 80:88], [[1, 8]], base=0,
                                         channel_multiplier=0,
                                         allow_small_or_imprecise_dtypes=True),
              writes=("RS_iota",))
        self.sc.fence()
        for it in range(S // 128):
            t0 = it * 128
            par = it & 1
            xt = RF[:, par * 1024: par * 1024 + 1024]
            xk = f"ax{par}"
            sq = RF[:, 2048 + par * 1024: 3072 + par * 1024]
            cols = RS[:, 128 + par * 8: 136 + par * 8]
            ck = f"ac{par}"
            sc.dma("gpsimd", xt, dm["xn1"][t0:t0 + 128, :], xk, rkeys=())
            sc.op("vector", lambda v, sq=sq, xt=xt: v.tensor_mul(sq, xt, xt),
                  reads=(xk,), writes=(f"asq{par}",))
            sc.op("vector", lambda v, cols=cols, sq=sq: v.reduce_sum(
                cols[:, 0:1], sq, axis=mybir.AxisListType.X),
                reads=(f"asq{par}",), writes=(ck,))
            sc.op("scalar", lambda s, cols=cols: s.activation(
                cols[:, 1:2], cols[:, 0:1], AF.Sqrt, bias=self.RS[:, 0:1], scale=1.0 / D),
                reads=(ck, "RS_eps6"), writes=(ck,))
            sc.op("vector", lambda v, cols=cols: v.reciprocal(cols[:, 2:3], cols[:, 1:2]),
                  reads=(ck,), writes=(ck,))
            ot = RG[:, par * 1024: par * 1024 + 1024]
            ok = f"ao{par}"
            sc.op("vector", lambda v, ot=ot, xt=xt, cols=cols: v.tensor_scalar_mul(
                ot, xt, cols[:, 2:3]),
                reads=(xk, ck), writes=(ok,))
            sc.dma("sync", dm["trms"][t0:t0 + 128, :], ot, "trms", rkeys=(ok,))
        self.sc.fence()
        gt = RW1[:, 0:64]
        sc.dma("sync", gt[:, 0:8], dm["c_gate_wT"][0:128, :], "gw", rkeys=("c_gate_wT",))
        for ch in range(1, 8):
            sc.dma("sync", gt[:, ch * 8:(ch + 1) * 8], dm["c_gate_wT"][ch * 128:(ch + 1) * 128, :],
                   "gw", rkeys=("c_gate_wT",))
        for ch in range(8):
            sc.op("vector", lambda v, ch=ch, gt=gt: v.tensor_scalar_mul(
                gt[:, ch * 8:(ch + 1) * 8], gt[:, ch * 8:(ch + 1) * 8], RS[:, 96 + ch:97 + ch]),
                reads=("gw", "RS_nwf"), writes=("gw",))
        self.sc.fence()
        for tt in range(8):
            t0 = tt * 512
            par = tt & 1
            tf = RG[:, par * 4096: par * 4096 + 4096]
            tfk = f"btf{par}"
            for ch in range(8):
                sc.dma("sync", tf[:, ch * 512:(ch + 1) * 512],
                       dm["trms"][t0:t0 + 512, ch * 128:(ch + 1) * 128],
                       tfk, rkeys=("trms",), transpose=True)
            for st4 in range(4):
                ts0 = t0 + st4 * 128
                ps = PS[par]
                pk = f"ps{par}"
                for ch in range(8):
                    sc.op("tensor", lambda t, ps=ps, ch=ch, st4=st4, tf=tf: t.matmul(
                        ps[:, 0:8], tf[:, ch * 512 + st4 * 128: ch * 512 + st4 * 128 + 128],
                        RW1[:, ch * 8:(ch + 1) * 8], start=(ch == 0), stop=(ch == 7)),
                        reads=("gw", tfk), writes=(pk,))
                w_ = RF[:, 4096 + par * 64: 4096 + par * 64 + 64]
                wk = f"bw{par}"
                sc.op("vector", lambda v, w_=w_, ps=ps: v.reduce_max(
                    w_[:, 8:9], ps[:, 0:8], axis=mybir.AxisListType.X, negate=True),
                    reads=(pk,), writes=(wk,))
                sc.op("scalar", lambda s, w_=w_, ps=ps: s.activation(
                    w_[:, 0:8], ps[:, 0:8], AF.Exp, bias=w_[:, 8:9]),
                    reads=(pk, wk), writes=(wk,))
                sc.op("vector", lambda v, w_=w_: v.reduce_sum(
                    w_[:, 9:10], w_[:, 0:8], axis=mybir.AxisListType.X),
                    reads=(wk,), writes=(wk,))
                sc.op("vector", lambda v, w_=w_: v.reciprocal(w_[:, 10:11], w_[:, 9:10]),
                      reads=(wk,), writes=(wk,))
                sc.op("vector", lambda v, w_=w_: v.tensor_scalar_mul(
                    w_[:, 0:8], w_[:, 0:8], w_[:, 10:11]),
                    reads=(wk,), writes=(wk,))
                sc.op("vector", lambda v, w_=w_: v.max(w_[:, 16:24], w_[:, 0:8]),
                      reads=(wk,), writes=(wk,))
                idx = self.RIDX[:, par * 8: par * 8 + 8]
                sc.op("vector", lambda v, w_=w_, idx=idx: v.max_index(
                    idx, w_[:, 16:24], w_[:, 0:8]),
                    reads=(wk,), writes=(f"bidx{par}",))
                sc.op("vector", lambda v, w_=w_: v.tensor_add(w_[:, 11:12], w_[:, 16:17], w_[:, 17:18]),
                      reads=(wk,), writes=(wk,))
                sc.op("vector", lambda v, w_=w_: v.reciprocal(w_[:, 12:13], w_[:, 11:12]),
                      reads=(wk,), writes=(wk,))
                sc.op("vector", lambda v, w_=w_: v.tensor_mul(w_[:, 13:14], w_[:, 16:17], w_[:, 12:13]),
                      reads=(wk,), writes=(wk,))
                sc.op("vector", lambda v, w_=w_: v.tensor_mul(w_[:, 14:15], w_[:, 17:18], w_[:, 12:13]),
                      reads=(wk,), writes=(wk,))
                sc.op("vector", lambda v, idx=idx, w_=w_: v.tensor_copy(w_[:, 24:26], idx[:, 0:2]),
                      reads=(f"bidx{par}",), writes=(wk,))
                oh = RF[:, 4352 + par * 32: 4352 + par * 32 + 16]
                sc.op("vector", lambda v, oh=oh, w_=w_: v.tensor_single_scalar(
                    oh[:, 0:8], RS[:, 80:88], w_[:, 24:25], ALU.is_equal),
                    reads=("RS_iota", wk), writes=(f"boh{par}",))
                sc.op("vector", lambda v, oh=oh, w_=w_: v.tensor_single_scalar(
                    oh[:, 8:16], RS[:, 80:88], w_[:, 25:26], ALU.is_equal),
                    reads=("RS_iota", wk), writes=(f"boh{par}",))
                sc.op("vector", lambda v, oh=oh, w_=w_: v.tensor_scalar_mul(
                    oh[:, 0:8], oh[:, 0:8], w_[:, 13:14]),
                    reads=(f"boh{par}", wk), writes=(f"boh{par}",))
                sc.op("vector", lambda v, oh=oh, w_=w_: v.scalar_tensor_tensor(
                    oh[:, 0:8], oh[:, 8:16], w_[:, 14:15], oh[:, 0:8], ALU.mult, ALU.add),
                    reads=(f"boh{par}", wk), writes=(f"boh{par}",))
                ohb = RG[:, 8192 + par * 128: 8192 + par * 128 + 8]
                sc.op("vector", lambda v, ohb=ohb, oh=oh: v.tensor_copy(ohb, oh[:, 0:8]),
                      reads=(f"boh{par}",), writes=(f"bohb{par}",))
                sc.op("tensor", lambda t, ohb=ohb: t.transpose(
                    self.PB[0][0:8, 0:128], ohb, RM[:, 0:128]),
                    reads=(f"bohb{par}", "RM_id"), writes=("pb0",))
                wr = RG[:, 8448 + par * 128: 8448 + par * 128 + 128]
                sc.op("vector", lambda v, wr=wr: v.tensor_copy(wr[0:8, :], self.PB[0][0:8, 0:128]),
                      reads=("pb0",), writes=(f"bwr{par}",))
                sc.dma("gpsimd", dm["wrow"][:, ts0:ts0 + 128], wr[0:8, :], "wrow",
                       rkeys=(f"bwr{par}",))
            self.sc.fence()
        for ipass, (wname, dst) in enumerate((("c_w1T", "base1"), ("c_w3T", "base3"))):
            for ch in range(8):
                sc.dma("sync", RW1[:, ch * 4096:(ch + 1) * 4096],
                       dm[wname][ch * 128:(ch + 1) * 128, :], "wbase", rkeys=(wname,))
            for ch in range(8):
                sc.op("vector", lambda v, ch=ch: v.tensor_scalar_mul(
                    RW1[:, ch * 4096:(ch + 1) * 4096], RW1[:, ch * 4096:(ch + 1) * 4096],
                    RS[:, 96 + ch:97 + ch]),
                    reads=("wbase", "RS_nwf"), writes=("wbase",))
            self.sc.fence()
            for tt in range(8):
                t0 = tt * 512
                par = tt & 1
                tf = RG[:, par * 4096: par * 4096 + 4096]
                tfk = f"ctf{par}"
                for ch in range(8):
                    sc.dma("sync", tf[:, ch * 512:(ch + 1) * 512],
                           dm["trms"][t0:t0 + 512, ch * 128:(ch + 1) * 128],
                           tfk, rkeys=("trms",), transpose=True)
                for ft in range(32):
                    ps = PS[ft & 1]
                    pk = f"ps{ft & 1}"
                    for ch in range(8):
                        sc.op("tensor", lambda t, ps=ps, ch=ch, ft=ft, tf=tf: t.matmul(
                            ps[:, 0:512],
                            RW1[:, ch * 4096 + ft * 128: ch * 4096 + ft * 128 + 128],
                            tf[:, ch * 512:(ch + 1) * 512], start=(ch == 0), stop=(ch == 7)),
                            reads=("wbase", tfk), writes=(pk,))
                    ev = RG[:, 8192 + (ft & 1) * 512: 8192 + (ft & 1) * 512 + 512]
                    evk = f"cev{ft & 1}"
                    sc.op("vector", lambda v, ev=ev, ps=ps: v.tensor_copy(ev, ps[:, 0:512]),
                          reads=(pk,), writes=(evk,))
                    sc.dma("sync", dm[dst][ft * 128:(ft + 1) * 128, t0:t0 + 512], ev,
                           dst, rkeys=(evk,))
                self.sc.fence()
        for tt in range(8):
            t0 = tt * 512
            tf = RW2[:, 16384:20480]
            for ch in range(8):
                sc.dma("sync", tf[:, ch * 512:(ch + 1) * 512],
                       dm["trms"][t0:t0 + 512, ch * 128:(ch + 1) * 128],
                       "dtf", rkeys=("trms",), transpose=True)
            sc.op("vector", lambda v: v.memset(RW2[:, 0:8192], 0.0),
                  writes=("dmix0", "dmix1", "dmix2", "dmix3"))
            sc.op("gpsimd", lambda g: g.memset(RW2[:, 8192:16384], 0.0),
                  writes=("dmix0", "dmix1", "dmix2", "dmix3"))
            for bname, off in (("base1", 0), ("base3", 16384)):
                for ft in range(32):
                    sc.dma("sync", RW1[:, off + ft * 512: off + (ft + 1) * 512],
                           dm[bname][ft * 128:(ft + 1) * 128, t0:t0 + 512],
                           "dbase", rkeys=(bname,))
            wr8 = self.RW2[:, 20480:24576]
            for e_ in range(H):
                sc.dma("gpsimd", wr8[0:1, e_ * 512:(e_ + 1) * 512],
                       dm["wrow"][e_, t0:t0 + 512], "dwr", rkeys=("wrow",))
            self.sc.fence()
            for e in range(H):
                for uu, aname in ((0, "c_a1T"), (1, "c_a3T")):
                    at = RG[:, uu * 128: uu * 128 + 128]
                    ak = f"da{uu}"
                    sc.dma("sync", at,
                           bass.AP(dm["c_" + aname[2:]] if False else dm[aname],
                                   e * 16, [[128, 128], [128 * 128, 8], [1, 16]]),
                           ak, rkeys=(aname,))
                    for ch in range(8):
                        sc.op("vector", lambda v, at=at, ch=ch: v.tensor_scalar_mul(
                            at[:, ch * 16:(ch + 1) * 16], at[:, ch * 16:(ch + 1) * 16],
                            RS[:, 96 + ch:97 + ch]),
                            reads=(ak, "RS_nwf"), writes=(ak,))
                    ps = PS[4 + uu]
                    pk = f"ps{4 + uu}"
                    for ch in range(8):
                        sc.op("tensor", lambda t, ps=ps, at=at, ch=ch, tf=tf: t.matmul(
                            ps[0:16, 0:512], at[:, ch * 16:(ch + 1) * 16],
                            tf[:, ch * 512:(ch + 1) * 512], start=(ch == 0), stop=(ch == 7)),
                            reads=(ak, "dtf"), writes=(pk,))
                    us = RG[:, 256 + uu * 512: 256 + uu * 512 + 512]
                    sc.op("scalar", lambda s, us=us, ps=ps: s.activation(
                        us[0:16, :], ps[0:16, 0:512], AF.Copy, scale=2.0),
                        reads=(pk,), writes=(f"dus{uu}",))
                for uu, bname in ((0, "c_b1T"), (1, "c_b3T")):
                    bt = RG[:, 1280 + uu * 4096: 1280 + uu * 4096 + 4096]
                    sc.dma("sync", bt[0:16, :], dm[bname][e, :, :], f"dbt{uu}", rkeys=(bname,))
                sc.op("tensor", lambda t, e=e: t.matmul(
                    PS[3][:, 0:512], RM[0:1, 896:1024],
                    self.RW2[0:1, 20480 + e * 512: 20480 + (e + 1) * 512],
                    start=True, stop=True),
                    reads=("RM_ones", "dwr"), writes=("ps3",))
                wbs = self.RW2[:, 24576:25088]
                sc.op("vector", lambda v, wbs=wbs: v.tensor_copy(wbs, PS[3][:, 0:512]),
                      reads=("ps3",), writes=("dwbs",))
                a2e = RG[:, 12544 - 512 + 512: 13056] if False else RG[:, 12544:13056]
                sc.dma("sync", a2e,
                       bass.AP(dm["c_a2T"], e * FF * 16, [[16, 128], [128 * 16, 32], [1, 16]]),
                       "da2", rkeys=("c_a2T",))
                for ft in range(32):
                    fp = ft & 1
                    psa = PS[fp]
                    pka = f"ps{fp}"
                    sil = RG[:, 9472 + fp * 512: 9472 + fp * 512 + 512]
                    act = RG[:, 10496 + fp * 512: 10496 + fp * 512 + 512]
                    tmp = RG[:, 11520 + fp * 512: 11520 + fp * 512 + 512]
                    sc.op("tensor", lambda t, psa=psa, ft=ft: t.matmul(
                        psa[:, 0:512], self.RG[0:16, 1280 + ft * 128: 1280 + ft * 128 + 128],
                        self.RG[0:16, 256:768], start=True, stop=True),
                        reads=("dbt0", "dus0"), writes=(pka,))
                    sc.op("vector", lambda v, psa=psa, ft=ft: v.tensor_add(
                        psa[:, 0:512], psa[:, 0:512], RW1[:, ft * 512:(ft + 1) * 512]),
                        reads=(pka, "dbase"), writes=(pka,))
                    sc.op("scalar", lambda s, psa=psa, sil=sil: s.activation(
                        sil, psa[:, 0:512], AF.Silu),
                        reads=(pka,), writes=(f"dsil{fp}",))
                    psb = PS[2] if fp == 0 else PS[3]
                    pkb = "ps2" if fp == 0 else "ps3"
                    sc.op("tensor", lambda t, psb=psb, ft=ft: t.matmul(
                        psb[:, 0:512], self.RG[0:16, 5376 + ft * 128: 5376 + ft * 128 + 128],
                        self.RG[0:16, 768:1280], start=True, stop=True),
                        reads=("dbt1", "dus1"), writes=(pkb,))
                    sc.op("vector", lambda v, psb=psb, ft=ft: v.tensor_add(
                        psb[:, 0:512], psb[:, 0:512], RW1[:, 16384 + ft * 512: 16384 + (ft + 1) * 512]),
                        reads=(pkb, "dbase"), writes=(pkb,))
                    sc.op("vector", lambda v, act=act, sil=sil, psb=psb: v.tensor_mul(
                        act, sil, psb[:, 0:512]),
                        reads=(f"dsil{fp}", pkb), writes=(f"dact{fp}",))
                    sc.op("tensor", lambda t, act=act, ft=ft: t.matmul(
                        PS[5][0:16, 0:512], self.RG[:, 12544 + ft * 16: 12544 + ft * 16 + 16],
                        act, start=(ft == 0), stop=(ft == 31)),
                        reads=("da2", f"dact{fp}"), writes=("ps5",))
                    sc.op("vector", lambda v, tmp=tmp, act=act: v.tensor_mul(
                        tmp, act, self.RW2[:, 24576:25088]),
                        reads=(f"dact{fp}", "dwbs"), writes=(f"dtmp{fp}",))
                    eng = "gpsimd" if fp else "vector"
                    sc.op(eng, lambda g, tmp=tmp, ft=ft: g.tensor_add(
                        RW2[:, ft * 512:(ft + 1) * 512], RW2[:, ft * 512:(ft + 1) * 512], tmp),
                        reads=(f"dtmp{fp}", f"dmix{ft & 3}"), writes=(f"dmix{ft & 3}",))
                u2s = RG[:, 12608:13120]
                u2c = RG[:, 13120:13632]
                sc.op("scalar", lambda s, u2s=u2s: s.activation(
                    u2s[0:16, :], PS[5][0:16, 0:512], AF.Copy, scale=2.0),
                    reads=("ps5",), writes=("du2s",))
                sc.op("vector", lambda v, u2c=u2c, u2s=u2s: v.tensor_mul(
                    u2c[0:16, :], u2s[0:16, :], self.RW2[0:16, 24576:25088]),
                    reads=("du2s", "dwbs"), writes=("du2c",))
                sc.dma("sync", dm["u2w"][e, :, t0:t0 + 512], u2c[0:16, :], "u2w",
                       rkeys=("du2c",))
            for ft in range(32):
                sc.dma("sync", dm["mixedfm"][ft * 128:(ft + 1) * 128, t0:t0 + 512],
                       RW2[:, ft * 512:(ft + 1) * 512], "mixedfm", rkeys=(f"dmix{ft & 3}",))
            self.sc.fence()
        for ch in range(32):
            sc.dma("sync", RW1[:, ch * 1024:(ch + 1) * 1024],
                   dm["c_w2T"][ch * 128:(ch + 1) * 128, :], "w2r", rkeys=("c_w2T",))
        for e in range(H):
            sc.dma("sync", RG[0:16, e * 512: e * 512 + 512],
                   dm["c_b2T"][e, :, 0:512], "b2ra", rkeys=("c_b2T",))
            sc.dma("sync", RG[0:16, 4096 + e * 512: 4096 + e * 512 + 512],
                   dm["c_b2T"][e, :, 512:1024], "b2rb", rkeys=("c_b2T",))
        self.sc.fence()
        for tt in range(8):
            t0 = tt * 512
            for ft in range(32):
                sc.dma("sync", RW2[:, ft * 512:(ft + 1) * 512],
                       dm["mixedfm"][ft * 128:(ft + 1) * 128, t0:t0 + 512],
                       "emx", rkeys=("mixedfm",))
            self.sc.fence()
            for st4 in range(4):
                ts0 = t0 + st4 * 128
                sp = st4 & 1
                x1t = RF[:, sp * 1024: sp * 1024 + 1024]
                sc.dma("sync", x1t, dm["x1"][ts0:ts0 + 128, :], f"ex1{sp}", rkeys=())
                l1t = RG[:, 8192 + sp * 1024: 9216 + sp * 1024]
                sc.dma("sync", l1t, dm["l1a"][ts0:ts0 + 128, :], f"el1{sp}", rkeys=())
                u2b = RG[:, 10240 + sp * 1024: 11264 + sp * 1024]
                for e in range(H):
                    sc.dma("gpsimd", u2b[0:16, e * 128:(e + 1) * 128],
                           dm["u2w"][e, :, ts0:ts0 + 128], f"eu2{sp}", rkeys=("u2w",))
                for dh in range(2):
                    ps = PS[2 * sp + dh]
                    pk = f"ps{2 * sp + dh}"
                    for ft in range(32):
                        sc.op("tensor", lambda t, ps=ps, ft=ft, dh=dh, st4=st4: t.matmul(
                            ps[:, 0:512],
                            RW2[:, ft * 512 + st4 * 128: ft * 512 + st4 * 128 + 128],
                            RW1[:, ft * 1024 + dh * 512: ft * 1024 + dh * 512 + 512],
                            start=(ft == 0), stop=False),
                            reads=("w2r", "emx"), writes=(pk,))
                    for e in range(H):
                        sc.op("tensor", lambda t, ps=ps, e=e, dh=dh, u2b=u2b: t.matmul(
                            ps[:, 0:512], u2b[0:16, e * 128:(e + 1) * 128],
                            self.RG[0:16, dh * 4096 + e * 512: dh * 4096 + e * 512 + 512],
                            start=False, stop=(e == H - 1)),
                            reads=(f"eu2{sp}", "b2ra", "b2rb"), writes=(pk,))
                    sc.op("vector", lambda v, ps=ps, dh=dh, x1t=x1t: v.tensor_add(
                        x1t[:, dh * 512:(dh + 1) * 512], ps[:, 0:512],
                        x1t[:, dh * 512:(dh + 1) * 512]),
                        reads=(pk, f"ex1{sp}"), writes=(f"ex1{sp}",))
                    sc.op("vector", lambda v, dh=dh, x1t=x1t, l1t=l1t: v.tensor_add(
                        x1t[:, dh * 512:(dh + 1) * 512], x1t[:, dh * 512:(dh + 1) * 512],
                        l1t[:, dh * 512:(dh + 1) * 512]),
                        reads=(f"el1{sp}",), writes=(f"ex1{sp}",))
                sc.dma("sync", dm["out"][ts0:ts0 + 128, :], x1t, "outw", rkeys=(f"ex1{sp}",))


def build_program(upto=99, taps=()):
    b = B(upto=upto, taps=taps)
    b.declare()
    phases = [
        (0, b.p0_prep),
        (1, b.p2_ln0),
        (2, b.p3_l0proj),
        (3, b.p3b_keymask),
        (4, b.p4_l0attn),
        (5, b.p1_fold),
        (6, b.p5_l0ffn),
    ]
    more = [
        (7, getattr(b, "p6_l1proj", None)),
        (8, getattr(b, "p7_scan", None)),
        (9, getattr(b, "p8_gate_gn", None)),
        (10, getattr(b, "p9_woproj", None)),
        (11, getattr(b, "p10_l1ffn", None)),
    ]
    for i, fn in phases + more:
        if fn is None or i > upto:
            continue
        fn()
        b.sc.fence()
    with b.nc.allow_low_precision(
            "bf16 operands feeding bf16 matmuls; the only bf16 accumulation is "
            "the top-2 expert mixture where 6 of 8 addends are exactly zero"):
        with b.nc.Block() as block:
            b.sc.analyze()
            b.sc.build(b.nc, block)
    return b


def make_in_map(inputs, aux, core):
    T = np.ascontiguousarray
    im = {
        "x": np.ascontiguousarray(np.asarray(inputs["L0_attn_x"])[core]).astype(np.float32),
        "L0_ffn_W_expT": T(np.asarray(inputs["L0_ffn_W_exp"]).T),
        "L1_ffn_w1T": T(np.asarray(inputs["L1_ffn_w1"]).T),
        "L1_ffn_w3T": T(np.asarray(inputs["L1_ffn_w3"]).T),
        "L1_ffn_w2T": T(np.asarray(inputs["L1_ffn_w2"]).T),
        "L1_ffn_gate_wT": T(np.asarray(inputs["L1_ffn_gate_w"]).T),
        "L1_ffn_a1T": T(np.asarray(inputs["L1_ffn_a1"]).transpose(2, 0, 1).reshape(D, -1)),
        "L1_ffn_a3T": T(np.asarray(inputs["L1_ffn_a3"]).transpose(2, 0, 1).reshape(D, -1)),
        "L1_ffn_b1T": T(np.asarray(inputs["L1_ffn_b1"]).transpose(0, 2, 1)),
        "L1_ffn_b3T": T(np.asarray(inputs["L1_ffn_b3"]).transpose(0, 2, 1)),
        "L1_ffn_a2T": T(np.asarray(inputs["L1_ffn_a2"]).transpose(0, 2, 1)),
        "L1_ffn_b2T": T(np.asarray(inputs["L1_ffn_b2"]).transpose(0, 2, 1)),
    }
    for n in ("L0_attn_Wq", "L0_attn_Wk", "L0_attn_Wv", "L0_attn_Wu",
              "L0_ffn_b_exp", "L0_ffn_up_bias", "L0_ffn_W_con", "L0_ffn_b_con",
              "L0_ffn_down_bias", "L1_attn_Wq", "L1_attn_Wk", "L1_attn_Wv",
              "L1_attn_Wo", "L1_attn_Wf1", "L1_attn_Wf2", "L1_attn_Wbeta",
              "L1_attn_Wog1", "L1_attn_Wog2", "L1_attn_norm_w", "L1_ffn_norm_w"):
        im[n] = np.ascontiguousarray(np.asarray(inputs[n]), dtype=np.float32)
    for n in ("r_pad", "FATu", "FAu", "FBTu", "FBu", "FATd", "FAd", "FBTd", "FBd",
              "ident", "antident", "m_tri_kq", "m_l", "m_lt", "m_kbt", "m_aq", "m_kq"):
        im[n] = aux[n]
    return im


def kernel(**inputs):
    aux = make_host_aux(inputs)
    b = build_program()
    in_maps = [make_in_map(inputs, aux, c) for c in range(8)]
    res = run_bass_kernel_spmd(b.nc, in_maps, core_ids=list(range(8)))
    return np.stack([res.results[c]["out"] for c in range(8)], axis=0)

# version 10 (final)
# speedup vs baseline: 1.3096x; 1.0206x over previous
import numpy as np
import ml_dtypes
import concourse.bass as bass
import concourse.mybir as mybir
from concourse.bass_utils import run_bass_kernel_spmd

F32 = mybir.dt.float32
BF16 = mybir.dt.bfloat16
AF = mybir.ActivationFunctionType
ALU = mybir.AluOpType

S = 4096
D = 1024
H = 8
HD = 128
FF = 4096
CT = 64
NCH = S // CT
NEG = -4294967295.0
EPS1 = 1e-5
EPS6 = 1e-6


class Sched:
    ENGINES = ("sync", "scalar", "tensor", "vector", "gpsimd")

    def __init__(self):
        self.ops = []
        self.fences = []

    def op(self, stream, fn, reads=(), writes=(), dma=False):
        assert stream in self.ENGINES
        self.ops.append(dict(stream=stream, fn=fn, reads=tuple(reads),
                             writes=tuple(writes), dma=dma,
                             need_inc=False, waits=None))

    def fence(self):
        self.fences.append(len(self.ops))

    def dma(self, stream, out, in_, wkey, rkeys=(), **kw):
        self.op(stream, lambda e: e.dma_start(out=out, in_=in_, **kw),
                reads=rkeys, writes=(wkey,), dma=True)

    DMA_RING = {"sync": 16, "gpsimd": 8, "scalar": 4}

    def analyze(self):
        fence_set = set(self.fences)
        writers = {}
        readers = {}
        last_by_chan = {}
        fence_deps = {}
        dma_count = {}
        deps = [set() for _ in self.ops]
        for i, o in enumerate(self.ops):
            if i in fence_set:
                fence_deps = dict(last_by_chan)
                writers = {}
                readers = {}
            for (st, dm_, slot), j in fence_deps.items():
                if st != o["stream"] or dm_:
                    deps[i].add(j)
            for k in o["reads"]:
                deps[i].update(writers.get(k, ()))
            for k in o["writes"]:
                deps[i].update(writers.get(k, ()))
                deps[i].update(readers.get(k, ()))
            for k in o["writes"]:
                writers.setdefault(k, []).append(i)
                readers[k] = []
            for k in o["reads"]:
                readers.setdefault(k, []).append(i)
            if o["dma"]:
                st = o["stream"]
                kcnt = dma_count.get(st, 0)
                dma_count[st] = kcnt + 1
                last_by_chan[(st, True, kcnt % self.DMA_RING[st])] = i
            else:
                last_by_chan[(o["stream"], False, 0)] = i
        for i, o in enumerate(self.ops):
            deps[i] = {j for j in deps[i]
                       if not (o["stream"] == "tensor"
                               and self.ops[j]["stream"] == "tensor"
                               and not self.ops[j]["dma"])}
        for i in range(len(self.ops)):
            for j in deps[i]:
                self.ops[j]["need_inc"] = True
        last_of = {}
        for i, o in enumerate(self.ops):
            if o["dma"]:
                o["need_inc"] = True
            last_of[(o["stream"], o["dma"])] = i
        for i in last_of.values():
            self.ops[i]["need_inc"] = True
        counts = {}
        pos = {}
        dma_idx = {}
        self.op_sem = {}
        for i, o in enumerate(self.ops):
            if o["dma"]:
                st = o["stream"]
                r = self.DMA_RING[st]
                k = dma_idx.get(st, 0)
                dma_idx[st] = k + 1
                sem = f"{st}_d{k % r}"
                prev = counts.get(sem, 0)
                o["throttle"] = (sem, prev) if prev > 0 else None
                counts[sem] = prev + 16
                pos[i] = (sem, prev + 16)
                self.op_sem[i] = sem
            else:
                o["throttle"] = None
                if o["need_inc"]:
                    sem = o["stream"] + "_c"
                    counts[sem] = counts.get(sem, 0) + 1
                    pos[i] = (sem, counts[sem])
                    self.op_sem[i] = sem
        known = {s: {} for s in self.ENGINES}
        nwaits = 0
        for i, o in enumerate(self.ops):
            need = {}
            for j in deps[i]:
                assert j in pos, f"dep {j} without inc"
                sem, c = pos[j]
                need[sem] = max(need.get(sem, 0), c)
            if o["throttle"]:
                sem, c = o["throttle"]
                need[sem] = max(need.get(sem, 0), c)
            w = []
            for sem, c in sorted(need.items()):
                if known[o["stream"]].get(sem, 0) < c:
                    w.append((sem, c))
                    known[o["stream"]][sem] = c
            o["waits"] = w
            nwaits += len(w)
        self.sem_totals = dict(counts)
        self.nwaits = nwaits

    def build(self, nc, block):
        sems = {}
        import contextlib
        self._es = contextlib.ExitStack()
        for name in sorted(set(self.sem_totals) | set(s for _, c in [(0, 0)] for s in [])):
            sems[name] = self._es.enter_context(nc.semaphore(name))
        ops = self.ops
        totals = self.sem_totals
        op_sem = self.op_sem

        def body_for(stream):
            def body(eng):
                for i, o in enumerate(ops):
                    if o["stream"] != stream:
                        continue
                    for sem, c in o["waits"]:
                        eng.wait_ge(sems[sem], c)
                    inst = o["fn"](eng)
                    if i in op_sem:
                        inst.then_inc(sems[op_sem[i]], 16 if o["dma"] else 1)
                if stream == "sync":
                    for sem, c in sorted(totals.items()):
                        eng.wait_ge(sems[sem], c)
            return body

        block.sync(body_for("sync"))
        block.vector(body_for("vector"))
        block.scalar(body_for("scalar"))
        block.tensor(body_for("tensor"))
        block.gpsimd(body_for("gpsimd"))


def make_factor_tiles(weights, transpose):
    FA = np.zeros((7, 32, 128, 128), np.float32)
    FB = np.zeros((5, 32, 128, 128), np.float32)
    for s in range(12):
        st = 2 ** s
        w = np.asarray(weights[s]).reshape(FF // (2 * st), st, 2, 2)
        f = np.arange(FF)
        b = f // (2 * st)
        j = f % st
        half = (f // st) % 2
        dcoef = np.where(half == 0, w[b, j, 0, 0], w[b, j, 1, 1])
        ocoef = np.where(half == 0, w[b, j, 0, 1], w[b, j, 1, 0])
        partner = f ^ st
        if transpose:
            ocoef = np.where(half == 0, w[b, j, 1, 0], w[b, j, 0, 1])
        if s < 7:
            c = f // 128
            p = f % 128
            pp = partner % 128
            FA[s, c, p, p] = dcoef
            FA[s, c, p, pp] = ocoef
        else:
            c = f % 32
            p = f // 32
            pp = partner // 32
            FB[s - 7, c, p, p] = dcoef
            FB[s - 7, c, p, pp] = ocoef
    FAT = FA.transpose(0, 1, 3, 2).copy()
    FBT = FB.transpose(0, 1, 3, 2).copy()
    return FA, FB, FAT, FBT


def make_host_aux(inputs):
    aux = {}
    rb = np.asarray(inputs["L0_attn_rel_bias"])
    idx = np.clip(np.arange(-4095, -4095 + 8192) + 2047, 0, 4094)
    aux["r_pad"] = rb[idx, :].T.copy().astype(np.float32)
    FA, FB, FAT, FBT = make_factor_tiles(np.asarray(inputs["L0_ffn_up_weights"]), False)
    aux["FAu"], aux["FBu"], aux["FATu"], aux["FBTu"] = FA, FB, FAT, FBT
    FA, FB, FAT, FBT = make_factor_tiles(np.asarray(inputs["L0_ffn_down_weights"]), True)
    aux["FAd"], aux["FBd"], aux["FATd"], aux["FBTd"] = FA, FB, FAT, FBT
    aux["ident"] = np.eye(128, dtype=np.float32)
    aux["antident"] = np.eye(128, dtype=np.float32)[::-1].copy()
    ii = np.arange(128)
    aux["m_tri_kq"] = (ii[:, None] <= ii[None, :]).astype(np.float32)
    C = 2 * CT
    st = np.arange(C)
    modd = 2 * np.arange(CT) + 1
    aux["m_l"] = (st[:, None] > st[None, :]).astype(np.float32)
    aux["m_lt"] = aux["m_l"].T.copy()
    aux["m_kbt"] = (modd[:, None] < st[None, :]).astype(np.float32)
    aux["m_aq"] = (st[:, None] <= modd[None, :]).astype(np.float32)
    aux["m_kq"] = (modd[:, None] <= modd[None, :]).astype(np.float32)
    return aux


class B:

    def __init__(self, upto=99, taps=()):
        self.upto = upto
        self.taps = set(taps)
        self.nc = bass.Bass("TRN2", target_bir_lowering=False)
        self.sc = Sched()
        self.dram = {}
        self.tap_handles = {}

    def din(self, name, shape, dtype=F32):
        h = self.nc.dram_tensor(name, list(shape), dtype, kind="ExternalInput")
        self.dram[name] = h
        return h

    def dscratch(self, name, shape, dtype=BF16):
        kind = "ExternalOutput" if name in self.taps else "Internal"
        h = self.nc.dram_tensor(name, list(shape), dtype, kind=kind)
        self.dram[name] = h
        return h

    def dout(self, name, shape, dtype=F32):
        h = self.nc.dram_tensor(name, list(shape), dtype, kind="ExternalOutput")
        self.dram[name] = h
        return h

    def tap(self, name, src_ap, shape, key):
        if name not in self.taps:
            return
        h = self.dout("tap_" + name, shape, F32)
        self.sc.dma("sync", h[tuple(slice(0, s) for s in shape)], src_ap,
                    "tap_" + name, rkeys=(key,))

    def declare(self):
        nc, d = self.nc, self.din
        d("x", (S, D))
        for n in ("L0_attn_Wq", "L0_attn_Wk", "L0_attn_Wv", "L0_attn_Wu"):
            d(n, (D, D))
        d("L0_ffn_W_expT", (FF, D))
        d("L0_ffn_b_exp", (FF,))
        d("L0_ffn_up_bias", (FF,))
        d("L0_ffn_W_con", (FF, D))
        d("L0_ffn_b_con", (D,))
        d("L0_ffn_down_bias", (D,))
        for n in ("L1_attn_Wq", "L1_attn_Wk", "L1_attn_Wv", "L1_attn_Wo"):
            d(n, (D, D))
        d("L1_attn_Wf1", (D, HD))
        d("L1_attn_Wf2", (HD, D))
        d("L1_attn_Wbeta", (D, H))
        d("L1_attn_Wog1", (D, HD))
        d("L1_attn_Wog2", (HD, D))
        d("L1_attn_norm_w", (D,))
        d("L1_ffn_norm_w", (D,))
        d("L1_ffn_w1T", (D, FF))
        d("L1_ffn_w3T", (D, FF))
        d("L1_ffn_w2T", (FF, D))
        d("L1_ffn_gate_wT", (D, H))
        d("L1_ffn_a1T", (D, H * 16))
        d("L1_ffn_a3T", (D, H * 16))
        d("L1_ffn_b1T", (H, 16, FF))
        d("L1_ffn_b3T", (H, 16, FF))
        d("L1_ffn_a2T", (H, FF, 16))
        d("L1_ffn_b2T", (H, 16, D))
        d("r_pad", (H, 8192))
        d("FATu", (7, 32, 128, 128)); d("FAu", (7, 32, 128, 128))
        d("FBTu", (5, 32, 128, 128)); d("FBu", (5, 32, 128, 128))
        d("FATd", (7, 32, 128, 128)); d("FAd", (7, 32, 128, 128))
        d("FBTd", (5, 32, 128, 128)); d("FBd", (5, 32, 128, 128))
        d("ident", (128, 128))
        d("antident", (128, 128))
        d("m_tri_kq", (128, 128))
        d("m_l", (128, 128)); d("m_lt", (128, 128))
        d("m_kbt", (64, 128)); d("m_aq", (128, 64)); d("m_kq", (64, 64))
        self.dout("out", (S, D))

        ds = self.dscratch
        for n in ("L0_attn_Wq", "L0_attn_Wk", "L0_attn_Wv", "L0_attn_Wu",
                  "L1_attn_Wq", "L1_attn_Wk", "L1_attn_Wv", "L1_attn_Wo"):
            ds("c_" + n, (D, D))
        ds("c_W_expT", (FF, D)); ds("c_W_con", (FF, D))
        ds("c_w1T", (D, FF)); ds("c_w3T", (D, FF)); ds("c_w2T", (FF, D))
        ds("c_gate_wT", (D, H))
        ds("c_a1T", (D, H * 16)); ds("c_a3T", (D, H * 16))
        ds("c_b1T", (H, 16, FF)); ds("c_b3T", (H, 16, FF))
        ds("c_a2T", (H, FF, 16)); ds("c_b2T", (H, 16, D))
        ds("c_wf1", (D, HD)); ds("c_wf2", (HD, D))
        ds("c_wbeta", (D, H)); ds("c_wog1", (D, HD)); ds("c_wog2", (HD, D))
        for n in ("FATu", "FAu", "FBTu", "FBu", "FATd", "FAd", "FBTd", "FBd"):
            ds("c_" + n, self.dram[n].shape)
        ds("w1pre", (FF, D))
        ds("xn0", (S, D)); ds("xn1", (S, D))
        ds("q0fm", (D, S)); ds("k0fm", (D, S))
        ds("u0s", (S, D)); ds("v0p", (S, D))
        ds("attn0", (S, D))
        ds("x1", (S, D), F32)
        ds("q1s", (S, D)); ds("kn", (S, D)); ds("knb", (S, D)); ds("v1s", (S, D))
        ds("lsp", (D, S), F32)
        ds("og1fm", (HD, S))
        ds("o1fm", (D, S))
        ds("yn2fm", (D, S))
        ds("l1a", (S, D))
        ds("trms", (S, D))
        ds("wrow", (H, S), F32)
        ds("base1", (FF, S)); ds("base3", (FF, S)); ds("mixedfm", (FF, S))
        ds("u2w", (H, 16, S))

        nb = self.nc
        self.RW1 = nb.alloc_sbuf_tensor("RW1", [128, 32768], BF16)
        self.RW2 = nb.alloc_sbuf_tensor("RW2", [128, 32768], BF16)
        self.RG = nb.alloc_sbuf_tensor("RG", [128, 16384], BF16)
        self.RF = nb.alloc_sbuf_tensor("RF", [128, 8192], F32)
        self.RS = nb.alloc_sbuf_tensor("RS", [128, 1024], F32)
        self.RM = nb.alloc_sbuf_tensor("RM", [128, 3072], BF16)
        self.RIDX = nb.alloc_sbuf_tensor("RIDX", [128, 16], mybir.dt.uint32)
        self.PS = [nb.alloc_psum_tensor(f"ps{i}", [128, 512], F32) for i in range(6)]
        self.PB = [nb.alloc_psum_tensor(f"pb{i}", [128, 1024], BF16) for i in range(2)]

    def p0_prep(self):
        sc, dm = self.sc, self.dram
        for n in ("L0_attn_Wq", "L0_attn_Wk", "L0_attn_Wv", "L0_attn_Wu",
                  "L1_attn_Wq", "L1_attn_Wk", "L1_attn_Wv", "L1_attn_Wo"):
            sc.dma("gpsimd", dm["c_" + n][:, :], dm[n][:, :], "c_" + n)
        sc.dma("gpsimd", dm["c_W_expT"][:, :], dm["L0_ffn_W_expT"][:, :], "c_W_expT")
        sc.dma("gpsimd", dm["c_W_con"][:, :], dm["L0_ffn_W_con"][:, :], "c_W_con")
        for a, b_ in (("c_w1T", "L1_ffn_w1T"), ("c_w3T", "L1_ffn_w3T"),
                      ("c_w2T", "L1_ffn_w2T"), ("c_gate_wT", "L1_ffn_gate_wT"),
                      ("c_a1T", "L1_ffn_a1T"), ("c_a3T", "L1_ffn_a3T"),
                      ("c_wf1", "L1_attn_Wf1"), ("c_wf2", "L1_attn_Wf2"),
                      ("c_wbeta", "L1_attn_Wbeta"), ("c_wog1", "L1_attn_Wog1"),
                      ("c_wog2", "L1_attn_Wog2")):
            sc.dma("gpsimd", dm[a][:, :], dm[b_][:, :], a)
        for a, b_ in (("c_b1T", "L1_ffn_b1T"), ("c_b3T", "L1_ffn_b3T"),
                      ("c_a2T", "L1_ffn_a2T"), ("c_b2T", "L1_ffn_b2T")):
            sc.dma("gpsimd", dm[a][:, :, :], dm[b_][:, :, :], a)
        for n in ("FATu", "FAu", "FBTu", "FBu", "FATd", "FAd", "FBTd", "FBd"):
            sc.dma("gpsimd", dm["c_" + n][:, :, :, :], dm[n][:, :, :, :], "c_" + n)
        RM = self.RM
        sc.dma("gpsimd", RM[:, 0:128], dm["ident"][:, :], "RM_id")
        sc.dma("gpsimd", RM[:, 128:256], dm["antident"][:, :], "RM_J")
        sc.dma("gpsimd", RM[:, 256:384], dm["m_tri_kq"][:, :], "RM_tri")
        sc.dma("gpsimd", RM[:, 384:512], dm["m_l"][:, :], "RM_ml")
        sc.dma("gpsimd", RM[:, 512:640], dm["m_lt"][:, :], "RM_mlt")
        sc.dma("gpsimd", RM[0:64, 640:768], dm["m_kbt"][:, :], "RM_mkbt")
        sc.dma("gpsimd", RM[:, 768:832], dm["m_aq"][:, :], "RM_maq")
        sc.dma("gpsimd", RM[0:64, 832:896], dm["m_kq"][:, :], "RM_mkq")
        self.sc.op("vector", lambda v: v.memset(RM[0:1, 896:1024], 1.0), writes=("RM_ones",))
        self.sc.op("vector", lambda v: v.memset(RM[0:1, 1024:1025], 1.0), writes=("RM_one1",))
        RS = self.RS
        self.sc.op("vector", lambda v: v.memset(RS[:, 0:1], EPS6), writes=("RS_eps6",))
        self.sc.op("vector", lambda v: v.memset(RS[:, 1:2], EPS1), writes=("RS_eps1",))
        sc.dma("sync", RS[:, 8:16], bass.AP(dm["L1_attn_norm_w"], 0, [[1, 128], [128, 8]]),
               "RS_nw1", allow_slow_non_contiguous=True)
        self.sc.op("vector", lambda v: v.memset(RS[:, 512:576], 0.0), writes=("RS_zero64",))
        self.sc.op("vector", lambda v: v.memset(self.RM[:, 1026:1027], 1.0), writes=("RM_onescol",))

    def p2_ln0(self):
        sc, dm = self.sc, self.dram
        RF, RS = self.RF, self.RS
        for it in range(S // 128):
            t0 = it * 128
            xk = f"ln0x{it & 1}"
            xt = RF[:, (it & 1) * 1024:(it & 1) * 1024 + 1024]
            sc.dma("sync", xt, dm["x"][t0:t0 + 128, :], xk)
            sq = RF[:, 2048 + (it & 1) * 1024: 3072 + (it & 1) * 1024]
            sqk = f"ln0sq{it & 1}"
            cols = RS[:, 16 + (it & 1) * 8: 24 + (it & 1) * 8]
            ck = f"ln0c{it & 1}"
            sc.op("vector", lambda v, xt=xt, c=cols: v.reduce_sum(c[:, 0:1], xt, axis=mybir.AxisListType.X),
                  reads=(xk,), writes=(ck,))
            sc.op("scalar", lambda s, xt=xt, sq=sq: s.activation(sq, xt, AF.Square),
                  reads=(xk,), writes=(sqk,))
            sc.op("vector", lambda v, sq=sq, c=cols: v.reduce_sum(c[:, 1:2], sq, axis=mybir.AxisListType.X),
                  reads=(sqk,), writes=(ck,))
            sc.op("vector", lambda v, c=cols: v.tensor_scalar_mul(c[:, 2:3], c[:, 0:1], 1.0 / D),
                  reads=(ck,), writes=(ck,))
            sc.op("vector", lambda v, c=cols: v.tensor_mul(c[:, 3:4], c[:, 2:3], c[:, 2:3]),
                  reads=(ck,), writes=(ck,))
            sc.op("vector", lambda v, c=cols: v.tensor_scalar(c[:, 4:5], c[:, 1:2], 1.0 / D, None, ALU.mult),
                  reads=(ck,), writes=(ck,))
            sc.op("vector", lambda v, c=cols: v.tensor_sub(c[:, 4:5], c[:, 4:5], c[:, 3:4]),
                  reads=(ck,), writes=(ck,))
            sc.op("scalar", lambda s, c=cols: s.activation(c[:, 5:6], c[:, 4:5], AF.Sqrt, bias=self.RS[:, 1:2]),
                  reads=(ck, "RS_eps1"), writes=(ck,))
            sc.op("vector", lambda v, c=cols: v.reciprocal(c[:, 6:7], c[:, 5:6]),
                  reads=(ck,), writes=(ck,))
            xn = self.RG[:, (it & 1) * 1024:(it & 1) * 1024 + 1024]
            xnk = f"ln0o{it & 1}"
            sc.op("vector", lambda v, xt=xt, xn=xn, c=cols: v.tensor_scalar(
                xn, xt, c[:, 2:3], c[:, 6:7], ALU.subtract, ALU.mult),
                reads=(xk, ck), writes=(xnk,))
            sc.dma("sync", dm["xn0"][t0:t0 + 128, :], xn, "xn0", rkeys=(xnk,))

    def load_w_lhsT(self, cname, region, off, douts, key):
        for ch in range(8):
            self.sc.dma("sync", region[:, off + ch * douts: off + (ch + 1) * douts],
                        self.dram[cname][ch * 128:(ch + 1) * 128, 0:douts],
                        key, rkeys=(cname,))

    def p3_l0proj(self):
        sc, dm = self.sc, self.dram
        RW1, RG, RF, RS, RM = self.RW1, self.RG, self.RF, self.RS, self.RM
        self.load_w_lhsT("c_L0_attn_Wq", RW1, 0, D, "wq0")
        self.load_w_lhsT("c_L0_attn_Wk", RW1, 8192, D, "wk0")
        self.load_w_lhsT("c_L0_attn_Wv", RW1, 16384, D, "wv0")
        self.load_w_lhsT("c_L0_attn_Wu", RW1, 24576, D, "wu0")
        PS = self.PS
        for tt in range(8):
            t0 = tt * 512
            pb = tt & 1
            xnf = RG[:, pb * 4096: pb * 4096 + 4096]
            xnfk = f"p3xn{pb}"
            for ch in range(8):
                sc.dma("sync", xnf[:, ch * 512:(ch + 1) * 512],
                       dm["xn0"][t0:t0 + 512, ch * 128:(ch + 1) * 128],
                       xnfk, rkeys=("xn0",), transpose=True)
            for wname, woff, dst in (("wq0", 0, "q0fm"), ("wk0", 8192, "k0fm")):
                for hh in range(8):
                    ps = PS[hh % 2]
                    pk = f"p3ps{hh % 2}"
                    for ch in range(8):
                        sc.op("tensor", lambda t, ps=ps, ch=ch, hh=hh, woff=woff, xnf=xnf:
                              t.matmul(ps[:, 0:512], self.RW1[:, woff + ch * D + hh * 128: woff + ch * D + hh * 128 + 128],
                                       xnf[:, ch * 512:(ch + 1) * 512],
                                       start=(ch == 0), stop=(ch == 7)),
                              reads=(wname, xnfk), writes=(pk,))
                    ev = RG[:, 8192 + (hh % 2) * 512: 8192 + (hh % 2) * 512 + 512]
                    evk = f"p3ev{hh % 2}"
                    sc.op("scalar", lambda s, ps=ps, ev=ev: s.activation(ev, ps[:, 0:512], AF.Silu),
                          reads=(pk,), writes=(evk,))
                    sc.dma("sync", dm[dst][hh * 128:(hh + 1) * 128, t0:t0 + 512], ev,
                           dst, rkeys=(evk,))
            for wname, woff, dst, dok in (("wu0", 24576, "u0s", 0), ("wv0", 16384, "v0p", 1)):
                for st4 in range(4):
                    ts0 = t0 + st4 * 128
                    for dh in range(2):
                        ps = PS[2 + dh]
                        pk = f"p3pt{dh}"
                        for ch in range(8):
                            sc.op("tensor", lambda t, ps=ps, ch=ch, dh=dh, woff=woff, xnf=xnf, st4=st4:
                                  t.matmul(ps[:, 0:512],
                                           xnf[:, ch * 512 + st4 * 128: ch * 512 + st4 * 128 + 128],
                                           self.RW1[:, woff + ch * D + dh * 512: woff + ch * D + dh * 512 + 512],
                                           start=(ch == 0), stop=(ch == 7)),
                                  reads=(wname, xnfk), writes=(pk,))
                    ev = RG[:, 9216 + (st4 & 1) * 1024: 9216 + (st4 & 1) * 1024 + 1024]
                    evk = f"p3evt{st4 & 1}"
                    for dh in range(2):
                        sc.op("scalar", lambda s, dh=dh, ev=ev: s.activation(
                            ev[:, dh * 512:(dh + 1) * 512], PS[2 + dh][:, 0:512], AF.Silu),
                            reads=(f"p3pt{dh}",), writes=(evk,))
                    sc.dma("sync", dm[dst][ts0:ts0 + 128, :], ev, dst, rkeys=(evk,))

    def p3b_keymask(self):
        sc, dm = self.sc, self.dram
        RG, RF = self.RG, self.RF
        for it in range(S // 128):
            t0 = it * 128
            pb = it & 1
            ktk = f"kmk{pb}"
            kt = RG[:, pb * 1024: pb * 1024 + 1024]
            for hh in range(8):
                sc.dma("sync", kt[:, hh * 128:(hh + 1) * 128],
                       dm["k0fm"][hh * 128:(hh + 1) * 128, t0:t0 + 128],
                       ktk, rkeys=("k0fm",), transpose=True)
            km = RF[:, 4096 + pb * 16: 4096 + pb * 16 + 16]
            kmk = f"kmc{pb}"
            for hh in range(8):
                sc.op("vector", lambda v, hh=hh, kt=kt, km=km: v.reduce_sum(
                    km[:, hh:hh + 1], kt[:, hh * 128:(hh + 1) * 128],
                    axis=mybir.AxisListType.X, apply_absolute_value=True),
                    reads=(ktk,), writes=(kmk,))
            sc.op("scalar", lambda s, km=km: s.activation(km[:, 8:16], km[:, 0:8], AF.Sign),
                  reads=(kmk,), writes=(kmk,))
            vt = RG[:, 2048 + pb * 1024: 2048 + pb * 1024 + 1024]
            vtk = f"kmv{pb}"
            sc.dma("sync", vt, dm["v0p"][t0:t0 + 128, :], vtk, rkeys=("v0p",))
            for hh in range(8):
                sc.op("vector", lambda v, hh=hh, vt=vt, km=km: v.tensor_scalar_mul(
                    vt[:, hh * 128:(hh + 1) * 128], vt[:, hh * 128:(hh + 1) * 128],
                    km[:, 8 + hh:9 + hh]),
                    reads=(kmk,), writes=(vtk,))
            sc.dma("sync", dm["v0p"][t0:t0 + 128, :], vt, "v0p", rkeys=(vtk,))

    def p4_l0attn(self):
        sc, dm = self.sc, self.dram
        RW1, RG, RM, PS = self.RW1, self.RG, self.RM, self.PS
        SK = 1.0 / np.sqrt(128.0)
        for h in range(H):
            skw = RW1[:, 0:8064]
            sc.dma("gpsimd", skw, bass.AP(dm["r_pad"], h * 8192, [[1, 128], [1, 8064]]),
                   "skw")
            kfm = RW1[:, 8192:12288]
            sc.dma("sync", kfm, dm["k0fm"][h * 128:(h + 1) * 128, :], "kfm")
            vst = RW1[:, 12288:16384]
            sc.dma("sync", vst, bass.AP(dm["v0p"], h * 128,
                                        [[1024, 128], [128 * 1024, 32], [1, 128]]),
                   "vst")
            for qg in range(8):
                q0 = qg * 512
                qb = RG[:, 0:512]
                sc.dma("sync", qb, dm["q0fm"][h * 128:(h + 1) * 128, q0:q0 + 512], "p4q")
                ub = RG[:, 512:1024]
                for sub in range(4):
                    sc.dma("sync", ub[:, sub * 128:(sub + 1) * 128],
                           dm["u0s"][q0 + sub * 128: q0 + (sub + 1) * 128,
                                     h * 128:(h + 1) * 128], "p4u")
                nkt = 4 * qg + 4

                def emit_scores(kt):
                    ps = PS[kt & 1]
                    pk = f"p4s{kt & 1}"
                    base = q0 - kt * 128 + 3968
                    sc.op("tensor", lambda t, ps=ps, kt=kt, qb=qb, kfm=kfm:
                          t.matmul(ps[:, 0:512], kfm[:, kt * 128:(kt + 1) * 128], qb,
                                   start=True, stop=False),
                          reads=("kfm", "p4q"), writes=(pk,))
                    sc.op("tensor", lambda t, ps=ps, base=base, skw=skw:
                          t.matmul(ps[:, 0:512], self.RM[:, 128:256],
                                   skw[:, base:base + 512], start=False, stop=True),
                          reads=("skw", "RM_J"), writes=(pk,))
                    wb = RG[:, 1024 + (kt & 1) * 512: 1024 + (kt & 1) * 512 + 512]
                    wk = f"p4w{kt & 1}"
                    sc.op("scalar", lambda s, ps=ps, wb=wb: s.activation(
                        wb, ps[:, 0:512], AF.Silu, scale=SK),
                        reads=(f"p4s{kt & 1}",), writes=(wk,))
                    j = kt - 4 * qg
                    if j >= 0:
                        if j > 0:
                            sc.op("vector", lambda v, wb=wb, j=j: v.memset(wb[:, 0:j * 128], 0.0),
                                  reads=(), writes=(wk,))
                        sc.op("vector", lambda v, wb=wb, j=j: v.tensor_mul(
                            wb[:, j * 128:(j + 1) * 128], wb[:, j * 128:(j + 1) * 128],
                            self.RM[:, 256:384]),
                            reads=("RM_tri",), writes=(wk,))

                def emit_attn(kt):
                    wb = RG[:, 1024 + (kt & 1) * 512: 1024 + (kt & 1) * 512 + 512]
                    wk = f"p4w{kt & 1}"
                    pa = PS[2]
                    for sub in range(4):
                        sc.op("tensor", lambda t, pa=pa, wb=wb, kt=kt, sub=sub, vst=vst, nkt=nkt:
                              t.matmul(pa[:, sub * 128:(sub + 1) * 128],
                                       wb[:, sub * 128:(sub + 1) * 128],
                                       vst[:, kt * 128:(kt + 1) * 128],
                                       start=(kt == 0 and sub == 0),
                                       stop=(kt == nkt - 1 and sub == 3)),
                              reads=(wk, "vst"), writes=("p4acc",))

                emit_scores(0)
                for kt in range(1, nkt):
                    emit_scores(kt)
                    emit_attn(kt - 1)
                emit_attn(nkt - 1)
                ob = RG[:, 3072:3584]
                sc.op("vector", lambda v, ob=ob, ub=ub: v.tensor_mul(
                    ob, PS[2][:, 0:512], ub),
                    reads=("p4acc", "p4u"), writes=("p4o",))
                sc.dma("sync",
                       bass.AP(dm["attn0"], q0 * D + h * 128,
                               [[D, 128], [128 * D, 4], [1, 128]]),
                       ob, "attn0", rkeys=("p4o",))
            self.sc.fence()

    def _compose(self, f_name, ft_name, stage_list, c, mslot, mtslot, keyb):
        sc, dm, RG = self.sc, self.dram, self.RG
        s0 = stage_list[0]
        m = RG[:, mslot:mslot + 128]
        mt = RG[:, mtslot:mtslot + 128]
        sc.dma("sync", m, dm[f_name][s0, c, :, :], keyb + "m")
        sc.dma("sync", mt, dm[ft_name][s0, c, :, :], keyb + "mt")
        ftile = RG[:, mslot + 512:mslot + 640]
        for i, s in enumerate(stage_list[1:]):
            sc.dma("sync", ftile, dm[ft_name][s, c, :, :], keyb + "f")
            pm, pmt = self.PS[4], self.PS[5]
            sc.op("tensor", lambda t, ftile=ftile, m=m, pm=pm: t.matmul(
                pm[:, 0:128], ftile, m, start=True, stop=True),
                reads=(keyb + "f", keyb + "m"), writes=(keyb + "pm",))
            sc.op("tensor", lambda t, ftile=ftile, mt=mt, pmt=pmt: t.matmul(
                pmt[:, 0:128], m, ftile, start=True, stop=True),
                reads=(keyb + "f", keyb + "m", keyb + "mt"), writes=(keyb + "pmt",))
            sc.op("vector", lambda v, m=m, pm=pm: v.tensor_copy(m, pm[:, 0:128]),
                  reads=(keyb + "pm",), writes=(keyb + "m",))
            sc.op("scalar", lambda s_, mt=mt, pmt=pmt: s_.copy(mt, pmt[:, 0:128]),
                  reads=(keyb + "pmt",), writes=(keyb + "mt",))
        return m, mt

    def p1_fold(self):
        sc, dm, RG, PS = self.sc, self.dram, self.RG, self.PS
        self.dscratch("w1a", (FF, D))
        self.dscratch("w2mid", (FF, D))
        self.dscratch("b1a", (FF,)); self.dscratch("b1b", (FF,))
        wct = RG[:, 1024:2048]
        bcol = RG[:, 2048:2049]
        for sub in range(4):
            for c in range(32):
                if sub == 0:
                    sc.dma("sync", wct, dm["c_W_expT"][c * 128:(c + 1) * 128, :], "p1w",
                           rkeys=("c_W_expT",))
                    sc.dma("gpsimd", bcol, bass.AP(dm["L0_ffn_b_exp"], c * 128, [[1, 128], [1, 1]]),
                           "p1b")
                    m, mt = self._compose("c_FAu", "c_FATu", [0, 1, 2, 3, 4, 5, 6], c, 4096, 4352, "p1")
                elif sub == 1:
                    sc.dma("sync", wct, bass.AP(dm["w1a"], c * 1024, [[32 * 1024, 128], [1, 1024]]),
                           "p1w", rkeys=("w1a",))
                    sc.dma("sync", bcol, bass.AP(dm["b1a"], c, [[32, 128], [1, 1]]),
                           "p1b", rkeys=("b1a",), allow_slow_non_contiguous=True)
                    m, mt = self._compose("c_FBu", "c_FBTu", [0, 1, 2, 3, 4], c, 4096, 4352, "p1")
                elif sub == 2:
                    sc.dma("sync", wct, bass.AP(dm["c_W_con"], c * 1024, [[32 * 1024, 128], [1, 1024]]),
                           "p1w", rkeys=("c_W_con",))
                    m, mt = self._compose("c_FBd", "c_FBTd", [4, 3, 2, 1, 0], c, 4096, 4352, "p1")
                else:
                    sc.dma("sync", wct, dm["w2mid"][c * 128:(c + 1) * 128, :], "p1w",
                           rkeys=("w2mid",))
                    m, mt = self._compose("c_FAd", "c_FATd", [6, 5, 4, 3, 2, 1, 0], c, 4096, 4352, "p1")
                for dh in range(2):
                    ps = PS[dh]
                    sc.op("tensor", lambda t, ps=ps, mt=mt, dh=dh, wct=wct: t.matmul(
                        ps[:, 0:512], mt, wct[:, dh * 512:(dh + 1) * 512],
                        start=True, stop=True),
                        reads=("p1m", "p1mt", "p1w"), writes=(f"p1p{dh}",))
                oc = RG[:, 2560:3584]
                for dh in range(2):
                    sc.op("vector", lambda v, oc=oc, dh=dh: v.tensor_copy(
                        oc[:, dh * 512:(dh + 1) * 512], PS[dh][:, 0:512]),
                        reads=(f"p1p{dh}",), writes=("p1o",))
                if sub in (0, 1):
                    ps = PS[2]
                    sc.op("tensor", lambda t, ps=ps, mt=mt, bcol=bcol: t.matmul(
                        ps[:, 0:1], mt, bcol, start=True, stop=True),
                        reads=("p1m", "p1mt", "p1b"), writes=("p1pb",))
                    ob = RG[:, 2049:2050]
                    sc.op("vector", lambda v, ob=ob, ps=ps: v.tensor_copy(ob, ps[:, 0:1]),
                          reads=("p1pb",), writes=("p1ob",))
                if sub == 0:
                    sc.dma("sync", dm["w1a"][c * 128:(c + 1) * 128, :], oc, "w1a", rkeys=("p1o",))
                    sc.dma("sync", bass.AP(dm["b1a"], c * 128, [[1, 128], [1, 1]]),
                           RG[:, 2049:2050], "b1a", rkeys=("p1ob",))
                elif sub == 1:
                    sc.dma("sync", bass.AP(dm["w1pre"], c * 1024, [[32 * 1024, 128], [1, 1024]]),
                           oc, "w1pre", rkeys=("p1o",))
                    sc.dma("sync", bass.AP(dm["b1b"], c, [[32, 128], [1, 1]]),
                           RG[:, 2049:2050], "b1b", rkeys=("p1ob",), allow_slow_non_contiguous=True)
                elif sub == 2:
                    sc.dma("sync", bass.AP(dm["w2mid"], c * 1024, [[32 * 1024, 128], [1, 1024]]),
                           oc, "w2mid", rkeys=("p1o",))
                else:
                    sc.op("vector", lambda v, c=c, oc=oc: v.tensor_copy(
                        self.RW2[:, c * 1024:(c + 1) * 1024], oc),
                        reads=("p1o",), writes=("W2res",))
                self.sc.fence()

    def _ln_tile(self, xt, xk, csl, ck, xn_out, xnk, eps_key="RS_eps1", par=0):
        sc, RS = self.sc, self.RS
        sq = self.RF[:, 6144 + par * 1024: 7168 + par * 1024]
        sqk = f"lnsq{par}"
        sc.op("vector", lambda v: v.reduce_sum(csl[:, 0:1], xt, axis=mybir.AxisListType.X),
              reads=(xk,), writes=(ck,))
        sc.op("scalar", lambda s: s.activation(sq, xt, AF.Square), reads=(xk,), writes=(sqk,))
        sc.op("vector", lambda v: v.reduce_sum(csl[:, 1:2], sq, axis=mybir.AxisListType.X),
              reads=(sqk,), writes=(ck,))
        sc.op("vector", lambda v: v.tensor_scalar_mul(csl[:, 2:3], csl[:, 0:1], 1.0 / D),
              reads=(ck,), writes=(ck,))
        sc.op("vector", lambda v: v.tensor_mul(csl[:, 3:4], csl[:, 2:3], csl[:, 2:3]),
              reads=(ck,), writes=(ck,))
        sc.op("vector", lambda v: v.tensor_scalar_mul(csl[:, 4:5], csl[:, 1:2], 1.0 / D),
              reads=(ck,), writes=(ck,))
        sc.op("vector", lambda v: v.tensor_sub(csl[:, 4:5], csl[:, 4:5], csl[:, 3:4]),
              reads=(ck,), writes=(ck,))
        ecol = self.RS[:, 1:2] if eps_key == "RS_eps1" else self.RS[:, 0:1]
        sc.op("scalar", lambda s: s.activation(csl[:, 5:6], csl[:, 4:5], AF.Sqrt, bias=ecol),
              reads=(ck, eps_key), writes=(ck,))
        sc.op("vector", lambda v: v.reciprocal(csl[:, 6:7], csl[:, 5:6]),
              reads=(ck,), writes=(ck,))
        sc.op("vector", lambda v: v.tensor_scalar(xn_out, xt, csl[:, 2:3], csl[:, 6:7],
                                                  ALU.subtract, ALU.mult),
              reads=(xk, ck), writes=(xnk,))

    def p5_l0ffn(self):
        sc, dm = self.sc, self.dram
        RW1, RW2, RG, RF, RS, RM, PS = self.RW1, self.RW2, self.RG, self.RF, self.RS, self.RM, self.PS
        for ch in range(8):
            sc.dma("sync", RW1[:, ch * 4096:(ch + 1) * 4096],
                   dm["w1pre"][:, ch * 128:(ch + 1) * 128], "W1res",
                   rkeys=("w1pre",), transpose=True)
        for ft in range(32):
            sc.dma("gpsimd", RF[:, 0:1], bass.AP(dm["b1b"], ft * 128, [[1, 128], [1, 1]]),
                   "p5b1", rkeys=("b1b",))
            sc.dma("sync", RF[:, 1:2], bass.AP(dm["L0_ffn_up_bias"], ft * 128, [[1, 128], [1, 1]]),
                   "p5ub")
            sc.op("vector", lambda v, ft=ft: v.tensor_add(RS[:, 32 + ft:33 + ft],
                                                          RF[:, 0:1], RF[:, 1:2]),
                  reads=("p5b1", "p5ub"), writes=("bgelu",))
            self.sc.fence()
        sc.dma("sync", RF[0:1, 0:1024], dm["L0_ffn_b_con"][:], "p5bc")
        sc.dma("sync", RF[0:1, 1024:2048], dm["L0_ffn_down_bias"][:], "p5db")
        sc.op("vector", lambda v: v.tensor_add(RF[0:1, 0:1024], RF[0:1, 0:1024],
                                               RF[0:1, 1024:2048]),
              reads=("p5bc", "p5db"), writes=("p5bc",))
        sc.op("vector", lambda v: v.tensor_copy(RM[0:1, 2048:3072], RF[0:1, 0:1024]),
              reads=("p5bc",), writes=("bsum",))
        self.sc.fence()
        for tt in range(16):
            t0 = tt * 256
            pb = tt & 1
            xb = RG[:, pb * 2048: pb * 2048 + 2048]
            xbk = f"p5x{pb}"
            for ch in range(8):
                sc.dma("sync", xb[:, ch * 256:(ch + 1) * 256],
                       dm["xn0"][t0:t0 + 256, ch * 128:(ch + 1) * 128],
                       xbk, rkeys=("xn0",), transpose=True)
            h2 = RG[:, 4096:12288]
            h2k = "p5h2"
            for ft in range(32):
                ps = PS[ft & 1]
                pk = f"p5p{ft & 1}"
                for ch in range(8):
                    sc.op("tensor", lambda t, ps=ps, ch=ch, ft=ft, xb=xb: t.matmul(
                        ps[:, 0:256], RW1[:, ch * 4096 + ft * 128: ch * 4096 + ft * 128 + 128],
                        xb[:, ch * 256:(ch + 1) * 256], start=(ch == 0), stop=(ch == 7)),
                        reads=("W1res", xbk), writes=(pk,))
                sc.op("scalar", lambda s, ps=ps, ft=ft, h2=h2: s.activation(
                    h2[:, ft * 256:(ft + 1) * 256], ps[:, 0:256], AF.Gelu,
                    bias=RS[:, 32 + ft:33 + ft]),
                    reads=(pk, "bgelu"), writes=(h2k,))
            for t128 in range(2):
                ts0 = t0 + t128 * 128
                x0t = RF[:, 0:1024]
                sc.dma("sync", x0t, dm["x"][ts0:ts0 + 128, :], "p5x0")
                att = RG[:, 12288:13312]
                sc.dma("sync", att, dm["attn0"][ts0:ts0 + 128, :], "p5at", rkeys=("attn0",))
                x1t = RF[:, 1024:2048]
                for dh in range(2):
                    ps = PS[2 + dh]
                    pk = f"p5y{dh}"
                    for ft in range(32):
                        sc.op("tensor", lambda t, ps=ps, ft=ft, dh=dh, h2=h2, t128=t128:
                              t.matmul(ps[:, 0:512],
                                       h2[:, ft * 256 + t128 * 128: ft * 256 + t128 * 128 + 128],
                                       RW2[:, ft * 1024 + dh * 512: ft * 1024 + dh * 512 + 512],
                                       start=(ft == 0), stop=False),
                              reads=(h2k, "W2res"), writes=(pk,))
                    sc.op("tensor", lambda t, ps=ps, dh=dh: t.matmul(
                        ps[:, 0:512], RM[0:1, 896:1024], RM[0:1, 2048 + dh * 512: 2048 + (dh + 1) * 512],
                        start=False, stop=True),
                        reads=("RM_ones", "bsum"), writes=(pk,))
                    sc.op("vector", lambda v, ps=ps, dh=dh, x0t=x0t, x1t=x1t: v.tensor_add(
                        x1t[:, dh * 512:(dh + 1) * 512], ps[:, 0:512], x0t[:, dh * 512:(dh + 1) * 512]),
                        reads=(pk, "p5x0"), writes=("p5x1",))
                    sc.op("vector", lambda v, dh=dh, x1t=x1t, att=att: v.tensor_add(
                        x1t[:, dh * 512:(dh + 1) * 512], x1t[:, dh * 512:(dh + 1) * 512],
                        att[:, dh * 512:(dh + 1) * 512]),
                        reads=("p5at",), writes=("p5x1",))
                sc.dma("sync", dm["x1"][ts0:ts0 + 128, :], x1t, "x1", rkeys=("p5x1",))
                xn1t = RG[:, 13312:14336]
                self._ln_tile(x1t, f"p5x1", RS[:, 16 + t128 * 8:24 + t128 * 8], f"p5lc{t128}",
                              xn1t, "p5xn1", par=t128)
                sc.dma("sync", dm["xn1"][ts0:ts0 + 128, :], xn1t, "xn1", rkeys=("p5xn1",))
            self.sc.fence()


    def p6_l1proj(self):
        sc, dm = self.sc, self.dram
        RW1, RG, RF, RS, PS = self.RW1, self.RG, self.RF, self.RS, self.PS
        self.load_w_lhsT("c_L1_attn_Wq", RW1, 0, D, "wq1")
        self.load_w_lhsT("c_L1_attn_Wk", RW1, 8192, D, "wk1")
        self.load_w_lhsT("c_L1_attn_Wv", RW1, 16384, D, "wv1")
        self.load_w_lhsT("c_wf1", RW1, 24576, HD, "wf1")
        self.load_w_lhsT("c_wog1", RW1, 25600, HD, "wog1")
        self.load_w_lhsT("c_wbeta", RW1, 26624, H, "wbeta")
        sc.dma("sync", RW1[:, 27648:28672], dm["c_wf2"][:, :], "wf2", rkeys=("c_wf2",))
        for tt in range(8):
            t0 = tt * 512
            pb = tt & 1
            xb = RG[:, pb * 4096: pb * 4096 + 4096]
            xbk = f"p6xn{pb}"
            for ch in range(8):
                sc.dma("sync", xb[:, ch * 512:(ch + 1) * 512],
                       dm["xn1"][t0:t0 + 512, ch * 128:(ch + 1) * 128],
                       xbk, rkeys=("xn1",), transpose=True)
            for wname, woff, dst in (("wq1", 0, "q1s"), ("wv1", 16384, "v1s")):
                for st4 in range(4):
                    ts0 = t0 + st4 * 128
                    for dh in range(2):
                        ps = PS[dh]
                        pk = f"p6pt{dh}"
                        for ch in range(8):
                            sc.op("tensor", lambda t, ps=ps, ch=ch, dh=dh, woff=woff, xb=xb, st4=st4:
                                  t.matmul(ps[:, 0:512],
                                           xb[:, ch * 512 + st4 * 128: ch * 512 + st4 * 128 + 128],
                                           RW1[:, woff + ch * D + dh * 512: woff + ch * D + dh * 512 + 512],
                                           start=(ch == 0), stop=(ch == 7)),
                                  reads=(wname, xbk), writes=(pk,))
                    ev = RG[:, 8192 + (st4 & 1) * 1024: 8192 + (st4 & 1) * 1024 + 1024]
                    evk = f"p6ev{st4 & 1}"
                    for dh in range(2):
                        sc.op("scalar", lambda s, dh=dh, ev=ev: s.activation(
                            ev[:, dh * 512:(dh + 1) * 512], PS[dh][:, 0:512], AF.Silu),
                            reads=(f"p6pt{dh}",), writes=(evk,))
                    sc.dma("sync", dm[dst][ts0:ts0 + 128, :], ev, dst, rkeys=(evk,))
            for st4 in range(4):
                ts0 = t0 + st4 * 128
                for dh in range(2):
                    ps = PS[dh]
                    pk = f"p6pt{dh}"
                    for ch in range(8):
                        sc.op("tensor", lambda t, ps=ps, ch=ch, dh=dh, xb=xb, st4=st4:
                              t.matmul(ps[:, 0:512],
                                       xb[:, ch * 512 + st4 * 128: ch * 512 + st4 * 128 + 128],
                                       RW1[:, 8192 + ch * D + dh * 512: 8192 + ch * D + dh * 512 + 512],
                                       start=(ch == 0), stop=(ch == 7)),
                              reads=("wk1", xbk), writes=(pk,))
                kt = RG[:, 8192 + (st4 & 1) * 1024: 8192 + (st4 & 1) * 1024 + 1024]
                ktk = f"p6ev{st4 & 1}"
                for dh in range(2):
                    sc.op("scalar", lambda s, dh=dh, kt=kt: s.activation(
                        kt[:, dh * 512:(dh + 1) * 512], PS[dh][:, 0:512], AF.Silu),
                        reads=(f"p6pt{dh}",), writes=(ktk,))
                psb = PS[2]
                for ch in range(8):
                    sc.op("tensor", lambda t, ch=ch, xb=xb, st4=st4, psb=psb:
                          t.matmul(psb[:, 0:8],
                                   xb[:, ch * 512 + st4 * 128: ch * 512 + st4 * 128 + 128],
                                   RW1[:, 26624 + ch * H: 26624 + (ch + 1) * H],
                                   start=(ch == 0), stop=(ch == 7)),
                          reads=("wbeta", xbk), writes=("p6bps",))
                cols = RF[:, 6144 + (st4 & 1) * 32: 6144 + (st4 & 1) * 32 + 32]
                ck = f"p6c{st4 & 1}"
                sc.op("scalar", lambda s, cols=cols, psb=psb: s.activation(
                    cols[:, 0:8], psb[:, 0:8], AF.Sigmoid),
                    reads=("p6bps",), writes=(ck,))
                sc.op("vector", lambda v, cols=cols: v.tensor_scalar_mul(
                    cols[:, 0:8], cols[:, 0:8], -2.0), reads=(ck,), writes=(ck,))
                sq = RF[:, 4096 + (st4 & 1) * 1024: 4096 + (st4 & 1) * 1024 + 1024]
                sqk = f"p6sq{st4 & 1}"
                sc.op("vector", lambda v, kt=kt, sq=sq: v.tensor_mul(sq, kt, kt),
                      reads=(ktk,), writes=(sqk,))
                for hh in range(8):
                    sc.op("vector", lambda v, hh=hh, sq=sq, cols=cols: v.reduce_sum(
                        cols[:, 8 + hh:9 + hh], sq[:, hh * 128:(hh + 1) * 128],
                        axis=mybir.AxisListType.X), reads=(sqk,), writes=(ck,))
                sc.op("scalar", lambda s, cols=cols: s.activation(
                    cols[:, 16:24], cols[:, 8:16], AF.Sqrt, bias=self.RS[:, 0:1]),
                    reads=(ck, "RS_eps6"), writes=(ck,))
                sc.op("vector", lambda v, cols=cols: v.reciprocal(cols[:, 24:32], cols[:, 16:24]),
                      reads=(ck,), writes=(ck,))
                knt = RG[:, 10240 + (st4 & 1) * 1024: 10240 + (st4 & 1) * 1024 + 1024]
                knbt = RG[:, 12288 + (st4 & 1) * 1024: 12288 + (st4 & 1) * 1024 + 1024]
                knk = f"p6kn{st4 & 1}"
                for hh in range(8):
                    sc.op("vector", lambda v, hh=hh, kt=kt, knt=knt, cols=cols: v.tensor_scalar_mul(
                        knt[:, hh * 128:(hh + 1) * 128], kt[:, hh * 128:(hh + 1) * 128],
                        cols[:, 24 + hh:25 + hh]), reads=(ktk, ck), writes=(knk,))
                    sc.op("vector", lambda v, hh=hh, knt=knt, knbt=knbt, cols=cols: v.tensor_scalar_mul(
                        knbt[:, hh * 128:(hh + 1) * 128], knt[:, hh * 128:(hh + 1) * 128],
                        cols[:, hh:hh + 1]), reads=(knk, ck), writes=(knk,))
                sc.dma("sync", dm["kn"][ts0:ts0 + 128, :], knt, "kn", rkeys=(knk,))
                sc.dma("sync", dm["knb"][ts0:ts0 + 128, :], knbt, "knb", rkeys=(knk,))
            ps = PS[3]
            for ch in range(8):
                sc.op("tensor", lambda t, ch=ch, xb=xb, ps=ps: t.matmul(
                    ps[:, 0:512], RW1[:, 24576 + ch * HD: 24576 + (ch + 1) * HD],
                    xb[:, ch * 512:(ch + 1) * 512], start=(ch == 0), stop=(ch == 7)),
                    reads=("wf1", xbk), writes=("p6f1",))
            t1 = RG[:, 14336:14848]
            sc.op("vector", lambda v, t1=t1, ps=ps: v.tensor_copy(t1, ps[:, 0:512]),
                  reads=("p6f1",), writes=("p6t1",))
            for hh in range(8):
                ps2 = PS[4]
                sc.op("tensor", lambda t, hh=hh, t1=t1, ps2=ps2: t.matmul(
                    ps2[:, 0:512], RW1[:, 27648 + hh * 128: 27648 + (hh + 1) * 128], t1,
                    start=True, stop=True),
                    reads=("wf2", "p6t1"), writes=("p6f2",))
                el = RF[:, 0 + (hh & 1) * 1024: 512 + (hh & 1) * 1024]
                ll = RF[:, 512 + (hh & 1) * 1024: 1024 + (hh & 1) * 1024]
                lk = f"p6l{hh & 1}"
                sc.op("scalar", lambda s, el=el, ps2=ps2: s.activation(el, ps2[:, 0:512], AF.Exp, scale=-1.0),
                      reads=("p6f2",), writes=(lk,))
                sc.op("scalar", lambda s, el=el, ll=ll: s.activation(ll, el, AF.Ln, bias=1.0),
                      reads=(lk,), writes=(lk,))
                sc.dma("sync", dm["lsp"][hh * 128:(hh + 1) * 128, t0:t0 + 512], ll,
                       "lsp", rkeys=(lk,))
            ps = PS[5]
            for ch in range(8):
                sc.op("tensor", lambda t, ch=ch, xb=xb, ps=ps: t.matmul(
                    ps[:, 0:512], RW1[:, 25600 + ch * HD: 25600 + (ch + 1) * HD],
                    xb[:, ch * 512:(ch + 1) * 512], start=(ch == 0), stop=(ch == 7)),
                    reads=("wog1", xbk), writes=("p6og",))
            og = RG[:, 14848:15360]
            sc.op("vector", lambda v, og=og, ps=ps: v.tensor_copy(og, ps[:, 0:512]),
                  reads=("p6og",), writes=("p6ogt",))
            sc.dma("sync", dm["og1fm"][:, t0:t0 + 512], og, "og1fm", rkeys=("p6ogt",))
            self.sc.fence()

    def p7_scan(self):
        sc, dm = self.sc, self.dram
        RW1, RW2, RF, RS, RM, PS, PB = self.RW1, self.RW2, self.RF, self.RS, self.RM, self.PS, self.PB
        for hh in range(H):
            sc.op("vector", lambda v, hh=hh: v.memset(RW2[:, hh * 128:(hh + 1) * 128], 0.0),
                  writes=(f"S0q{hh // 4}",))
        for r in range(4):
            sc.dma("gpsimd", RW2[:, 25088 + r * 128: 25216 + r * 128], dm["m_l"][:, :], "WM")
            sc.dma("gpsimd", RW2[:, 25600 + r * 128: 25728 + r * 128], dm["m_lt"][:, :], "WM")
            sc.dma("gpsimd", RW2[0:64, 26112 + r * 128: 26240 + r * 128], dm["m_kbt"][:, :], "WM")
            sc.dma("gpsimd", RW2[:, 26624 + r * 64: 26688 + r * 64], dm["m_aq"][:, :], "WM")
            sc.dma("gpsimd", RW2[0:64, 26880 + r * 64: 26944 + r * 64], dm["m_kq"][:, :], "WM")
        ML4 = RW2[:, 25088:25600]; MLT4 = RW2[:, 25600:26112]
        MKBT4 = RW2[0:64, 26112:26624]; MAQ4 = RW2[:, 26624:26880]; MKQ4 = RW2[0:64, 26880:27136]
        IDE = RM[:, 0:128]
        self.sc.fence()
        for c in range(NCH):
            t0 = c * CT
            for q in range(2):
                P = q * 9216
                W = lambda o, w: RW1[:, P + o: P + o + w]
                kn4, knb4, q4 = W(0, 256), W(256, 256), W(512, 256)
                A4, B4 = W(768, 512), W(1280, 512)
                K4, Q4 = W(1792, 256), W(2048, 256)
                L4, LT4 = W(2304, 512), W(2816, 512)
                KBT4 = W(3328, 512)
                AQ4, KQ4 = W(3840, 256), W(4096, 256)
                X4 = W(4352, 1024)
                PPT4a, PPT4b = W(5376, 512), W(5888, 512)
                At4, Kt4 = W(6400, 512), W(6912, 512)
                Qe4 = W(7424, 256)
                dph4 = W(7680, 512)
                vt4 = W(8192, 512)
                ob4 = W(8704 - 0, 0) if False else None
                Fq = q * 1600
                lt4 = RF[:, Fq: Fq + 256]
                cs4 = RF[:, Fq + 256: Fq + 516]
                gm4 = RF[:, Fq + 516: Fq + 776]
                igm4 = RF[:, Fq + 776: Fq + 1036]
                tmpf4 = RF[:, Fq + 1036: Fq + 1548]
                k_ = lambda n: f"s{n}{q}"
                S0q = RW2[:, q * 512:(q + 1) * 512]
                S0k = f"S0q{q}"
                h0 = q * 4
                for j in range(4):
                    hh = h0 + j
                    sc.dma("sync", kn4[:, j * 64:(j + 1) * 64],
                           dm["kn"][t0:t0 + CT, hh * 128:(hh + 1) * 128],
                           k_("kn"), rkeys=(), transpose=True)
                    sc.dma("sync", knb4[:, j * 64:(j + 1) * 64],
                           dm["knb"][t0:t0 + CT, hh * 128:(hh + 1) * 128],
                           k_("knb"), rkeys=(), transpose=True)
                    sc.dma("sync", q4[:, j * 64:(j + 1) * 64],
                           dm["q1s"][t0:t0 + CT, hh * 128:(hh + 1) * 128],
                           k_("qf"), rkeys=(), transpose=True)
                sc.dma("sync", vt4[0:64, :], dm["v1s"][t0:t0 + CT, h0 * 128:(h0 + 4) * 128],
                       k_("vt"))
                sc.dma("sync", lt4, bass.AP(dm["lsp"], (h0 * 128) * S + t0,
                                            [[S, 128], [128 * S, 4], [1, 64]]),
                       k_("l"))
                sc.op("vector", lambda v, cs4=cs4: v.memset(
                    bass.AP(cs4.tensor, cs4.offset, [[8192, 128], [65, 4], [1, 1]]), 0.0),
                    writes=(k_("cs"),))
                for j in range(4):
                    sc.op("vector", lambda v, cs4=cs4, lt4=lt4, j=j: v.tensor_tensor_scan(
                        cs4[:, j * 65 + 1: j * 65 + 65], lt4[:, j * 64:(j + 1) * 64],
                        RS[:, 512:576], 0.0, ALU.add, ALU.add),
                        reads=(k_("l"), "RS_zero64"), writes=(k_("cs"),))
                sc.op("scalar", lambda s, gm4=gm4, cs4=cs4: s.activation(gm4, cs4, AF.Exp, scale=-1.0),
                      reads=(k_("cs"),), writes=(k_("gm"),))
                sc.op("scalar", lambda s, igm4=igm4, cs4=cs4: s.activation(igm4, cs4, AF.Exp, scale=1.0),
                      reads=(k_("cs"),), writes=(k_("igm"),))
                gprev = lambda g4: bass.AP(g4.tensor, g4.offset, [[8192, 128], [65, 4], [1, 64]])
                gcur = lambda g4: bass.AP(g4.tensor, g4.offset + 1, [[8192, 128], [65, 4], [1, 64]])
                evenv = lambda t4: bass.AP(t4.tensor, t4.offset, [[32768, 128], [128, 4], [2, 64]])
                oddv = lambda t4: bass.AP(t4.tensor, t4.offset + 1, [[32768, 128], [128, 4], [2, 64]])
                v3 = lambda t4, w: bass.AP(t4.tensor, t4.offset, [[32768, 128], [w, 4], [1, w]])
                for dst, s1, s2, ka, kb, ko in (
                        (evenv(A4), v3(knb4, 64), gprev(igm4), k_("knb"), k_("igm"), k_("A")),
                        (oddv(A4), v3(knb4, 64), gcur(igm4), k_("knb"), k_("igm"), k_("A")),
                        (evenv(B4), v3(kn4, 64), gprev(gm4), k_("kn"), k_("gm"), k_("B")),
                        (oddv(B4), v3(kn4, 64), gcur(gm4), k_("kn"), k_("gm"), k_("B")),
                        (K4, kn4, gcur(igm4), k_("kn"), k_("igm"), k_("K")),
                        (Q4, q4, gcur(gm4), k_("qf"), k_("gm"), k_("Q"))):
                    sc.op("vector", lambda v, dst=dst, s1=s1, s2=s2: v.tensor_mul(dst, s1, s2),
                          reads=(ka, kb), writes=(ko,))
                for j in range(4):
                    sc.op("tensor", lambda t, j=j, A4=A4: t.transpose(
                        PB[q][:, (j & 1) * 256:(j & 1) * 256 + 128],
                        A4[:, j * 128:(j + 1) * 128], IDE),
                        reads=(k_("A"), "RM_id"), writes=(f"pb{q}",))
                    sc.op("vector", lambda v, j=j, At4=At4: v.tensor_copy(
                        At4[:, j * 128:(j + 1) * 128],
                        PB[q][:, (j & 1) * 256:(j & 1) * 256 + 128]),
                        reads=(f"pb{q}",), writes=(k_("At"),))
                    sc.op("tensor", lambda t, j=j, B4=B4: t.transpose(
                        PB[q][:, (j & 1) * 256 + 128:(j & 1) * 256 + 256],
                        B4[:, j * 128:(j + 1) * 128], IDE),
                        reads=(k_("B"), "RM_id"), writes=(f"pb{q}",))
                    sc.op("vector", lambda v, j=j, X4=X4: v.tensor_copy(
                        X4[:, j * 256: j * 256 + 128],
                        PB[q][:, (j & 1) * 256 + 128:(j & 1) * 256 + 256]),
                        reads=(f"pb{q}",), writes=(k_("X"),))
                    sc.op("tensor", lambda t, j=j, K4=K4: t.transpose(
                        PB[q][0:64, (j & 1) * 256:(j & 1) * 256 + 128],
                        K4[:, j * 64:(j + 1) * 64], IDE),
                        reads=(k_("K"), "RM_id"), writes=(f"pb{q}",))
                    sc.op("vector", lambda v, j=j, Kt4=Kt4: v.tensor_copy(
                        Kt4[0:64, j * 128:(j + 1) * 128],
                        PB[q][0:64, (j & 1) * 256:(j & 1) * 256 + 128]),
                        reads=(f"pb{q}",), writes=(k_("Kt"),))
                G = PS[q]
                gk = f"psg{q}"
                def gram4(lhs4, rhs4, lw, rw, lk, rk, out4, mask, ok, rows=128, G=G, gk=gk):
                    for j in range(4):
                        po_ = G[0:rows, j * 128: j * 128 + rw] if rw == 128 else G[0:rows, j * 64:(j + 1) * 64]
                        sc.op("tensor", lambda t, po_=po_, la=lhs4[:, j * lw:(j + 1) * lw],
                              ra=rhs4[:, j * rw:(j + 1) * rw]: t.matmul(
                            po_, la, ra, start=True, stop=True),
                            reads=(lk, rk), writes=(gk,))
                    gsl = G[0:rows, 0:512] if rw == 128 else G[0:rows, 0:256]
                    sc.op("vector", lambda v, o_=out4[0:rows, :], gsl=gsl, mask=mask: v.tensor_mul(
                        o_, gsl, mask),
                        reads=(gk,), writes=(ok,))
                gram4(B4, A4, 128, 128, k_("B"), k_("A"), L4, ML4, k_("L"))
                gram4(A4, B4, 128, 128, k_("A"), k_("B"), LT4, MLT4, k_("LT"))
                gram4(K4, B4, 64, 128, k_("K"), k_("B"), KBT4, MKBT4, k_("KBT"), rows=64)
                gram4(A4, Q4, 128, 64, k_("A"), k_("Q"), AQ4, MAQ4, k_("AQ"))
                gram4(K4, Q4, 64, 64, k_("K"), k_("Q"), KQ4, MKQ4, k_("KQ"), rows=64)
                for j in range(4):
                    sc.op("tensor", lambda t, G=G, j=j, KBT4=KBT4, vt4=vt4: t.matmul(
                        G[:, j * 128:(j + 1) * 128], KBT4[0:64, j * 128:(j + 1) * 128],
                        vt4[0:64, j * 128:(j + 1) * 128], start=True, stop=True),
                        reads=(k_("KBT"), k_("vt")), writes=(gk,))
                x4r2 = bass.AP(X4.tensor, X4.offset + 128, [[32768, 128], [256, 4], [1, 128]])
                sc.op("vector", lambda v, x4r2=x4r2, G=G: v.tensor_copy(x4r2, G[:, 0:512]),
                      reads=(gk,), writes=(k_("X"),))
                BA, BB = PS[4 - 2 * q], PS[5 - 2 * q]
                bak, bbk = f"psa{q}", f"psb{q}"
                psl = lambda j: L4[:, j * 128:(j + 1) * 128]
                ptl = lambda j: LT4[:, j * 128:(j + 1) * 128]
                psp = lambda j: RW1[:, P + 5376 + j * 256: P + 5376 + j * 256 + 128]
                ptp = lambda j: RW1[:, P + 5376 + j * 256 + 128: P + 5376 + j * 256 + 256]
                pget, ptget, pck, ptk = psl, ptl, k_("L"), k_("LT")
                kpow = 1
                while kpow <= 64:
                    for j in range(4):
                        bank, bkk = (BA, bak) if j < 2 else (BB, bbk)
                        sc.op("tensor", lambda t, j=j, bank=bank, pt=ptget(j), X4=X4: t.matmul(
                            bank[:, (j & 1) * 256:(j & 1) * 256 + 256],
                            pt, X4[:, j * 256:(j + 1) * 256],
                            start=True, stop=True),
                            reads=(ptk, k_("X")), writes=(bkk,))
                    sc.op("vector", lambda v, X4=X4, BA=BA: v.tensor_add(
                        X4[:, 0:512], X4[:, 0:512], BA[:, 0:512]),
                        reads=(bak,), writes=(k_("X"),))
                    sc.op("vector", lambda v, X4=X4, BB=BB: v.tensor_add(
                        X4[:, 512:1024], X4[:, 512:1024], BB[:, 0:512]),
                        reads=(bbk,), writes=(k_("X"),))
                    if kpow < 64:
                        for j in range(4):
                            bank, bkk = (BA, bak) if j < 2 else (BB, bbk)
                            r0 = (j & 1) * 256
                            sc.op("tensor", lambda t, bank=bank, r0=r0, pt=ptget(j), pc=pget(j): t.matmul(
                                bank[:, r0:r0 + 128], pt, pc, start=True, stop=True),
                                reads=(ptk, pck), writes=(bkk,))
                            sc.op("tensor", lambda t, bank=bank, r0=r0, pt=ptget(j), pc=pget(j): t.matmul(
                                bank[:, r0 + 128:r0 + 256], pc, pt, start=True, stop=True),
                                reads=(ptk, pck), writes=(bkk,))
                        sc.op("vector", lambda v, PPT4a=PPT4a, BA=BA: v.tensor_copy(PPT4a, BA[:, 0:512]),
                              reads=(bak,), writes=(k_("PP"),))
                        sc.op("vector", lambda v, PPT4b=PPT4b, BB=BB: v.tensor_copy(PPT4b, BB[:, 0:512]),
                              reads=(bbk,), writes=(k_("PP"),))
                        pget, ptget, pck, ptk = psp, ptp, k_("PP"), k_("PP")
                    kpow *= 2
                for j in range(4):
                    sc.op("tensor", lambda t, G=G, j=j, X4=X4, AQ4=AQ4: t.matmul(
                        G[:, j * 64:(j + 1) * 64], X4[:, j * 256: j * 256 + 128],
                        AQ4[:, j * 64:(j + 1) * 64], start=True, stop=True),
                        reads=(k_("X"), k_("AQ")), writes=(gk,))
                sc.op("vector", lambda v, Qe4=Qe4, G=G, Q4=Q4: v.tensor_add(Qe4, G[:, 0:256], Q4),
                      reads=(gk, k_("Q")), writes=(k_("Qe"),))
                for j in range(4):
                    sc.op("tensor", lambda t, G=G, j=j, S0q=S0q, Qe4=Qe4: t.matmul(
                        G[:, 256 + j * 64: 256 + (j + 1) * 64], S0q[:, j * 128:(j + 1) * 128],
                        Qe4[:, j * 64:(j + 1) * 64], start=True, stop=(False)),
                        reads=(S0k, k_("Qe")), writes=(gk,))
                    sc.op("tensor", lambda t, G=G, j=j, X4=X4, AQ4=AQ4: t.matmul(
                        G[:, 256 + j * 64: 256 + (j + 1) * 64], X4[:, j * 256 + 128: j * 256 + 256],
                        AQ4[:, j * 64:(j + 1) * 64], start=False, stop=False),
                        reads=(k_("X"), k_("AQ")), writes=(gk,))
                    sc.op("tensor", lambda t, G=G, j=j, vt4=vt4, KQ4=KQ4: t.matmul(
                        G[:, 256 + j * 64: 256 + (j + 1) * 64], vt4[0:64, j * 128:(j + 1) * 128],
                        KQ4[0:64, j * 64:(j + 1) * 64], start=False, stop=True),
                        reads=(k_("vt"), k_("KQ")), writes=(gk,))
                ob4 = RW1[:, P + 8704: P + 8960]
                sc.op("vector", lambda v, ob4=ob4, G=G: v.tensor_copy(ob4, G[:, 256:512]),
                      reads=(gk,), writes=(k_("ob"),))
                sc.dma("sync", bass.AP(dm["o1fm"], (h0 * 128) * S + t0,
                                       [[S, 128], [128 * S, 4], [1, 64]]),
                       ob4, "o1fm", rkeys=(k_("ob"),))
                for j in range(4):
                    sc.op("tensor", lambda t, G=G, j=j, X4=X4, At4=At4: t.matmul(
                        G[:, j * 128:(j + 1) * 128], X4[:, j * 256: j * 256 + 128],
                        At4[:, j * 128:(j + 1) * 128], start=True, stop=True),
                        reads=(k_("X"), k_("At")), writes=(gk,))
                sc.op("vector", lambda v, dph4=dph4, G=G: v.tensor_copy(dph4, G[:, 0:512]),
                      reads=(gk,), writes=(k_("dph"),))
                for j in range(4):
                    sc.op("tensor", lambda t, G=G, j=j, dph4=dph4, S0q=S0q: t.matmul(
                        G[:, j * 128:(j + 1) * 128], dph4[:, j * 128:(j + 1) * 128],
                        S0q[:, j * 128:(j + 1) * 128], start=True, stop=False),
                        reads=(k_("dph"), S0k), writes=(gk,))
                    sc.op("tensor", lambda t, G=G, j=j, At4=At4, X4=X4: t.matmul(
                        G[:, j * 128:(j + 1) * 128], At4[:, j * 128:(j + 1) * 128],
                        X4[:, j * 256 + 128: j * 256 + 256], start=False, stop=False),
                        reads=(k_("At"), k_("X")), writes=(gk,))
                    sc.op("tensor", lambda t, G=G, j=j, Kt4=Kt4, vt4=vt4: t.matmul(
                        G[:, j * 128:(j + 1) * 128], Kt4[0:64, j * 128:(j + 1) * 128],
                        vt4[0:64, j * 128:(j + 1) * 128], start=False, stop=True),
                        reads=(k_("Kt"), k_("vt")), writes=(gk,))
                sc.op("vector", lambda v, tmpf4=tmpf4, S0q=S0q, G=G: v.tensor_add(
                    tmpf4, G[:, 0:512], S0q),
                    reads=(gk, S0k), writes=(k_("tf"),))
                for j in range(4):
                    sc.op("vector", lambda v, j=j, S0q=S0q, tmpf4=tmpf4, gm4=gm4: v.tensor_scalar_mul(
                        S0q[:, j * 128:(j + 1) * 128], tmpf4[:, j * 128:(j + 1) * 128],
                        gm4[:, j * 65 + 64: j * 65 + 65]),
                        reads=(k_("tf"), k_("gm")), writes=(S0k,))
            if (c & 7) == 7:
                self.sc.fence()

    def p8_gate_gn(self):
        sc, dm = self.sc, self.dram
        RG, RW1, RS, RM, PS = self.RG, self.RW1, self.RS, self.RM, self.PS
        sc.dma("sync", RW1[:, 0:1024], dm["c_wog2"][:, :], "wog2", rkeys=("c_wog2",))
        for hh in range(H):
            for tt in range(8):
                t0 = tt * 512
                par = tt & 1
                Pb = par * 4096
                ot = RG[:, Pb: Pb + 512]
                og = RG[:, Pb + 512: Pb + 1024]
                g_ = RG[:, Pb + 1024: Pb + 1536]
                y_ = RG[:, Pb + 1536: Pb + 2048]
                yn = RG[:, Pb + 2048: Pb + 2560]
                y2 = RG[:, Pb + 2560: Pb + 3072]
                k_ = lambda n: f"g{n}{par}"
                sc.dma("sync", ot, dm["o1fm"][hh * 128:(hh + 1) * 128, t0:t0 + 512],
                       k_("o"), rkeys=())
                sc.dma("sync", og, dm["og1fm"][:, t0:t0 + 512], k_("og"), rkeys=())
                ps = PS[par]
                pk = f"ps{par}"
                sc.op("tensor", lambda t, ps=ps, og=og, hh=hh: t.matmul(
                    ps[:, 0:512], RW1[:, hh * 128:(hh + 1) * 128], og,
                    start=True, stop=True),
                    reads=("wog2", k_("og")), writes=(pk,))
                sc.op("scalar", lambda s, g_=g_, ps=ps: s.activation(g_, ps[:, 0:512], AF.Sigmoid),
                      reads=(pk,), writes=(k_("g"),))
                sc.op("vector", lambda v, y_=y_, ot=ot, g_=g_: v.tensor_mul(y_, ot, g_),
                      reads=(k_("o"), k_("g")), writes=(k_("y"),))
                sc.op("vector", lambda v, y2=y2, y_=y_: v.tensor_mul(y2, y_, y_),
                      reads=(k_("y"),), writes=(k_("y2"),))
                ps2 = PS[2 + par]
                pk2 = f"ps{2 + par}"
                sc.op("tensor", lambda t, ps2=ps2, y2=y2: t.matmul(
                    ps2[0:1, 0:512], RM[:, 1026:1027], y2, start=True, stop=True),
                    reads=("RM_onescol", k_("y2")), writes=(pk2,))
                rs_ = RS[:, 128 + par * 16: 128 + par * 16 + 8]
                sc.op("scalar", lambda s, rs_=rs_, ps2=ps2: s.activation(
                    rs_[0:1, 0:4].rearrange("a (b c) -> a (b c)") if False else rs_[0:1, 0:1],
                    ps2[0:1, 0:1], AF.Sqrt, bias=self.RS[0:1, 0:1], scale=1.0 / HD) if False else None,
                    reads=(), writes=()) if False else None
                rsr = RG[:, Pb + 3072: Pb + 3584]
                sc.op("scalar", lambda s, rsr=rsr, ps2=ps2: s.activation(
                    rsr[0:1, 0:512], ps2[0:1, 0:512], AF.Sqrt, bias=self.RS[0:1, 0:1],
                    scale=1.0 / HD),
                    reads=(pk2, "RS_eps6"), writes=(k_("rs"),))
                sc.op("vector", lambda v, rsr=rsr: v.reciprocal(rsr[0:1, 0:512], rsr[0:1, 0:512]),
                      reads=(k_("rs"),), writes=(k_("rs"),))
                sc.op("tensor", lambda t, ps2=ps2, rsr=rsr: t.matmul(
                    ps2[:, 0:512], RM[0:1, 896:1024], rsr[0:1, 0:512],
                    start=True, stop=True),
                    reads=("RM_ones", k_("rs")), writes=(pk2,))
                sc.op("vector", lambda v, yn=yn, y_=y_, ps2=ps2: v.tensor_mul(
                    yn, y_, ps2[:, 0:512]),
                    reads=(k_("y"), pk2), writes=(k_("yn"),))
                sc.op("vector", lambda v, yn=yn, hh=hh: v.tensor_scalar_mul(
                    yn, yn, RS[:, 8 + hh:9 + hh]),
                    reads=(k_("yn"), "RS_nw1"), writes=(k_("yn"),))
                sc.dma("sync", dm["yn2fm"][hh * 128:(hh + 1) * 128, t0:t0 + 512], yn,
                       "yn2fm", rkeys=(k_("yn"),))
            self.sc.fence()

    def p9_woproj(self):
        sc, dm = self.sc, self.dram
        RG, RW1, PS = self.RG, self.RW1, self.PS
        self.load_w_lhsT("c_L1_attn_Wo", RW1, 0, D, "wo1")
        for tt in range(8):
            t0 = tt * 512
            par = tt & 1
            Pb = par * 4096
            yt = RG[:, Pb: Pb + 4096]
            yk = f"p9y{par}"
            for hh in range(8):
                sc.dma("sync", yt[:, hh * 512:(hh + 1) * 512],
                       dm["yn2fm"][hh * 128:(hh + 1) * 128, t0:t0 + 512], yk, rkeys=())
            for st4 in range(4):
                ts0 = t0 + st4 * 128
                for dh in range(2):
                    ps = PS[dh]
                    pk = f"ps{dh}"
                    for hh in range(8):
                        sc.op("tensor", lambda t, ps=ps, hh=hh, dh=dh, st4=st4, yt=yt:
                              t.matmul(ps[:, 0:512],
                                       yt[:, hh * 512 + st4 * 128: hh * 512 + st4 * 128 + 128],
                                       RW1[:, hh * D + dh * 512: hh * D + dh * 512 + 512],
                                       start=(hh == 0), stop=(hh == 7)),
                              reads=("wo1", yk), writes=(pk,))
                ev = RG[:, 8192 + (st4 & 1) * 1024: 8192 + (st4 & 1) * 1024 + 1024]
                evk = f"p9e{st4 & 1}"
                for dh in range(2):
                    sc.op("vector", lambda v, dh=dh, ev=ev: v.tensor_copy(
                        ev[:, dh * 512:(dh + 1) * 512], PS[dh][:, 0:512]),
                        reads=(f"ps{dh}",), writes=(evk,))
                sc.dma("sync", dm["l1a"][ts0:ts0 + 128, :], ev, "l1a", rkeys=(evk,))
            self.sc.fence()

    def p10_l1ffn(self):
        sc, dm = self.sc, self.dram
        RG, RW1, RW2, RF, RS, RM, PS = self.RG, self.RW1, self.RW2, self.RF, self.RS, self.RM, self.PS
        sc.dma("sync", RS[:, 96:104], bass.AP(dm["L1_ffn_norm_w"], 0, [[1, 128], [128, 8]]),
               "RS_nwf", allow_slow_non_contiguous=True)
        sc.op("gpsimd", lambda g: g.iota(RS[:, 80:88], [[1, 8]], base=0,
                                         channel_multiplier=0,
                                         allow_small_or_imprecise_dtypes=True),
              writes=("RS_iota",))
        self.sc.fence()
        for it in range(S // 128):
            t0 = it * 128
            par = it & 1
            xt = RF[:, par * 1024: par * 1024 + 1024]
            xk = f"ax{par}"
            sq = RF[:, 2048 + par * 1024: 3072 + par * 1024]
            cols = RS[:, 128 + par * 8: 136 + par * 8]
            ck = f"ac{par}"
            sc.dma("gpsimd", xt, dm["xn1"][t0:t0 + 128, :], xk, rkeys=())
            sc.op("vector", lambda v, sq=sq, xt=xt: v.tensor_mul(sq, xt, xt),
                  reads=(xk,), writes=(f"asq{par}",))
            sc.op("vector", lambda v, cols=cols, sq=sq: v.reduce_sum(
                cols[:, 0:1], sq, axis=mybir.AxisListType.X),
                reads=(f"asq{par}",), writes=(ck,))
            sc.op("scalar", lambda s, cols=cols: s.activation(
                cols[:, 1:2], cols[:, 0:1], AF.Sqrt, bias=self.RS[:, 0:1], scale=1.0 / D),
                reads=(ck, "RS_eps6"), writes=(ck,))
            sc.op("vector", lambda v, cols=cols: v.reciprocal(cols[:, 2:3], cols[:, 1:2]),
                  reads=(ck,), writes=(ck,))
            ot = RG[:, par * 1024: par * 1024 + 1024]
            ok = f"ao{par}"
            sc.op("vector", lambda v, ot=ot, xt=xt, cols=cols: v.tensor_scalar_mul(
                ot, xt, cols[:, 2:3]),
                reads=(xk, ck), writes=(ok,))
            sc.dma("sync", dm["trms"][t0:t0 + 128, :], ot, "trms", rkeys=(ok,))
        self.sc.fence()
        gt = RW1[:, 0:64]
        sc.dma("sync", gt[:, 0:8], dm["c_gate_wT"][0:128, :], "gw", rkeys=("c_gate_wT",))
        for ch in range(1, 8):
            sc.dma("sync", gt[:, ch * 8:(ch + 1) * 8], dm["c_gate_wT"][ch * 128:(ch + 1) * 128, :],
                   "gw", rkeys=("c_gate_wT",))
        for ch in range(8):
            sc.op("vector", lambda v, ch=ch, gt=gt: v.tensor_scalar_mul(
                gt[:, ch * 8:(ch + 1) * 8], gt[:, ch * 8:(ch + 1) * 8], RS[:, 96 + ch:97 + ch]),
                reads=("gw", "RS_nwf"), writes=("gw",))
        self.sc.fence()
        for tt in range(8):
            t0 = tt * 512
            par = tt & 1
            tf = RG[:, par * 4096: par * 4096 + 4096]
            tfk = f"btf{par}"
            for ch in range(8):
                sc.dma("sync", tf[:, ch * 512:(ch + 1) * 512],
                       dm["trms"][t0:t0 + 512, ch * 128:(ch + 1) * 128],
                       tfk, rkeys=("trms",), transpose=True)
            for st4 in range(4):
                ts0 = t0 + st4 * 128
                ps = PS[par]
                pk = f"ps{par}"
                for ch in range(8):
                    sc.op("tensor", lambda t, ps=ps, ch=ch, st4=st4, tf=tf: t.matmul(
                        ps[:, 0:8], tf[:, ch * 512 + st4 * 128: ch * 512 + st4 * 128 + 128],
                        RW1[:, ch * 8:(ch + 1) * 8], start=(ch == 0), stop=(ch == 7)),
                        reads=("gw", tfk), writes=(pk,))
                w_ = RF[:, 4096 + par * 64: 4096 + par * 64 + 64]
                wk = f"bw{par}"
                sc.op("vector", lambda v, w_=w_, ps=ps: v.reduce_max(
                    w_[:, 8:9], ps[:, 0:8], axis=mybir.AxisListType.X, negate=True),
                    reads=(pk,), writes=(wk,))
                sc.op("scalar", lambda s, w_=w_, ps=ps: s.activation(
                    w_[:, 0:8], ps[:, 0:8], AF.Exp, bias=w_[:, 8:9]),
                    reads=(pk, wk), writes=(wk,))
                sc.op("vector", lambda v, w_=w_: v.reduce_sum(
                    w_[:, 9:10], w_[:, 0:8], axis=mybir.AxisListType.X),
                    reads=(wk,), writes=(wk,))
                sc.op("vector", lambda v, w_=w_: v.reciprocal(w_[:, 10:11], w_[:, 9:10]),
                      reads=(wk,), writes=(wk,))
                sc.op("vector", lambda v, w_=w_: v.tensor_scalar_mul(
                    w_[:, 0:8], w_[:, 0:8], w_[:, 10:11]),
                    reads=(wk,), writes=(wk,))
                sc.op("vector", lambda v, w_=w_: v.max(w_[:, 16:24], w_[:, 0:8]),
                      reads=(wk,), writes=(wk,))
                idx = self.RIDX[:, par * 8: par * 8 + 8]
                sc.op("vector", lambda v, w_=w_, idx=idx: v.max_index(
                    idx, w_[:, 16:24], w_[:, 0:8]),
                    reads=(wk,), writes=(f"bidx{par}",))
                sc.op("vector", lambda v, w_=w_: v.tensor_add(w_[:, 11:12], w_[:, 16:17], w_[:, 17:18]),
                      reads=(wk,), writes=(wk,))
                sc.op("vector", lambda v, w_=w_: v.reciprocal(w_[:, 12:13], w_[:, 11:12]),
                      reads=(wk,), writes=(wk,))
                sc.op("vector", lambda v, w_=w_: v.tensor_mul(w_[:, 13:14], w_[:, 16:17], w_[:, 12:13]),
                      reads=(wk,), writes=(wk,))
                sc.op("vector", lambda v, w_=w_: v.tensor_mul(w_[:, 14:15], w_[:, 17:18], w_[:, 12:13]),
                      reads=(wk,), writes=(wk,))
                sc.op("vector", lambda v, idx=idx, w_=w_: v.tensor_copy(w_[:, 24:26], idx[:, 0:2]),
                      reads=(f"bidx{par}",), writes=(wk,))
                oh = RF[:, 4352 + par * 32: 4352 + par * 32 + 16]
                sc.op("vector", lambda v, oh=oh, w_=w_: v.tensor_single_scalar(
                    oh[:, 0:8], RS[:, 80:88], w_[:, 24:25], ALU.is_equal),
                    reads=("RS_iota", wk), writes=(f"boh{par}",))
                sc.op("vector", lambda v, oh=oh, w_=w_: v.tensor_single_scalar(
                    oh[:, 8:16], RS[:, 80:88], w_[:, 25:26], ALU.is_equal),
                    reads=("RS_iota", wk), writes=(f"boh{par}",))
                sc.op("vector", lambda v, oh=oh, w_=w_: v.tensor_scalar_mul(
                    oh[:, 0:8], oh[:, 0:8], w_[:, 13:14]),
                    reads=(f"boh{par}", wk), writes=(f"boh{par}",))
                sc.op("vector", lambda v, oh=oh, w_=w_: v.scalar_tensor_tensor(
                    oh[:, 0:8], oh[:, 8:16], w_[:, 14:15], oh[:, 0:8], ALU.mult, ALU.add),
                    reads=(f"boh{par}", wk), writes=(f"boh{par}",))
                ohb = RG[:, 8192 + par * 128: 8192 + par * 128 + 8]
                sc.op("vector", lambda v, ohb=ohb, oh=oh: v.tensor_copy(ohb, oh[:, 0:8]),
                      reads=(f"boh{par}",), writes=(f"bohb{par}",))
                sc.op("tensor", lambda t, ohb=ohb: t.transpose(
                    self.PB[0][0:8, 0:128], ohb, RM[:, 0:128]),
                    reads=(f"bohb{par}", "RM_id"), writes=("pb0",))
                wr = RG[:, 8448 + par * 128: 8448 + par * 128 + 128]
                sc.op("vector", lambda v, wr=wr: v.tensor_copy(wr[0:8, :], self.PB[0][0:8, 0:128]),
                      reads=("pb0",), writes=(f"bwr{par}",))
                sc.dma("gpsimd", dm["wrow"][:, ts0:ts0 + 128], wr[0:8, :], "wrow",
                       rkeys=(f"bwr{par}",))
            self.sc.fence()
        for ipass, (wname, dst) in enumerate((("c_w1T", "base1"), ("c_w3T", "base3"))):
            for ch in range(8):
                sc.dma("sync", RW1[:, ch * 4096:(ch + 1) * 4096],
                       dm[wname][ch * 128:(ch + 1) * 128, :], "wbase", rkeys=(wname,))
            for ch in range(8):
                sc.op("vector", lambda v, ch=ch: v.tensor_scalar_mul(
                    RW1[:, ch * 4096:(ch + 1) * 4096], RW1[:, ch * 4096:(ch + 1) * 4096],
                    RS[:, 96 + ch:97 + ch]),
                    reads=("wbase", "RS_nwf"), writes=("wbase",))
            self.sc.fence()
            for tt in range(8):
                t0 = tt * 512
                par = tt & 1
                tf = RG[:, par * 4096: par * 4096 + 4096]
                tfk = f"ctf{par}"
                for ch in range(8):
                    sc.dma("sync", tf[:, ch * 512:(ch + 1) * 512],
                           dm["trms"][t0:t0 + 512, ch * 128:(ch + 1) * 128],
                           tfk, rkeys=("trms",), transpose=True)
                for ft in range(32):
                    ps = PS[ft & 1]
                    pk = f"ps{ft & 1}"
                    for ch in range(8):
                        sc.op("tensor", lambda t, ps=ps, ch=ch, ft=ft, tf=tf: t.matmul(
                            ps[:, 0:512],
                            RW1[:, ch * 4096 + ft * 128: ch * 4096 + ft * 128 + 128],
                            tf[:, ch * 512:(ch + 1) * 512], start=(ch == 0), stop=(ch == 7)),
                            reads=("wbase", tfk), writes=(pk,))
                    ev = RG[:, 8192 + (ft & 1) * 512: 8192 + (ft & 1) * 512 + 512]
                    evk = f"cev{ft & 1}"
                    sc.op("vector", lambda v, ev=ev, ps=ps: v.tensor_copy(ev, ps[:, 0:512]),
                          reads=(pk,), writes=(evk,))
                    sc.dma("sync", dm[dst][ft * 128:(ft + 1) * 128, t0:t0 + 512], ev,
                           dst, rkeys=(evk,))
                self.sc.fence()
        for tt in range(8):
            t0 = tt * 512
            tf = RW2[:, 16384:20480]
            for ch in range(8):
                sc.dma("sync", tf[:, ch * 512:(ch + 1) * 512],
                       dm["trms"][t0:t0 + 512, ch * 128:(ch + 1) * 128],
                       "dtf", rkeys=("trms",), transpose=True)
            sc.op("vector", lambda v: v.memset(RW2[:, 0:8192], 0.0),
                  writes=("dmix0", "dmix1", "dmix2", "dmix3"))
            sc.op("gpsimd", lambda g: g.memset(RW2[:, 8192:16384], 0.0),
                  writes=("dmix0", "dmix1", "dmix2", "dmix3"))
            for bname, off in (("base1", 0), ("base3", 16384)):
                for ft in range(32):
                    sc.dma("sync", RW1[:, off + ft * 512: off + (ft + 1) * 512],
                           dm[bname][ft * 128:(ft + 1) * 128, t0:t0 + 512],
                           "dbase", rkeys=(bname,))
            wr8 = self.RW2[:, 20480:24576]
            for e_ in range(H):
                sc.dma("gpsimd", wr8[0:1, e_ * 512:(e_ + 1) * 512],
                       dm["wrow"][e_, t0:t0 + 512], "dwr", rkeys=("wrow",))
            self.sc.fence()
            for e in range(H):
                for uu, aname in ((0, "c_a1T"), (1, "c_a3T")):
                    at = RG[:, uu * 128: uu * 128 + 128]
                    ak = f"da{uu}"
                    sc.dma("sync", at,
                           bass.AP(dm["c_" + aname[2:]] if False else dm[aname],
                                   e * 16, [[128, 128], [128 * 128, 8], [1, 16]]),
                           ak, rkeys=(aname,))
                    for ch in range(8):
                        sc.op("vector", lambda v, at=at, ch=ch: v.tensor_scalar_mul(
                            at[:, ch * 16:(ch + 1) * 16], at[:, ch * 16:(ch + 1) * 16],
                            RS[:, 96 + ch:97 + ch]),
                            reads=(ak, "RS_nwf"), writes=(ak,))
                    ps = PS[4 + uu]
                    pk = f"ps{4 + uu}"
                    for ch in range(8):
                        sc.op("tensor", lambda t, ps=ps, at=at, ch=ch, tf=tf: t.matmul(
                            ps[0:16, 0:512], at[:, ch * 16:(ch + 1) * 16],
                            tf[:, ch * 512:(ch + 1) * 512], start=(ch == 0), stop=(ch == 7)),
                            reads=(ak, "dtf"), writes=(pk,))
                    us = RG[:, 256 + uu * 512: 256 + uu * 512 + 512]
                    sc.op("scalar", lambda s, us=us, ps=ps: s.activation(
                        us[0:16, :], ps[0:16, 0:512], AF.Copy, scale=2.0),
                        reads=(pk,), writes=(f"dus{uu}",))
                for uu, bname in ((0, "c_b1T"), (1, "c_b3T")):
                    bt = RG[:, 1280 + uu * 4096: 1280 + uu * 4096 + 4096]
                    sc.dma("sync", bt[0:16, :], dm[bname][e, :, :], f"dbt{uu}", rkeys=(bname,))
                sc.op("tensor", lambda t, e=e: t.matmul(
                    PS[3][:, 0:512], RM[0:1, 896:1024],
                    self.RW2[0:1, 20480 + e * 512: 20480 + (e + 1) * 512],
                    start=True, stop=True),
                    reads=("RM_ones", "dwr"), writes=("ps3",))
                wbs = self.RW2[:, 24576:25088]
                sc.op("vector", lambda v, wbs=wbs: v.tensor_copy(wbs, PS[3][:, 0:512]),
                      reads=("ps3",), writes=("dwbs",))
                a2e = RG[:, 12544 - 512 + 512: 13056] if False else RG[:, 12544:13056]
                sc.dma("sync", a2e,
                       bass.AP(dm["c_a2T"], e * FF * 16, [[16, 128], [128 * 16, 32], [1, 16]]),
                       "da2", rkeys=("c_a2T",))
                for ft in range(32):
                    fp = ft & 1
                    psa = PS[fp]
                    pka = f"ps{fp}"
                    sil = RG[:, 9472 + fp * 512: 9472 + fp * 512 + 512]
                    act = RG[:, 10496 + fp * 512: 10496 + fp * 512 + 512]
                    tmp = RG[:, 11520 + fp * 512: 11520 + fp * 512 + 512]
                    sc.op("tensor", lambda t, psa=psa, ft=ft: t.matmul(
                        psa[:, 0:512], self.RG[0:16, 1280 + ft * 128: 1280 + ft * 128 + 128],
                        self.RG[0:16, 256:768], start=True, stop=False),
                        reads=("dbt0", "dus0"), writes=(pka,))
                    sc.op("tensor", lambda t, psa=psa, ft=ft: t.matmul(
                        psa[:, 0:512], RM[:, 0:128], RW1[:, ft * 512:(ft + 1) * 512],
                        start=False, stop=True),
                        reads=("RM_id", "dbase"), writes=(pka,))
                    sc.op("scalar", lambda s, psa=psa, sil=sil: s.activation(
                        sil, psa[:, 0:512], AF.Silu),
                        reads=(pka,), writes=(f"dsil{fp}",))
                    psb = PS[2] if fp == 0 else PS[3]
                    pkb = "ps2" if fp == 0 else "ps3"
                    sc.op("tensor", lambda t, psb=psb, ft=ft: t.matmul(
                        psb[:, 0:512], self.RG[0:16, 5376 + ft * 128: 5376 + ft * 128 + 128],
                        self.RG[0:16, 768:1280], start=True, stop=False),
                        reads=("dbt1", "dus1"), writes=(pkb,))
                    sc.op("tensor", lambda t, psb=psb, ft=ft: t.matmul(
                        psb[:, 0:512], RM[:, 0:128], RW1[:, 16384 + ft * 512: 16384 + (ft + 1) * 512],
                        start=False, stop=True),
                        reads=("RM_id", "dbase"), writes=(pkb,))
                    sc.op("vector", lambda v, act=act, sil=sil, psb=psb: v.tensor_mul(
                        act, sil, psb[:, 0:512]),
                        reads=(f"dsil{fp}", pkb), writes=(f"dact{fp}",))
                    sc.op("tensor", lambda t, act=act, ft=ft: t.matmul(
                        PS[5][0:16, 0:512], self.RG[:, 12544 + ft * 16: 12544 + ft * 16 + 16],
                        act, start=(ft == 0), stop=(ft == 31)),
                        reads=("da2", f"dact{fp}"), writes=("ps5",))
                    sc.op("vector", lambda v, tmp=tmp, act=act: v.tensor_mul(
                        tmp, act, self.RW2[:, 24576:25088]),
                        reads=(f"dact{fp}", "dwbs"), writes=(f"dtmp{fp}",))
                    eng = "gpsimd"
                    sc.op(eng, lambda g, tmp=tmp, ft=ft: g.tensor_add(
                        RW2[:, ft * 512:(ft + 1) * 512], RW2[:, ft * 512:(ft + 1) * 512], tmp),
                        reads=(f"dtmp{fp}", f"dmix{ft & 3}"), writes=(f"dmix{ft & 3}",))
                u2s = RG[:, 12608:13120]
                u2c = RG[:, 13120:13632]
                sc.op("scalar", lambda s, u2s=u2s: s.activation(
                    u2s[0:16, :], PS[5][0:16, 0:512], AF.Copy, scale=2.0),
                    reads=("ps5",), writes=("du2s",))
                sc.op("vector", lambda v, u2c=u2c, u2s=u2s: v.tensor_mul(
                    u2c[0:16, :], u2s[0:16, :], self.RW2[0:16, 24576:25088]),
                    reads=("du2s", "dwbs"), writes=("du2c",))
                sc.dma("sync", dm["u2w"][e, :, t0:t0 + 512], u2c[0:16, :], "u2w",
                       rkeys=("du2c",))
            for ft in range(32):
                sc.dma("sync", dm["mixedfm"][ft * 128:(ft + 1) * 128, t0:t0 + 512],
                       RW2[:, ft * 512:(ft + 1) * 512], "mixedfm", rkeys=(f"dmix{ft & 3}",))
            self.sc.fence()
        for ch in range(32):
            sc.dma("sync", RW1[:, ch * 1024:(ch + 1) * 1024],
                   dm["c_w2T"][ch * 128:(ch + 1) * 128, :], "w2r", rkeys=("c_w2T",))
        for e in range(H):
            sc.dma("sync", RG[0:16, e * 512: e * 512 + 512],
                   dm["c_b2T"][e, :, 0:512], "b2ra", rkeys=("c_b2T",))
            sc.dma("sync", RG[0:16, 4096 + e * 512: 4096 + e * 512 + 512],
                   dm["c_b2T"][e, :, 512:1024], "b2rb", rkeys=("c_b2T",))
        self.sc.fence()
        for tt in range(8):
            t0 = tt * 512
            for ft in range(32):
                sc.dma("sync", RW2[:, ft * 512:(ft + 1) * 512],
                       dm["mixedfm"][ft * 128:(ft + 1) * 128, t0:t0 + 512],
                       "emx", rkeys=("mixedfm",))
            self.sc.fence()
            for st4 in range(4):
                ts0 = t0 + st4 * 128
                sp = st4 & 1
                x1t = RF[:, sp * 1024: sp * 1024 + 1024]
                sc.dma("sync", x1t, dm["x1"][ts0:ts0 + 128, :], f"ex1{sp}", rkeys=())
                l1t = RG[:, 8192 + sp * 1024: 9216 + sp * 1024]
                sc.dma("sync", l1t, dm["l1a"][ts0:ts0 + 128, :], f"el1{sp}", rkeys=())
                u2b = RG[:, 10240 + sp * 1024: 11264 + sp * 1024]
                for e in range(H):
                    sc.dma("gpsimd", u2b[0:16, e * 128:(e + 1) * 128],
                           dm["u2w"][e, :, ts0:ts0 + 128], f"eu2{sp}", rkeys=("u2w",))
                for dh in range(2):
                    ps = PS[2 * sp + dh]
                    pk = f"ps{2 * sp + dh}"
                    for ft in range(32):
                        sc.op("tensor", lambda t, ps=ps, ft=ft, dh=dh, st4=st4: t.matmul(
                            ps[:, 0:512],
                            RW2[:, ft * 512 + st4 * 128: ft * 512 + st4 * 128 + 128],
                            RW1[:, ft * 1024 + dh * 512: ft * 1024 + dh * 512 + 512],
                            start=(ft == 0), stop=False),
                            reads=("w2r", "emx"), writes=(pk,))
                    for e in range(H):
                        sc.op("tensor", lambda t, ps=ps, e=e, dh=dh, u2b=u2b: t.matmul(
                            ps[:, 0:512], u2b[0:16, e * 128:(e + 1) * 128],
                            self.RG[0:16, dh * 4096 + e * 512: dh * 4096 + e * 512 + 512],
                            start=False, stop=(e == H - 1)),
                            reads=(f"eu2{sp}", "b2ra", "b2rb"), writes=(pk,))
                    sc.op("vector", lambda v, ps=ps, dh=dh, x1t=x1t: v.tensor_add(
                        x1t[:, dh * 512:(dh + 1) * 512], ps[:, 0:512],
                        x1t[:, dh * 512:(dh + 1) * 512]),
                        reads=(pk, f"ex1{sp}"), writes=(f"ex1{sp}",))
                    sc.op("vector", lambda v, dh=dh, x1t=x1t, l1t=l1t: v.tensor_add(
                        x1t[:, dh * 512:(dh + 1) * 512], x1t[:, dh * 512:(dh + 1) * 512],
                        l1t[:, dh * 512:(dh + 1) * 512]),
                        reads=(f"el1{sp}",), writes=(f"ex1{sp}",))
                sc.dma("sync", dm["out"][ts0:ts0 + 128, :], x1t, "outw", rkeys=(f"ex1{sp}",))


def build_program(upto=99, taps=()):
    b = B(upto=upto, taps=taps)
    b.declare()
    phases = [
        (0, b.p0_prep),
        (1, b.p2_ln0),
        (2, b.p3_l0proj),
        (3, b.p3b_keymask),
        (4, b.p4_l0attn),
        (5, b.p1_fold),
        (6, b.p5_l0ffn),
    ]
    more = [
        (7, getattr(b, "p6_l1proj", None)),
        (8, getattr(b, "p7_scan", None)),
        (9, getattr(b, "p8_gate_gn", None)),
        (10, getattr(b, "p9_woproj", None)),
        (11, getattr(b, "p10_l1ffn", None)),
    ]
    for i, fn in phases + more:
        if fn is None or i > upto:
            continue
        fn()
        b.sc.fence()
    with b.nc.allow_low_precision(
            "bf16 operands feeding bf16 matmuls; the only bf16 accumulation is "
            "the top-2 expert mixture where 6 of 8 addends are exactly zero"):
        with b.nc.Block() as block:
            b.sc.analyze()
            b.sc.build(b.nc, block)
    return b


def make_in_map(inputs, aux, core):
    T = np.ascontiguousarray
    im = {
        "x": np.ascontiguousarray(np.asarray(inputs["L0_attn_x"])[core]).astype(np.float32),
        "L0_ffn_W_expT": T(np.asarray(inputs["L0_ffn_W_exp"]).T),
        "L1_ffn_w1T": T(np.asarray(inputs["L1_ffn_w1"]).T),
        "L1_ffn_w3T": T(np.asarray(inputs["L1_ffn_w3"]).T),
        "L1_ffn_w2T": T(np.asarray(inputs["L1_ffn_w2"]).T),
        "L1_ffn_gate_wT": T(np.asarray(inputs["L1_ffn_gate_w"]).T),
        "L1_ffn_a1T": T(np.asarray(inputs["L1_ffn_a1"]).transpose(2, 0, 1).reshape(D, -1)),
        "L1_ffn_a3T": T(np.asarray(inputs["L1_ffn_a3"]).transpose(2, 0, 1).reshape(D, -1)),
        "L1_ffn_b1T": T(np.asarray(inputs["L1_ffn_b1"]).transpose(0, 2, 1)),
        "L1_ffn_b3T": T(np.asarray(inputs["L1_ffn_b3"]).transpose(0, 2, 1)),
        "L1_ffn_a2T": T(np.asarray(inputs["L1_ffn_a2"]).transpose(0, 2, 1)),
        "L1_ffn_b2T": T(np.asarray(inputs["L1_ffn_b2"]).transpose(0, 2, 1)),
    }
    for n in ("L0_attn_Wq", "L0_attn_Wk", "L0_attn_Wv", "L0_attn_Wu",
              "L0_ffn_b_exp", "L0_ffn_up_bias", "L0_ffn_W_con", "L0_ffn_b_con",
              "L0_ffn_down_bias", "L1_attn_Wq", "L1_attn_Wk", "L1_attn_Wv",
              "L1_attn_Wo", "L1_attn_Wf1", "L1_attn_Wf2", "L1_attn_Wbeta",
              "L1_attn_Wog1", "L1_attn_Wog2", "L1_attn_norm_w", "L1_ffn_norm_w"):
        im[n] = np.ascontiguousarray(np.asarray(inputs[n]), dtype=np.float32)
    for n in ("r_pad", "FATu", "FAu", "FBTu", "FBu", "FATd", "FAd", "FBTd", "FBd",
              "ident", "antident", "m_tri_kq", "m_l", "m_lt", "m_kbt", "m_aq", "m_kq"):
        im[n] = aux[n]
    return im


def kernel(**inputs):
    aux = make_host_aux(inputs)
    b = build_program()
    in_maps = [make_in_map(inputs, aux, c) for c in range(8)]
    res = run_bass_kernel_spmd(b.nc, in_maps, core_ids=list(range(8)))
    return np.stack([res.results[c]["out"] for c in range(8)], axis=0)